# version 17
# baseline (speedup 1.0000x reference)
import contextlib
import numpy as np
import ml_dtypes
import concourse.bass as bass
import concourse.mybir as mybir
from concourse.bass_utils import run_bass_kernel_spmd

F32 = mybir.dt.float32
BF16 = mybir.dt.bfloat16
AF = mybir.ActivationFunctionType
ALU = mybir.AluOpType
AX = mybir.AxisListType.X
BF = ml_dtypes.bfloat16

B, T, D = 2, 8192, 1024
NH, DH, CH = 16, 64, 80
RQ = 6
NT = T // 128
OWN = 2048
NOT_ = OWN // 128
NST = 4
E16, RM, OH = 16, 512, 4096
ROPE_BASE = 10000.0
LN_EPS = 1e-5
ENGS = ("sync", "vector", "scalar", "tensor", "gpsimd")


def var_kv_tiles(v):
    return 8 * (8 - v)


def own_tile_global(v, o):
    return 8 * v + o if o < 8 else (56 - 8 * v) + (o - 8)


def stripe_nblk(v, s):
    return [8 * v + 4, 8 * v + 8, 60 - 8 * v, 64 - 8 * v][s]


_CHAINED = {"vector", "scalar", "gpsimd"}
_NOCHAIN_OPS = {"dma_start", "dma_start_transpose", "nop", "sem_inc",
                "partition_id", "collective_compute"}
DMA_SEMS = {"ld", "ngst", "wcast", "p_x", "p_kst", "p_vst0", "p_vst1",
            "p_xnst0", "p_xnst1", "q_ld", "q_st",
            "a_kv0", "a_kv1", "a_qt", "x_qb", "x_st", "dbst", "a_ynst",
            "f_wld", "f_xld", "f_xt",
            "f_yld", "f_xold", "f_ost"}
BAR_SEMS = {"bar_v", "bar_s", "bar_g", "bar_t", "rdone_vector", "rdone_scalar",
            "rdone_tensor", "rdone_gpsimd", "rdone_sync"}


def _is_virtual(sem, owner):
    return (owner.get(sem) in _CHAINED and sem not in DMA_SEMS
            and sem not in BAR_SEMS and not sem.startswith("ch_"))


class _Instr:
    __slots__ = ("ctx", "real", "chained", "eng")

    def __init__(self, ctx, real, chained, eng):
        self.ctx = ctx
        self.real = real
        self.chained = chained
        self.eng = eng

    def inc(self, sem, k=1):
        ctx = self.ctx
        ctx.sems[sem] = ctx.sems.get(sem, 0) + k
        if self.eng in _CHAINED and _is_virtual(sem, ctx.owner):
            ctx.hist.setdefault(sem, {})[ctx.sems[sem]] = ctx.sems.get(
                "ch_" + self.eng, 0)
        else:
            assert not self.chained, f"real sem {sem} inc on chained op"
            if self.real is not None:
                self.real.then_inc(ctx.semh[sem], k)
        return self


class _Prox:
    def __init__(self, ctx, name):
        self._ctx = ctx
        self._name = name

    def __getattr__(self, op):
        ctx = self._ctx
        name = self._name
        chain = name in _CHAINED and op not in _NOCHAIN_OPS
        csem = "ch_" + name

        def call(*a, **kw):
            if op == "nop" and "nofuse" not in kw:
                kw["nofuse"] = True
            real = None
            if chain:
                cur = ctx.sems.get(csem, 0)
                if cur > 0 and ctx.active == name and ctx.dry == 0:
                    getattr(ctx.nc, name).wait_ge(ctx.semh[csem], cur)
            if ctx.active == name and ctx.dry == 0:
                real = getattr(getattr(ctx.nc, name), op)(*a, **kw)
            ins = _Instr(ctx, real, chain, name)
            if chain:
                ctx.sems[csem] = ctx.sems.get(csem, 0) + 1
                if real is not None:
                    real.then_inc(ctx.semh[csem], 1)
            if name == "tensor" and op in ("matmul", "transpose"):
                ctx.last_tensor = ins
            return ins

        return call

    def mark(self, sem, k=1):
        assert self._name == "tensor"
        self._ctx.last_tensor.inc(sem, k)

    def wait(self, sem, val=None):
        ctx = self._ctx
        v = ctx.sems.get(sem, 0) if val is None else val
        if v <= 0:
            return
        own = ctx.owner.get(sem)
        if _is_virtual(sem, ctx.owner):
            if v <= ctx.hist_floor.get(sem, 0):
                return
            cc = ctx.hist.get(sem, {}).get(v)
            if cc is None:
                raise RuntimeError(f"wait {sem}>={v}: no hist entry "
                                   f"(have {sorted(ctx.hist.get(sem, {}))[-4:]})")
            sem, v = "ch_" + own, cc
        key = (self._name, sem)
        if ctx.lastwait.get(key, -1) >= v:
            return
        ctx.lastwait[key] = v
        if ctx.active == self._name and ctx.dry == 0:
            getattr(ctx.nc, self._name).wait_ge(ctx.semh[sem], v)

    def pad_sem(self, sem, target):
        ctx = self._ctx
        cur = ctx.sems.get(sem, 0)
        if target > cur:
            if ctx.active == self._name and ctx.dry == 0:
                eng = getattr(ctx.nc, self._name)
                step = 16 if sem in DMA_SEMS else 1
                n = target - cur
                assert n % step == 0, (sem, cur, target)
                for _ in range(n // step):
                    eng.nop(cycle_cnt=4, nofuse=True).then_inc(ctx.semh[sem], step)
            ctx.sems[sem] = target


class Ctx:
    def __init__(self, nc, semh, active, owner):
        self.nc = nc
        self.semh = semh
        self.active = active
        self.owner = owner
        self.sems = {}
        self.hist = {}
        self.hist_floor = {}
        self.lastwait = {}
        self.dry = 0
        self.last_tensor = _Instr(self, None, False, 'tensor')
        self.sp = _Prox(self, "sync")
        self.v = _Prox(self, "vector")
        self.sc = _Prox(self, "scalar")
        self.t = _Prox(self, "tensor")
        self.g = _Prox(self, "gpsimd")
        self.prox = {"sync": self.sp, "vector": self.v, "scalar": self.sc,
                     "tensor": self.t, "gpsimd": self.g}

    def _snap(self):
        return (dict(self.sems), {k: dict(v) for k, v in self.hist.items()},
                dict(self.hist_floor), dict(self.lastwait))

    def _restore(self, snap):
        self.sems.clear()
        self.sems.update(snap[0])
        self.hist.clear()
        self.hist.update({k: dict(v) for k, v in snap[1].items()})
        self.hist_floor.clear()
        self.hist_floor.update(snap[2])
        self.lastwait.clear()
        self.lastwait.update(snap[3])

    def variant_region(self, engines, body_fn):
        snap = self._snap()
        ends = []
        for v in range(4):
            self._restore(snap)
            self.dry += 1
            body_fn(v)
            self.dry -= 1
            ends.append(self.sems)
        allsems = set()
        for e in ends:
            allsems |= set(e)
        incremented = sorted(k for k in allsems
                             if max(e.get(k, 0) for e in ends) > snap[0].get(k, 0))
        for k in incremented:
            if self.owner.get(k) not in engines:
                raise RuntimeError(
                    f"sem {k} (owner {self.owner.get(k)}) incremented in "
                    f"region with engines {engines}")

        def epilogue():
            for k in incremented:
                own = self.owner[k]
                if own in engines:
                    self.prox[own].wait(k)
            for e in engines:
                if e in _CHAINED:
                    self.prox[e].wait("ch_" + e)
                self.prox[e].drain()

        if self.active in engines and self.dry == 0:
            eng = getattr(self.nc, self.active)
            pid = eng.partition_id()
            idx = pid & 3
            for arm in eng.Switch(idx, 4):
                self._restore(snap)
                body_fn(arm)
                epilogue()
        else:
            self._restore(snap)
            self.dry += 1
            body_fn(0)
            epilogue()
            self.dry -= 1
        self._restore(snap)
        self.pending_region = (engines, incremented)

    def clear_region(self):
        engines, incremented = self.pending_region
        for k in incremented:
            own = self.owner[k]
            if self.active == own and self.dry == 0:
                getattr(self.nc, own).sem_clear(self.semh[k])
            self.sems[k] = 0
            self.hist.pop(k, None)
            self.hist_floor.pop(k, None)
            for kk in list(self.lastwait):
                if kk[1] == k:
                    del self.lastwait[kk]
        for e in engines:
            if e in _CHAINED:
                if self.active == e and self.dry == 0:
                    getattr(self.nc, e).sem_clear(self.semh["ch_" + e])
                self.sems["ch_" + e] = 0
                for kk in list(self.lastwait):
                    if kk[1] == "ch_" + e:
                        del self.lastwait[kk]
                for k in list(self.hist):
                    if self.owner.get(k) == e:
                        self.hist.pop(k)
                        self.hist_floor[k] = self.sems.get(k, 0)


SEM_OWNER = {
    "ld": "sync", "ngst": "sync", "ng": "vector",
    "wcast": "gpsimd", "vset": "gpsimd",
    "ch_vector": "vector", "ch_scalar": "scalar", "ch_gpsimd": "gpsimd",
    "p_x": "scalar", "p_stats": "vector", "p_rstd": "scalar", "p_xnt": "tensor",
    "p_xne": "scalar", "p_mm": "tensor", "p_pev": "scalar", "p_rot": "gpsimd",
    "p_kprod": "vector", "p_ktr": "tensor", "p_kte": "scalar", "p_kst": "scalar",
    "p_vprod": "gpsimd", "p_vst0": "gpsimd", "p_vst1": "gpsimd",
    "p_xnst0": "scalar", "p_xnst1": "scalar",
    "q_ld": "scalar", "q_mm": "tensor", "q_ev": "scalar", "q_prod": "vector",
    "q_tr": "tensor", "q_te": "scalar", "q_st": "scalar",
    "bar_v": "vector", "bar_t": "tensor", "bar_s": "scalar", "bar_g": "gpsimd",
    "rdone_vector": "vector", "rdone_scalar": "scalar", "rdone_tensor": "tensor",
    "rdone_gpsimd": "gpsimd", "rdone_sync": "sync",
    "a_kv0": "scalar", "a_kv1": "scalar",
    "a_qt": "scalar", "x_qb": "scalar", "x_st": "scalar", "dbst": "scalar",
    "x_mm": "tensor", "x_mx": "vector", "x_ls": "scalar",
    "x_tr": "tensor", "x_ev": "vector",
    "a_qk": "tensor", "a_exp": "scalar",
    "a_msk": "gpsimd", "a_av": "tensor", "a_evac": "vector",
    "a_ftr": "tensor", "a_fv": "vector", "a_ftr2": "tensor", "a_fe": "vector",
    "a_ynst": "scalar",
    "f_wld": "gpsimd", "f_xld": "scalar", "f_xt": "sync", "f_yld": "sync",
    "f_xold": "sync",
    "f_mm": "tensor", "f_ve": "vector", "f_act": "scalar", "f_ost": "sync", "f_res": "vector",
    "f_gel": "scalar",
}


def build_nc(phase=3, debug=False):
    nc = bass.Bass("TRN2", target_bir_lowering=False)

    def inp(name, shape, dt=F32):
        return nc.dram_tensor(name, shape, dt, kind="ExternalInput").ap()

    x_full = inp("x_full", [T, D])
    x_own = inp("x_own", [OWN, D])
    wabqT = inp("wabqT", [D, RQ * CH])
    wabkvT = inp("wabkvT", [D, 4 * CH])
    w_proj = inp("w_proj", [D, E16])
    w_c1 = inp("w_c1", [E16, RM])
    w_c2 = inp("w_c2", [E16, RM])
    w_in1 = inp("w_in1", [D, RM])
    w_b1 = inp("w_b1", [RM, 1])
    w_out1T = inp("w_out1T", [RM, OH])
    w_in2 = inp("w_in2", [OH, RM])
    w_b2 = inp("w_b2", [RM, 1])
    w_out2T = inp("w_out2T", [RM, D])
    w_oT = inp("w_oT", [D, D])
    c_identb = inp("c_identb", [128, 128], BF16)
    c_identf = inp("c_identf", [128, 128], F32)
    c_ropeC = inp("c_ropeC", [NT, 128, 32])
    c_ropeS = inp("c_ropeS", [NT, 128, 32])
    c_dmask = inp("c_dmask", [4, 128, 512], BF16)
    c_emx = inp("c_emx", [128, 56])
    c_auxmb = inp("c_auxmb", [128, 512])

    y_out = nc.dram_tensor("y_own_out", [OWN, D], F32, kind="ExternalOutput").ap()

    kv_k = nc.dram_tensor("kv_k", [NH, 65, T], BF16, kind="Internal").ap()
    kv_v = nc.dram_tensor("kv_v", [T, NH * 65], BF16, kind="Internal").ap()
    xnT_all = nc.dram_tensor("xnT_all", [8, 128, T], BF16, kind="Internal").ap()
    xnT_own = nc.dram_tensor("xnT_own", [8, 128, OWN], BF16, kind="Internal").ap()
    qT_d = nc.dram_tensor("qT_d", [8, 128, OWN], BF16, kind="Internal").ap()
    ynT_d = nc.dram_tensor("ynT_d", [8, 128, OWN], BF16, kind="Internal").ap()
    mu_d = nc.dram_tensor("mu_d", [64, 128], BF16, kind="Internal").ap()
    muT_d = nc.dram_tensor("muT_d", [16, OWN], BF16, kind="Internal").ap()

    dbg = {}
    if debug:
        def dout(name, shape, dt):
            dbg[name] = nc.dram_tensor("dbg_" + name, shape, dt, kind="ExternalOutput").ap()
        dout("xnT", [8, 128, T], BF16)
        dout("kvk", [NH, 65, T], BF16)
        dout("kvv", [T, NH * 65], BF16)
        dout("qT", [8, 128, OWN], BF16)
        dout("ynT", [8, 128, OWN], BF16)
        dout("yacc", [NST, 65, 16 * 512], F32)
        dout("mu", [NST, 1, 16 * 512], BF16)

    with contextlib.ExitStack() as stack:
        semh = {s: stack.enter_context(nc.semaphore(s)) for s in SEM_OWNER}

        def sb(name, shape, dt):
            return stack.enter_context(nc.sbuf_tensor(name, shape, dt)).ap()

        identb = sb("identb", [128, 128], BF16)
        identf = sb("identf", [128, 128], F32)
        dmask = sb("dmask", [128, 4 * 512], BF16)
        emx = sb("emx", [128, 56], F32)
        auxmb = sb("auxmb", [128, 512], F32)
        cols = sb("cols", [128, 16], F32)
        negrow = sb("negrow", [128, 1024], BF16)

        ARENA = 196 * 1024
        U8 = mybir.dt.uint8
        arena = sb("arena", [128, ARENA], U8)
        _esize = {F32: 4, BF16: 2}

        def mkregion():
            off = [0]

            def alloc(shape, dt):
                nb = shape[1] * _esize[dt]
                o = (off[0] + 31) // 32 * 32
                assert o + nb <= ARENA, f"arena overflow {o + nb}"
                off[0] = o + nb
                vw = arena[:, o:o + nb].bitcast(dt)
                if shape[0] != 128:
                    vw = vw[0:shape[0], :]
                return vw
            return alloc

        P = mkregion()
        ropeC = P([128, NT * 32], F32)
        ropeS = P([128, NT * 32], F32)
        wabq_b = P([128, 8 * 480], BF16)
        wabkv_b = P([128, 8 * 320], BF16)
        xt = P([128, 2 * 1024], F32)
        scr = P([128, 1024], F32)
        rscr = P([128, 2 * 256], F32)
        xnb = P([128, 1024], BF16)
        xnT_sb = P([128, 2 * 1024], BF16)
        akv = P([128, 2 * 64], F32)
        bkv = P([128, 2 * 256], F32)
        bkvr = P([128, 2 * 256], F32)
        Pt = P([128, 2 * 2048 + 1024], F32)
        ptb = P([128, 2 * 1024], BF16)
        Pvt = P([128, 1024], BF16)
        vstage = P([128, 2 * 1040], BF16)
        kT_sb = P([128, 2 * 1024], BF16)
        xq_t = P([128, 2 * 1024], BF16)
        aq = P([128, 2 * 96], F32)
        bq = P([128, 2 * 384], F32)
        qT_sb = P([128, 2 * 1024], BF16)
        kq = P([128, 1024], F32)
        ksq = P([128, 16], F32)
        kmax = P([128, 16], F32)
        kmaxT = P([16, 128], F32)
        km1 = P([16, 1], F32)
        km1r = P([1, 16], F32)
        kbr = P([128, 16], F32)
        ones1 = P([1, 128], F32)
        qsq = P([128, 16], F32)
        mu2 = P([128, 16], F32)
        muv = P([128, 16], F32)
        muT_sb = P([16, 128], BF16)

        A = mkregion()
        kr_k = A([65, 2 * 16 * 1024], BF16)
        kr_v = A([128, 2 * 8 * 1040], BF16)
        qbuf = A([64, 16 * 512], BF16)
        qT_ext = A([65, 2 * 16 * 512], BF16)
        rmx = A([128, 64], F32)
        mstage = A([64, 128], BF16)
        mtmp = A([128, 1], F32)
        junk = A([128, 512], F32)
        junk2 = A([128, 512], BF16)
        lsb = A([128, 512], F32)
        red32 = A([128, 32], F32)
        pTb = A([128, 2 * 1024], BF16)
        pT0 = pTb[:, 0:1024]
        pT1 = pTb[:, 1024:2048]
        y_acc = A([65, 16 * 512], F32)
        ytokn = A([128, 4 * 64], BF16)
        ynT_sb = A([64, 512], BF16)
        recipS = A([128, 4], F32)

        F = mkregion()
        w1b = F([128, 8 * 512], BF16)
        wo1b = F([128, 4 * 4096], BF16)
        w2b = F([128, 32 * 512], BF16)
        wo2b = F([128, 4 * 1024], BF16)
        woTb = F([128, 8 * 1024], BF16)
        projb = F([128, 8 * 16], BF16)
        c1b = F([16, 512], BF16)
        c2b = F([16, 512], BF16)
        b1c = F([128, 4], F32)
        b2c = F([128, 4], F32)
        fxn = F([128, 8 * 512], BF16)
        fyn = F([128, 8 * 512], BF16)
        fxo = F([128, 1024], F32)
        zproj = F([16, 512], F32)
        ztok = F([128, 64], F32)
        ecols = F([128, 24], F32)
        ezs = F([128, 64], F32)
        ez3 = F([128, 64], F32)
        etmp = F([128, 64], F32)
        etmp2 = F([128, 64], F32)
        ecs = F([128, 64], F32)
        ecs2 = F([128, 64], F32)
        ab16 = F([128, 64], BF16)
        aT_all = F([16, 4 * 512], BF16)
        g1T = F([128, 4 * 512], F32)
        g2T = F([128, 4 * 512], F32)
        z1b = F([128, 4 * 512], BF16)
        hb = F([128, 32 * 512], BF16)
        z2b = F([128, 4 * 512], BF16)
        outsb = F([128, 2 * 1024], F32)

        ps_qk = stack.enter_context(nc.psum_tensor("ps_qk", [128, 2048], F32)).ap()
        ps_s0 = ps_qk[:, 0:1024]
        ps_s1 = ps_qk[:, 1024:2048]
        ps_av = stack.enter_context(nc.psum_tensor("ps_av", [128, 1024], F32)).ap()
        ps_u = stack.enter_context(nc.psum_tensor("ps_u", [128, 512], F32)).ap()
        ps_x = stack.enter_context(nc.psum_tensor("ps_x", [128, 512], F32)).ap()

        def bfv(ps, bank):
            return ps[:, bank * 512:(bank + 1) * 512].bitcast(BF16)

        if True:
            def program(ctx: Ctx, sec: int):
                sp, v, sc, t, g = ctx.sp, ctx.v, ctx.sc, ctx.t, ctx.g
                S = ctx.sems

                def drain_dma():
                    for s_ in sorted(DMA_SEMS):
                        sp.wait(s_)

                if sec == 0:
                    run_init(ctx)
                elif sec == 1:
                    run_prep(ctx)
                elif sec == 2:
                    ctx.clear_region()
                elif sec == 3:
                    run_kfin(ctx)
                elif sec == 4:
                    run_qprep(ctx)
                elif sec == 5:
                    ctx.clear_region()
                elif sec == 6:
                    run_bias(ctx)
                elif sec == 7:
                    if phase < 2:
                        run_placeholder_out(ctx)
                    else:
                        run_attention(ctx)
                elif sec == 8:
                    ctx.clear_region()
                elif sec == 9:
                    run_postattn(ctx)
                elif sec == 10:
                    if phase < 3:
                        run_placeholder_out(ctx)
                    else:
                        run_ffn(ctx)
                else:
                    raise RuntimeError(sec)

            def run_init(ctx: Ctx):
                sp, v, sc, t, g = ctx.sp, ctx.v, ctx.sc, ctx.t, ctx.g
                S = ctx.sems
                sp.dma_start(out=identb[:], in_=c_identb[:]).inc("ld", 16)
                sp.dma_start(out=identf[:], in_=c_identf[:]).inc("ld", 16)
                sp.dma_start(out=ropeC.rearrange("p (g c) -> p g c", c=32),
                             in_=c_ropeC.rearrange("g p c -> p g c")).inc("ld", 16)
                sp.dma_start(out=ropeS.rearrange("p (g c) -> p g c", c=32),
                             in_=c_ropeS.rearrange("g p c -> p g c")).inc("ld", 16)
                sp.dma_start(out=dmask.rearrange("p (j c) -> p j c", c=512),
                             in_=c_dmask.rearrange("j p c -> p j c")).inc("ld", 16)
                sp.dma_start(out=emx[:], in_=c_emx[:]).inc("ld", 16)
                sp.dma_start(out=auxmb[:], in_=c_auxmb[:]).inc("ld", 16)
                g.dma_start(out=wabq_b.rearrange("p (c f) -> p c f", c=8),
                            in_=wabqT.rearrange("(c p) f -> p c f", p=128)).inc("wcast", 16)
                g.dma_start(out=wabkv_b.rearrange("p (c f) -> p c f", c=8),
                            in_=wabkvT.rearrange("(c p) f -> p c f", p=128)).inc("wcast", 16)
                g.memset(vstage.rearrange("p (b h c) -> p b h c", b=2, h=16)[:, :, :, 64:65],
                         1.0).inc("vset", 1)
                v.memset(ones1[0:1, :], 1.0)
                v.memset(negrow[:], -1.0).inc("ng", 1)
                sp.wait("ng")
                sp.dma_start(
                    out=kv_k[:, 64:65, :].rearrange("h o (p t) -> p h o t", p=128),
                    in_=negrow.rearrange("p (h o t) -> p h o t", h=16, o=1)
                ).inc("ngst", 16)

            def run_prep(ctx: Ctx):
                sp, v, sc, t, g = ctx.sp, ctx.v, ctx.sc, ctx.t, ctx.g
                S = ctx.sems
                def kv_prep(vv):
                    ntl = var_kv_tiles(vv)
                    v.memset(kmax[:, 0:16], 0.0)

                    def stage1(i):
                        pb = i % 2
                        xti = xt[:, pb * 1024:(pb + 1) * 1024]
                        sc.wait("p_stats", max(0, 2 * (i - 1)))
                        sc.dma_start(out=xti, in_=x_full[i * 128:(i + 1) * 128, :]).inc("p_x", 16)
                        v.wait("p_x")
                        if i == 0:
                            v.wait("wcast", 32)
                            v.wait("vset", 1)
                            v.wait("ld", 112)
                        v.reduce_sum(cols[:, 0:1], xti, axis=AX)
                        v.scalar_tensor_tensor(scr[:], xti, 1.0, xti, ALU.mult, ALU.mult,
                                               accum_out=cols[:, 1:2])
                        v.tensor_scalar(cols[:, 2:3], cols[:, 0:1], 1.0 / D, None, ALU.mult)
                        v.tensor_scalar(cols[:, 3:4], cols[:, 1:2], 1.0 / D, None, ALU.mult)
                        v.tensor_mul(cols[:, 4:5], cols[:, 2:3], cols[:, 2:3])
                        v.tensor_sub(cols[:, 5:6], cols[:, 3:4], cols[:, 4:5]).inc("p_stats", 1)
                        sc.wait("p_stats", 2 * i + 1)
                        sc.activation(cols[:, 6:7], cols[:, 5:6], AF.Ln, bias=emx[:, 48:49])
                        sc.activation(cols[:, 7:8], cols[:, 6:7], AF.Exp, scale=-0.5).inc("p_rstd", 1)
                        v.wait("p_rstd")
                        v.tensor_scalar(xnb[:], xti, cols[:, 2:3], cols[:, 7:8],
                                        ALU.subtract, ALU.mult).inc("p_stats", 1)
                        xnt_ps = bfv(ps_u if i % 2 == 0 else ps_x, 0)
                        t.wait("p_stats", 2 * i + 2)
                        t.wait("p_xne", max(0, i - 1))
                        for fc in range(8):
                            t.matmul(xnt_ps[:, fc * 128:(fc + 1) * 128],
                                     xnb[:, fc * 128:(fc + 1) * 128], identb[:],
                                     is_transpose=True, start=True, stop=True)
                        t.mark("p_xnt", 1)
                        xnti = xnT_sb[:, pb * 1024:(pb + 1) * 1024]
                        sc.wait("p_xnt")
                        sc.wait("p_mm", max(0, i - 1))
                        sc.wait("p_xnst0" if pb == 0 else "p_xnst1")
                        sc.activation(xnti, xnt_ps, AF.Copy, bias=0.0).inc("p_xne", 1)
                        sc.wait("p_xne")
                        sc.dma_start(out=xnT_all.rearrange("c p t -> p c t")[:, :, i * 128:(i + 1) * 128],
                                     in_=xnti.rearrange("p (c t) -> p c t", c=8)).inc("p_xnst0" if pb == 0 else "p_xnst1", 16)
                        t.wait("p_xne")
                        t.wait("p_pev", max(0, i - 1))
                        s0c = ps_s0[:, pb * 512:pb * 512 + 320]
                        for dc in range(8):
                            t.matmul(s0c, xnti[:, dc * 128:(dc + 1) * 128],
                                     wabkv_b[:, dc * 320:(dc + 1) * 320],
                                     start=(dc == 0), stop=(dc == 7))
                        t.mark("p_mm", 1)
                        akv_i = akv[:, pb * 64:(pb + 1) * 64]
                        bkv_i = bkv[:, pb * 256:(pb + 1) * 256]
                        bkvr_i = bkvr[:, pb * 256:(pb + 1) * 256]
                        sc.wait("p_mm")
                        sc.wait("p_vprod", max(0, i - 1))
                        sc.wait("p_rot", max(0, i - 1))
                        aps = s0c.rearrange("p (r c) -> p r c", r=4)
                        sc.activation(akv_i.rearrange("p (r h) -> p r h", r=4),
                                      aps[:, :, 0:16], AF.Copy, scale=0.5, bias=0.0)
                        sc.activation(bkv_i.rearrange("p (r c) -> p r c", r=4),
                                      aps[:, :, 16:80], AF.Copy, bias=0.0).inc("p_pev", 1)

                    def stage2(i):
                        pb = i % 2
                        akv_i = akv[:, pb * 64:(pb + 1) * 64]
                        bkv_i = bkv[:, pb * 256:(pb + 1) * 256]
                        bkvr_i = bkvr[:, pb * 256:(pb + 1) * 256]
                        g.wait("p_pev", i + 1)
                        g.wait("p_kprod", max(0, i - 1))
                        cosr = ropeC[:, i * 32:(i + 1) * 32].rearrange(
                            "p (o c) -> p o c", o=1).broadcast_to([128, 4, 32])
                        sinr = ropeS[:, i * 32:(i + 1) * 32].rearrange(
                            "p (o c) -> p o c", o=1).broadcast_to([128, 4, 32])
                        bk4 = bkv_i.rearrange("p (r c) -> p r c", r=4)
                        br4 = bkvr_i.rearrange("p (r c) -> p r c", r=4)
                        rs4 = rscr[:, pb * 256:(pb + 1) * 256].rearrange(
                            "p (r c) -> p r c", r=8)
                        g.tensor_tensor(br4[:, :, 0:32], bk4[:, :, 0:32], cosr, ALU.mult)
                        g.tensor_tensor(rs4[:, 0:4, :], bk4[:, :, 32:64], sinr, ALU.mult)
                        g.tensor_tensor(br4[:, :, 0:32], br4[:, :, 0:32], rs4[:, 0:4, :], ALU.add)
                        g.tensor_tensor(br4[:, :, 32:64], bk4[:, :, 32:64], cosr, ALU.mult)
                        g.tensor_tensor(rs4[:, 4:8, :], bk4[:, :, 0:32], sinr, ALU.mult)
                        g.tensor_tensor(br4[:, :, 32:64], br4[:, :, 32:64], rs4[:, 4:8, :],
                                        ALU.subtract).inc("p_rot", 1)
                        ptv = Pt[:, pb * 2048:(pb + 1) * 2048]
                        v.wait("p_rot", i + 1)
                        v.wait("p_ktr", max(0, i - 1))
                        for s_ in range(2):
                            a_b = akv_i[:, s_ * 16:(s_ + 1) * 16].rearrange(
                                "p (o h) -> p h o", o=1).broadcast_to([128, 16, 64])
                            b_b = bkvr_i[:, s_ * 64:(s_ + 1) * 64].rearrange(
                                "p (o c) -> p o c", o=1).broadcast_to([128, 16, 64])
                            v.tensor_tensor(
                                ptv[:, s_ * 1024:(s_ + 1) * 1024].rearrange("p (h c) -> p h c", h=16),
                                a_b, b_b, ALU.mult)
                        v.tensor_tensor(ptb[:, pb * 1024:(pb + 1) * 1024],
                                        ptv[:, 0:1024], ptv[:, 1024:2048],
                                        ALU.add)
                        v.nop().inc("p_kprod", 1)
                        ptbk = ptb[:, pb * 1024:(pb + 1) * 1024]
                        v.scalar_tensor_tensor(kq[:], ptbk, 1.0, ptbk,
                                               ALU.mult, ALU.mult)
                        v.tensor_reduce(ksq[:, 0:16],
                                        kq.rearrange("p (h c) -> p h c", h=16),
                                        axis=AX, op=ALU.add)
                        v.tensor_tensor(kmax[:, 0:16], kmax[:, 0:16],
                                        ksq[:, 0:16], ALU.max)
                        kt_ps = bfv(ps_s1 if pb == 0 else ps_av, 0)
                        t.wait("p_kprod", i + 1)
                        t.wait("p_kte", max(0, i - 1))
                        for fc in range(8):
                            t.matmul(kt_ps[:, fc * 128:(fc + 1) * 128],
                                     ptb[:, pb * 1024 + fc * 128:
                                         pb * 1024 + (fc + 1) * 128],
                                     identb[:], is_transpose=True,
                                     start=True, stop=True)
                        t.mark("p_ktr", 1)
                        kT_i = kT_sb[:, pb * 1024:(pb + 1) * 1024]
                        sc.wait("p_ktr", i + 1)
                        sc.wait("p_kst", 32 * i)
                        sc.activation(kT_i, kt_ps[:, 0:1024], AF.Copy,
                                      bias=0.0).inc("p_kte", 1)
                        sc.wait("p_kte", i + 1)
                        kv2 = kv_k.rearrange("(c two) r t -> two r c t", two=2)
                        for two in range(2):
                            sc.dma_start(
                                out=kv2[two][0:64, :, i * 128:(i + 1) * 128],
                                in_=kT_i[64 * two:64 * two + 64, :].rearrange(
                                    "p (c t) -> p c t", c=8)).inc("p_kst", 16)
                        vst = vstage[:, pb * 1040:(pb + 1) * 1040]
                        vsh = vst.rearrange("p (h c) -> p h c", h=16)[:, :, 0:64]
                        g.wait("p_vst0" if pb == 0 else "p_vst1")
                        for s_ in (2, 3):
                            a_b = akv_i[:, s_ * 16:(s_ + 1) * 16].rearrange(
                                "p (o h) -> p h o", o=1).broadcast_to([128, 16, 64])
                            b_b = bkvr_i[:, s_ * 64:(s_ + 1) * 64].rearrange(
                                "p (o c) -> p o c", o=1).broadcast_to([128, 16, 64])
                            if s_ == 2:
                                g.tensor_tensor(vsh, a_b, b_b, ALU.mult)
                            else:
                                g.tensor_tensor(Pvt.rearrange("p (h c) -> p h c", h=16),
                                                a_b, b_b, ALU.mult)
                                g.tensor_tensor(vsh, vsh,
                                                Pvt.rearrange("p (h c) -> p h c", h=16),
                                                ALU.add)
                        g.nop().inc("p_vprod", 1)
                        g.wait("ch_gpsimd")
                        g.dma_start(out=kv_v[i * 128:(i + 1) * 128, :], in_=vst).inc("p_vst0" if pb == 0 else "p_vst1", 16)

                    for i in range(ntl + 1):
                        if i < ntl:
                            stage1(i)
                        if i >= 1:
                            stage2(i - 1)

                ctx.variant_region(("vector", "scalar", "tensor", "gpsimd"), kv_prep)


            def run_kfin(ctx: Ctx):
                pass

            def run_qprep(ctx: Ctx):
                sp, v, sc, t, g = ctx.sp, ctx.v, ctx.sc, ctx.t, ctx.g
                S = ctx.sems
                def q_prep(vv):
                    qtr_n = [S.get("q_tr", 0)]
                    t.matmul(ps_u[0:16, 0:128], kmax[:, 0:16], identf[:],
                             is_transpose=True, start=True, stop=True)
                    t.mark("x_tr", 1)
                    v.wait("x_tr", S.get("x_tr", 0))
                    v.tensor_copy(kmaxT[0:16, :], ps_u[0:16, 0:128])
                    v.tensor_reduce(km1[0:16, 0:1], kmaxT[0:16, :], axis=AX,
                                    op=ALU.max)
                    v.nop().inc("x_mx", 1)
                    t.wait("x_mx", S.get("x_mx", 0))
                    t.matmul(ps_u[0:1, 128:144], km1[0:16, 0:1],
                             identf[0:16, 0:16], is_transpose=True,
                             start=True, stop=True)
                    t.mark("x_tr", 1)
                    v.wait("x_tr", S.get("x_tr", 0))
                    v.tensor_copy(km1r[0:1, :], ps_u[0:1, 128:144])
                    v.nop().inc("x_mx", 1)
                    t.wait("x_mx", S.get("x_mx", 0))
                    t.matmul(ps_u[0:128, 256:272], ones1[0:1, :], km1r[0:1, :],
                             start=True, stop=True)
                    t.mark("x_tr", 1)
                    v.wait("x_tr", S.get("x_tr", 0))
                    v.tensor_copy(kbr[:, 0:16], ps_u[:, 256:272])
                    v.nop().inc("x_mx", 1)

                    def stage1(o):
                        gt = own_tile_global(vv, o)
                        pb = o % 2
                        xq_i = xq_t[:, pb * 1024:(pb + 1) * 1024]
                        sc.wait("q_mm", max(0, o - 1))
                        sc.dma_start(out=xq_i.rearrange("p (c t) -> p c t", c=8),
                                     in_=xnT_all.rearrange("c p t -> p c t")[:, :, gt * 128:(gt + 1) * 128]).inc("q_ld", 16)
                        sc.dma_start(out=xnT_own[:, :, o * 128:(o + 1) * 128],
                                     in_=xnT_all[:, :, gt * 128:(gt + 1) * 128]).inc("q_ld", 16)
                        t.wait("q_ld")
                        t.wait("q_ev", max(0, o - 1))
                        s0c = ps_s0[:, pb * 512:pb * 512 + 480]
                        for dc in range(8):
                            t.matmul(s0c, xq_i[:, dc * 128:(dc + 1) * 128],
                                     wabq_b[:, dc * 480:(dc + 1) * 480],
                                     start=(dc == 0), stop=(dc == 7))
                        t.mark("q_mm", 1)
                        sc.wait("q_mm")
                        sc.wait("q_prod", max(0, o - 1))
                        apq = s0c.rearrange("p (r c) -> p r c", r=6)
                        sc.activation(aq[:, pb * 96:(pb + 1) * 96].rearrange(
                                          "p (r h) -> p r h", r=6),
                                      apq[:, :, 0:16], AF.Copy, scale=1.0 / RQ,
                                      bias=0.0)
                        sc.activation(bq[:, pb * 384:(pb + 1) * 384].rearrange(
                                          "p (r c) -> p r c", r=6),
                                      apq[:, :, 16:80], AF.Copy, bias=0.0).inc("q_ev", 1)

                    qacc = Pt[:, 4096:5120]
                    qacc4 = qacc.rearrange("p (h c) -> p h c", h=16)

                    def stage2(o):
                        gt = own_tile_global(vv, o)
                        pb = o % 2
                        qt_ps = bfv(ps_s1 if pb == 0 else ps_av, 0)
                        ptbq = ptb[:, pb * 1024:(pb + 1) * 1024]
                        v.wait("q_ev", o + 1)
                        for r in range(6):
                            pr = r % 2
                            a_b = aq[:, pb * 96 + r * 16:pb * 96 + (r + 1) * 16].rearrange(
                                "p (o h) -> p h o", o=1).broadcast_to([128, 16, 64])
                            b_b = bq[:, pb * 384 + r * 64:pb * 384 + (r + 1) * 64].rearrange(
                                "p (o c) -> p o c", o=1).broadcast_to([128, 16, 64])
                            if r == 0:
                                v.tensor_tensor(qacc4, a_b, b_b, ALU.mult)
                            else:
                                v.tensor_tensor(
                                    Pt[:, pr * 1024:(pr + 1) * 1024].rearrange(
                                        "p (h c) -> p h c", h=16),
                                    a_b, b_b, ALU.mult)
                                if r < 5:
                                    v.scalar_tensor_tensor(
                                        qacc, Pt[:, pr * 1024:(pr + 1) * 1024],
                                        1.0, qacc, ALU.mult, ALU.add)
                                else:
                                    v.wait("q_tr", max(0, o - 1))
                                    v.scalar_tensor_tensor(
                                        ptbq, Pt[:, pr * 1024:(pr + 1) * 1024],
                                        1.0, qacc, ALU.mult, ALU.add)
                        v.nop().inc("q_prod", 1)
                        t.wait("q_prod", o + 1)
                        t.wait("q_te", max(0, o - 1))
                        for fc in range(8):
                            t.matmul(qt_ps[:, fc * 128:(fc + 1) * 128],
                                     ptbq[:, fc * 128:(fc + 1) * 128],
                                     identb[:], is_transpose=True,
                                     start=True, stop=True)
                        t.mark("q_tr", 1)
                        qT_i = qT_sb[:, pb * 1024:(pb + 1) * 1024]
                        sc.wait("q_tr", o + 1)
                        sc.wait("q_st", 16 * o)
                        sc.activation(qT_i, qt_ps[:, 0:1024], AF.Copy,
                                      bias=0.0).inc("q_te", 1)
                        sc.wait("q_te", o + 1)
                        sc.dma_start(out=qT_d.rearrange("c p t -> p c t")[:, :, o * 128:(o + 1) * 128],
                                     in_=qT_i.rearrange("p (c t) -> p c t", c=8)).inc("q_st", 16)
                        v.wait("x_ls", S.get("x_ls", 0))
                        v.scalar_tensor_tensor(kq[:], ptbq, 1.0, ptbq,
                                               ALU.mult, ALU.mult)
                        v.tensor_reduce(qsq[:, 0:16],
                                        kq.rearrange("p (h c) -> p h c", h=16),
                                        axis=AX, op=ALU.add)
                        v.tensor_tensor(mu2[:, 0:16], qsq[:, 0:16],
                                        kbr[:, 0:16], ALU.mult)
                        v.nop().inc("x_mx", 1)
                        sc.wait("x_mx", S.get("x_mx", 0))
                        sc.activation(qsq[:, 0:16], mu2[:, 0:16], AF.Ln,
                                      bias=emx[:, 49:50])
                        sc.activation(muv[:, 0:16], qsq[:, 0:16], AF.Exp,
                                      scale=0.5, bias=emx[:, 54:55]).inc("x_ls", 1)
                        t.wait("x_ls", S.get("x_ls", 0))
                        t.wait("x_mx", S.get("x_mx", 0))
                        t.matmul(ps_u[0:16, 0:128], muv[:, 0:16], identf[:],
                                 is_transpose=True, start=True, stop=True)
                        t.mark("x_tr", 1)
                        sc.wait("x_tr", S.get("x_tr", 0))
                        sc.wait("x_st", S.get("x_st", 0))
                        sc.activation(muT_sb[0:16, :], ps_u[0:16, 0:128],
                                      AF.Copy, bias=0.0).inc("x_ls", 1)
                        sc.dma_start(out=muT_d[:, o * 128:(o + 1) * 128],
                                     in_=muT_sb[0:16, :]).inc("x_st", 16)

                    for o in range(NOT_ + 1):
                        if o < NOT_:
                            stage1(o)
                        if o >= 1:
                            stage2(o - 1)

                ctx.variant_region(("vector", "scalar", "tensor"), q_prep)

            def run_bias(ctx: Ctx):
                sp, v, sc, t, g = ctx.sp, ctx.v, ctx.sc, ctx.t, ctx.g
                if debug:
                    sp.wait("ngst", ctx.sems.get("ngst", 0))
                    sp.wait("ld", ctx.sems.get("ld", 0))
                    sp.dma_start(out=dbg["xnT"][:], in_=xnT_all[:]).inc("ld", 16)
                    sp.dma_start(out=dbg["kvk"][:], in_=kv_k[:]).inc("ld", 16)
                    sp.dma_start(out=dbg["kvv"][:], in_=kv_v[:]).inc("ld", 16)
                    sp.dma_start(out=dbg["qT"][:], in_=qT_d[:]).inc("ld", 16)

            def run_placeholder_out(ctx: Ctx):
                sp, v, sc, t, g = ctx.sp, ctx.v, ctx.sc, ctx.t, ctx.g
                S = ctx.sems

                def drain_dma():
                    for s_ in sorted(DMA_SEMS):
                        sp.wait(s_)
                v.memset(outsb[:, 0:1024], 0.0).inc("f_res", 1)
                sp.wait("f_res")
                for ti in range(NOT_):
                    sp.dma_start(out=y_out[ti * 128:(ti + 1) * 128, :],
                                 in_=outsb[:, 0:1024]).inc("f_ost", 16)
                sp.wait("f_ost")
                drain_dma()
                v.wait("f_ost")
                t.wait("f_ost")
                sc.wait("f_ost")
                g.wait("f_ost")

            def run_attention(ctx: Ctx):
                sp, v, sc, t, g = ctx.sp, ctx.v, ctx.sc, ctx.t, ctx.g
                S = ctx.sems
                def attention(vv):
                    nblk = [stripe_nblk(vv, s_) for s_ in range(NST)]
                    CC = [(n + 7) // 8 for n in nblk]
                    gts = {(s_, qt): own_tile_global(vv, 4 * s_ + qt)
                           for s_ in range(NST) for qt in range(4)}
                    avcnt = [0]
                    ucnt = [0]
                    pend_av = []
                    last_expS = {0: 0, 1: 0}
                    last_avpT = {0: 0, 1: 0}
                    kbusy = {}
                    vbusy = {}
                    akv_end = {}
                    qk_end = {}

                    def kvload(ci, nbig, with_v, nv=0):
                        blocks = min(8, nbig - 8 * ci)
                        slot = ci % 2
                        sem = "a_kv%d" % slot
                        sc.wait(sem, S.get(sem, 0))
                        if slot in kbusy:
                            sc.wait("a_qk", kbusy[slot])
                        g0 = ci * 1024
                        sc.dma_start(
                            out=kr_k[:, slot * 16384:(slot + 1) * 16384].rearrange(
                                "r (h t) -> r h t", h=16)[:, :, 0:blocks * 128],
                            in_=kv_k.rearrange("h r t -> r h t")[:, :, g0:g0 + blocks * 128]
                        ).inc(sem, 16)
                        if with_v:
                            bv = min(8, nv - 8 * ci)
                            if slot in vbusy:
                                sc.wait("a_av", vbusy[slot])
                            sc.dma_start(
                                out=kr_v[:, slot * 8320:slot * 8320 + bv * 1040].rearrange(
                                    "p (b c) -> p b c", b=bv),
                                in_=kv_v[g0:g0 + bv * 128, :].rearrange(
                                    "(b p) c -> p b c", p=128)).inc(sem, 16)
                        akv_end[ci] = (sem, S.get(sem, 0))

                    def load_qext_rows(sa):
                        sc.wait("a_qt", S.get("a_qt", 0))
                        if sa > 1:
                            sc.wait("a_qk", qk_end[sa - 2])
                        par = sa % 2
                        for h in range(16):
                            sc.dma_start(
                                out=qT_ext[0:64, par * 8192 + h * 512:
                                           par * 8192 + (h + 1) * 512],
                                in_=qT_d[h // 2][64 * (h % 2):64 * (h % 2) + 64,
                                                 sa * 512:(sa + 1) * 512]).inc("a_qt", 16)
                        sc.wait("x_st", S.get("x_st", 0))
                        sc.dma_start(
                            out=qT_ext[64:65, par * 8192:(par + 1) * 8192].rearrange(
                                "o (h t) -> o h t", h=16),
                            in_=muT_d[:, sa * 512:(sa + 1) * 512].rearrange(
                                "(o h) t -> o h t", o=1)).inc("x_st", 16)

                    def emit_av(pa):
                        (s_, ci, h, bp, blocks, slot, up, expn, mskn) = pa
                        first = (bp == 0)
                        avs = avcnt[0] % 2
                        pT = pT0 if up == 0 else pT1
                        t.wait("a_exp", expn)
                        if mskn:
                            t.wait("a_msk", mskn)
                        if first:
                            t.wait("a_evac", max(0, avcnt[0] - 1))
                            if ci == 0 and h == 0:
                                t.wait("a_fv", S.get("a_fv", 0))
                        b0 = 2 * bp
                        for j in (0, 1):
                            vco = slot * 8320 + (b0 + j) * 1040 + h * 65
                            t.matmul(ps_av[0:65, avs * 512:(avs + 1) * 512],
                                     kr_v[:, vco:vco + 65],
                                     pT[:, j * 512:(j + 1) * 512],
                                     start=(first and j == 0),
                                     stop=(bp == blocks // 2 - 1 and j == 1))
                        t.mark("a_av", 1)
                        last_avpT[up] = S["a_av"]
                        if bp == blocks // 2 - 1:
                            v.wait("a_av")
                            ya = y_acc[:, h * 512:(h + 1) * 512]
                            if ci == 0:
                                if debug and h == 0:
                                    v.wait("dbst", S.get("dbst", 0))
                                v.tensor_copy(ya, ps_av[0:65, avs * 512:(avs + 1) * 512])
                            else:
                                v.scalar_tensor_tensor(
                                    ya, ps_av[0:65, avs * 512:(avs + 1) * 512],
                                    1.0, ya, ALU.mult, ALU.add)
                            v.nop().inc("a_evac", 1)
                            avcnt[0] += 1

                    def flush_av(all_=True):
                        while pend_av and (all_ or len(pend_av) > 1):
                            emit_av(pend_av.pop(0))

                    def main_chunk(s_, ci):
                        blocks = min(8, nblk[s_] - 8 * ci)
                        slot = ci % 2
                        N = nblk[s_]
                        for h in range(16):
                            for bpp in range(blocks // 4):
                                t.wait(*akv_end[ci])
                                if ci == 0 and h == 0 and bpp == 0:
                                    t.wait("a_qt", S.get("a_qt", 0))
                                    t.wait("x_st", S.get("x_st", 0))
                                    t.wait("a_fe", S.get("a_fe", 0))
                                t.wait("a_exp", last_expS[0])
                                qpar = (s_ % 2) * 8192
                                for u in (0, 1):
                                    b0 = 2 * (2 * bpp + u)
                                    Sps = ps_s0 if u == 0 else ps_s1
                                    for j in (0, 1):
                                        co = slot * 16384 + h * 1024 + (b0 + j) * 128
                                        t.matmul(Sps[:, j * 512:(j + 1) * 512],
                                                 kr_k[0:65, co:co + 128],
                                                 qT_ext[0:65, qpar + h * 512:
                                                        qpar + (h + 1) * 512],
                                                 start=True, stop=True)
                                t.mark("a_qk", 1)
                                flush_av()
                                sc.wait("a_qk")
                                sc.wait("a_av", last_avpT[0])
                                sc.wait("a_av", last_avpT[1])
                                sc.activation(pTb[:], ps_qk[:], AF.Exp,
                                              bias=emx[:, 53:54],
                                              scale=0.125).inc("a_exp", 1)
                                last_expS[0] = S["a_exp"]
                                ndiag = 0
                                for u in (0, 1):
                                    pT = pT0 if u == 0 else pT1
                                    b0 = 2 * (2 * bpp + u)
                                    for j in (0, 1):
                                        blk = ci * 8 + b0 + j
                                        dj = blk - (N - 4)
                                        if dj >= 0:
                                            g.wait("a_exp")
                                            g.tensor_tensor(pT[:, j * 512:(j + 1) * 512],
                                                            pT[:, j * 512:(j + 1) * 512],
                                                            dmask[:, dj * 512:(dj + 1) * 512],
                                                            ALU.mult)
                                            ndiag += 1
                                if ndiag:
                                    g.nop().inc("a_msk", 1)
                                mskv = S.get("a_msk", 0) if ndiag else 0
                                for u in (0, 1):
                                    pend_av.append(
                                        (s_, ci, h, 2 * bpp + u, blocks, slot, u,
                                         S["a_exp"], mskv))
                                yield

                    def y_finalize(s_):
                        if debug:
                            sc.wait("dbst", S.get("dbst", 0))
                            sc.wait("a_evac", S.get("a_evac", 0))
                            sc.dma_start(out=dbg["yacc"][s_], in_=y_acc[:]).inc("dbst", 16)
                        for h in range(16):
                            ya = y_acc[:, h * 512:(h + 1) * 512]
                            t.wait("a_evac")
                            t.wait("a_fv", S.get("a_fv", 0))
                            for q4 in range(4):
                                t.matmul(ps_av[:, q4 * 128:q4 * 128 + 65],
                                         ya[:, q4 * 128:(q4 + 1) * 128], identf[0:65, 0:65],
                                         is_transpose=True, start=True, stop=True)
                            t.mark("a_ftr", 1)
                            v.wait("a_ftr")
                            for q4 in range(4):
                                pu = ps_av[:, q4 * 128:q4 * 128 + 65]
                                v.reciprocal(recipS[:, q4:q4 + 1], pu[:, 64:65])
                                v.tensor_scalar(ytokn[:, q4 * 64:(q4 + 1) * 64],
                                                pu[:, 0:64], recipS[:, q4:q4 + 1],
                                                None, ALU.mult)
                            v.nop().inc("a_fv", 1)
                            t.wait("a_fv")
                            t.wait("a_fe", S.get("a_fe", 0))
                            ynps = bfv(ps_s1, 1)
                            for q4 in range(4):
                                t.matmul(ynps[0:64, q4 * 128:(q4 + 1) * 128],
                                         ytokn[:, q4 * 64:(q4 + 1) * 64], identb[:],
                                         is_transpose=True, start=True, stop=True)
                            t.mark("a_ftr2", 1)
                            v.wait("a_ftr2")
                            v.wait("a_ynst", max(0, S.get("a_ynst", 0)))
                            v.tensor_copy(ynT_sb[:], ynps[0:64, 0:512]).inc("a_fe", 1)
                            sc.wait("a_fe")
                            sc.wait("a_ynst", S.get("a_ynst", 0))
                            sc.dma_start(out=ynT_d[h // 2][64 * (h % 2):64 * (h % 2) + 64,
                                                           s_ * 512:(s_ + 1) * 512],
                                         in_=ynT_sb[:]).inc("a_ynst", 16)

                    sc.wait("ld", S.get("ld", 0))
                    sc.wait("ngst", S.get("ngst", 0))
                    load_qext_rows(0)
                    kvload(0, nblk[0], True, nblk[0])
                    for s_ in range(NST):
                        for ci in range(CC[s_]):
                            if ci + 1 < CC[s_]:
                                kvload(ci + 1, nblk[s_], True, nblk[s_])
                            for _ in main_chunk(s_, ci):
                                pass
                            flush_av()
                            kbusy[ci % 2] = S.get("a_qk", 0)
                            vbusy[ci % 2] = S.get("a_av", 0)
                        qk_end[s_] = S.get("a_qk", 0)
                        if debug:
                            sc.wait("dbst", S.get("dbst", 0))
                            sc.dma_start(out=dbg["mu"][s_],
                                         in_=qT_ext[64:65, (s_ % 2) * 8192:
                                                    (s_ % 2 + 1) * 8192]).inc("dbst", 16)
                        if s_ < 3:
                            load_qext_rows(s_ + 1)
                            kvload(0, nblk[s_ + 1], True, nblk[s_ + 1])
                        y_finalize(s_)

                ctx.variant_region(("vector", "scalar", "tensor", "gpsimd"), attention)

            def run_postattn(ctx: Ctx):
                sp, v, sc, t, g = ctx.sp, ctx.v, ctx.sc, ctx.t, ctx.g
                if debug:
                    sp.dma_start(out=dbg["ynT"][:], in_=ynT_d[:]).inc("ld", 16)

            def run_ffn(ctx: Ctx):
                sp, v, sc, t, g = ctx.sp, ctx.v, ctx.sc, ctx.t, ctx.g
                S = ctx.sems

                def drain_dma():
                    for s_ in sorted(DMA_SEMS):
                        sp.wait(s_)
                def wld(out, in_, **kw):
                    g.wait("f_wld", S.get("f_wld", 0))
                    g.dma_start(out=out, in_=in_, **kw).inc("f_wld", 16)
                wld(projb.rearrange("p (c f) -> p c f", c=8),
                    w_proj.rearrange("(c p) f -> p c f", p=128))
                wld(c1b[:], w_c1[:])
                wld(c2b[:], w_c2[:])
                wld(w1b.rearrange("p (c f) -> p c f", c=8),
                    w_in1.rearrange("(c p) f -> p c f", p=128))
                wld(wo1b.rearrange("p (c f) -> p c f", c=4),
                    w_out1T.rearrange("(c p) f -> p c f", p=128))
                wld(w2b.rearrange("p (c f) -> p c f", c=32),
                    w_in2.rearrange("(c p) f -> p c f", p=128))
                wld(wo2b.rearrange("p (c f) -> p c f", c=4),
                    w_out2T.rearrange("(c p) f -> p c f", p=128))
                wld(woTb.rearrange("p (c f) -> p c f", c=8),
                    w_oT.rearrange("(c p) f -> p c f", p=128))
                wld(b1c.rearrange("p c -> p c"),
                    w_b1.rearrange("(c p) o -> p (c o)", p=128),
                    allow_slow_non_contiguous=True)
                wld(b2c.rearrange("p c -> p c"),
                    w_b2.rearrange("(c p) o -> p (c o)", p=128),
                    allow_slow_non_contiguous=True)

                inv_k = emx[:, 0:16]
                kvec = emx[:, 16:32]
                iota16 = emx[:, 32:48]

                for ti in range(4):
                    sc.wait("f_act", S.get("f_act", 0))
                    sc.wait("f_xld", S.get("f_xld", 0))
                    sc.wait("f_mm", S.get("f_mm", 0))
                    sc.dma_start(out=fxn.rearrange("p (c f) -> p c f", c=8),
                                 in_=xnT_own.rearrange("c p t -> p c t")[:, :, ti * 512:(ti + 1) * 512]).inc("f_xld", 16)
                    t.wait("f_xld", 16 * (ti + 1))
                    if ti == 0:
                        t.wait("f_wld", 16 * 3)
                    t.wait("f_ve", S.get("f_ve", 0))
                    for dc in range(8):
                        t.matmul(ps_u[0:16, 0:512], projb[:, dc * 16:(dc + 1) * 16],
                                 fxn[:, dc * 512:(dc + 1) * 512],
                                 start=(dc == 0), stop=(dc == 7))
                    t.mark("f_mm", 1)
                    v.wait("f_mm")
                    v.tensor_copy(zproj[:], ps_u[0:16, 0:512]).inc("f_ve", 1)
                    t.wait("f_ve")
                    for q4 in range(4):
                        t.matmul(ps_s0[:, q4 * 128:q4 * 128 + 16],
                                 zproj[0:16, q4 * 128:(q4 + 1) * 128], identf[0:16, 0:16],
                                 is_transpose=True, start=True, stop=True)
                    t.mark("f_mm", 1)
                    v.wait("f_mm")
                    for q4 in range(4):
                        v.tensor_copy(ztok[:, q4 * 16:(q4 + 1) * 16],
                                      ps_s0[:, q4 * 128:q4 * 128 + 16])
                    v.nop().inc("f_ve", 1)
                    z4 = ztok.rearrange("p (g e) -> p g e", g=4)
                    v.tensor_reduce(ecols[:, 0:4], z4, axis=AX, op=ALU.add)
                    v.tensor_scalar(ecols[:, 0:4], ecols[:, 0:4], 1.0 / 16, None, ALU.mult)
                    mb = ecols[:, 0:4].rearrange("p (g e) -> p g e", e=1).broadcast_to([128, 4, 16])
                    ez4 = ezs.rearrange("p (g e) -> p g e", g=4)
                    v.tensor_tensor(ez4, z4, mb, ALU.subtract)
                    t4_ = etmp.rearrange("p (g e) -> p g e", g=4)
                    v.tensor_tensor(t4_, ez4, ez4, ALU.mult)
                    v.tensor_reduce(ecols[:, 4:8], t4_, axis=AX, op=ALU.add)
                    v.tensor_scalar(ecols[:, 4:8], ecols[:, 4:8], 1.0 / 16, None,
                                    ALU.mult).inc("f_ve", 1)
                    sc.wait("f_ve")
                    sc.activation(ecols[:, 8:12], ecols[:, 4:8], AF.Ln, bias=emx[:, 48:49])
                    sc.activation(ecols[:, 12:16], ecols[:, 8:12], AF.Exp,
                                  scale=-0.5).inc("f_act", 1)
                    v.wait("f_act")
                    rb = ecols[:, 12:16].rearrange("p (g e) -> p g e", e=1).broadcast_to([128, 4, 16])
                    v.tensor_tensor(ez4, ez4, rb, ALU.mult)
                    v.tensor_scalar(ezs[:], ezs[:], 0.5, None, ALU.mult)
                    v.tensor_reduce(ecols[:, 16:20], ez4, axis=AX, op=ALU.max)
                    xb_ = ecols[:, 16:20].rearrange("p (g e) -> p g e", e=1).broadcast_to([128, 4, 16])
                    v.tensor_tensor(ez4, ez4, xb_, ALU.subtract)
                    v.tensor_copy(ez3[:], ezs[:])
                    for rnd in range(16):
                        off = rnd % 2
                        n = (16 - off) // 2
                        src = ezs.rearrange("p (g e) -> p g e", g=4)[:, :, off:off + 2 * n]
                        pa = src.rearrange("p g (e2 two) -> p g e2 two", two=2)[:, :, :, 0:1]
                        pb_ = src.rearrange("p g (e2 two) -> p g e2 two", two=2)[:, :, :, 1:2]
                        ta = etmp.rearrange("p (g e) -> p g e", g=4)[:, :, 0:n].rearrange(
                            "p g (e2 o) -> p g e2 o", o=1)
                        v.tensor_tensor(ta, pa, pb_, ALU.max)
                        v.tensor_tensor(pb_, pa, pb_, ALU.min)
                        v.tensor_copy(pa, ta)
                    cs4 = ecs.rearrange("p (g e) -> p g e", g=4)
                    cq4 = ecs2.rearrange("p (g e) -> p g e", g=4)
                    v.tensor_tensor(cq4, ez4, ez4, ALU.mult)
                    v.tensor_copy(ecs[:], ezs[:])
                    for d_ in (1, 2, 4, 8):
                        for buf in (ecs, ecs2):
                            c4 = buf.rearrange("p (g e) -> p g e", g=4)
                            tt4 = etmp2.rearrange("p (g e) -> p g e", g=4)
                            v.tensor_copy(tt4[:, :, 0:16 - d_], c4[:, :, 0:16 - d_])
                            v.tensor_tensor(c4[:, :, d_:16], c4[:, :, d_:16],
                                            tt4[:, :, 0:16 - d_], ALU.add)
                    ik4 = inv_k.rearrange("p (o e) -> p o e", o=1).broadcast_to([128, 4, 16])
                    kv4 = kvec.rearrange("p (o e) -> p o e", o=1).broadcast_to([128, 4, 16])
                    v.tensor_tensor(cs4, cs4, ik4, ALU.mult)
                    v.tensor_tensor(cq4, cq4, ik4, ALU.mult)
                    v.tensor_tensor(t4_, cs4, cs4, ALU.mult)
                    v.tensor_tensor(t4_, cq4, t4_, ALU.subtract)
                    v.tensor_tensor(t4_, t4_, kv4, ALU.mult)
                    v.tensor_scalar(etmp[:], etmp[:], -1.0, 1.0, ALU.mult, ALU.add)
                    v.tensor_tensor(t4_, t4_, ik4, ALU.mult)
                    v.tensor_scalar(etmp[:], etmp[:], 0.0, None, ALU.max).inc("f_ve", 1)
                    sc.wait("f_ve")
                    sc.activation(etmp2[:], etmp[:], AF.Ln, bias=emx[:, 50:51])
                    sc.activation(etmp2[:], etmp2[:], AF.Exp, scale=0.5).inc("f_act", 1)
                    v.wait("f_act")
                    t24 = etmp2.rearrange("p (g e) -> p g e", g=4)
                    v.tensor_tensor(t24, cs4, t24, ALU.subtract)
                    s4_ = cs4
                    v.tensor_tensor(s4_, t24, ez4, ALU.is_le)
                    v.tensor_reduce(ecols[:, 20:24], s4_, axis=AX, op=ALU.add)
                    v.tensor_scalar(ecols[:, 20:24], ecols[:, 20:24], -1.0, None, ALU.add)
                    io4 = iota16.rearrange("p (o e) -> p o e", o=1).broadcast_to([128, 4, 16])
                    sbx = ecols[:, 20:24].rearrange("p (g e) -> p g e", e=1).broadcast_to([128, 4, 16])
                    v.tensor_tensor(s4_, io4, sbx, ALU.is_equal)
                    v.tensor_tensor(s4_, s4_, t24, ALU.mult)
                    v.tensor_reduce(ecols[:, 16:20], s4_, axis=AX, op=ALU.add)
                    tb_ = ecols[:, 16:20].rearrange("p (g e) -> p g e", e=1).broadcast_to([128, 4, 16])
                    e34 = ez3.rearrange("p (g e) -> p g e", g=4)
                    v.tensor_tensor(e34, e34, tb_, ALU.subtract)
                    v.tensor_scalar(ez3[:], ez3[:], 0.0, None, ALU.max)
                    v.tensor_tensor(e34, e34, e34, ALU.mult)
                    v.tensor_copy(ab16[:], ez3[:]).inc("f_ve", 1)
                    t.wait("f_ve")
                    aps_ = bfv(ps_u, 0)
                    for q4 in range(4):
                        t.matmul(aps_[0:16, q4 * 128:(q4 + 1) * 128],
                                 ab16[:, q4 * 16:(q4 + 1) * 16], identb[:],
                                 is_transpose=True, start=True, stop=True)
                    t.mark("f_mm", 1)
                    v.wait("f_mm")
                    v.tensor_copy(aT_all[0:16, ti * 512:(ti + 1) * 512],
                                  aps_[0:16, 0:512]).inc("f_ve", 1)

                for ti in range(4):
                    sp.wait("f_ost", max(0, S.get("f_ost", 0) - 16))
                    sp.wait("f_xt", S.get("f_xt", 0))
                    sp.wait("f_yld", S.get("f_yld", 0))
                    if ti == 0:
                        sp.wait("f_xld", S.get("f_xld", 0))
                    sp.wait("f_mm", S.get("f_mm", 0))
                    sp.dma_start(out=fxn.rearrange("p (c f) -> p c f", c=8),
                                 in_=xnT_own.rearrange("c p t -> p c t")[:, :, ti * 512:(ti + 1) * 512]).inc("f_xt", 16)
                    sp.dma_start(out=fyn.rearrange("p (c f) -> p c f", c=8),
                                 in_=ynT_d.rearrange("c p t -> p c t")[:, :, ti * 512:(ti + 1) * 512]).inc("f_yld", 16)
                    t.wait("f_wld", 16 * 10)
                    if ti == 0:
                        v.wait("f_wld", 16 * 10)
                    for rc in range(4):
                        t.wait("f_ve", S.get("f_ve", 0))
                        t.matmul(ps_s0[:, 0:512], c1b[0:16, rc * 128:(rc + 1) * 128],
                                 aT_all[0:16, ti * 512:(ti + 1) * 512], start=True, stop=True)
                        t.mark("f_mm", 1)
                        v.wait("f_mm")
                        v.tensor_copy(g1T[:, rc * 512:(rc + 1) * 512], ps_s0[:, 0:512]).inc("f_ve", 1)
                        t.wait("f_ve")
                        t.matmul(ps_s0[:, 0:512], c2b[0:16, rc * 128:(rc + 1) * 128],
                                 aT_all[0:16, ti * 512:(ti + 1) * 512], start=True, stop=True)
                        t.mark("f_mm", 1)
                        v.wait("f_mm")
                        v.tensor_copy(g2T[:, rc * 512:(rc + 1) * 512], ps_s0[:, 0:512]).inc("f_ve", 1)
                    t.wait("f_xt", S.get("f_xt", 0))
                    for rc in range(4):
                        t.wait("f_ve", S.get("f_ve", 0))
                        for dc in range(8):
                            t.matmul(ps_s1[:, 0:512],
                                     w1b[:, dc * 512 + rc * 128: dc * 512 + (rc + 1) * 128],
                                     fxn[:, dc * 512:(dc + 1) * 512],
                                     start=(dc == 0), stop=(dc == 7))
                        t.mark("f_mm", 1)
                        v.wait("f_mm")
                        v.scalar_tensor_tensor(z1b[:, rc * 512:(rc + 1) * 512],
                                               ps_s1[:, 0:512], b1c[:, rc:rc + 1],
                                               g1T[:, rc * 512:(rc + 1) * 512],
                                               ALU.add, ALU.mult).inc("f_ve", 1)
                    for oc in range(32):
                        pp = oc % 2
                        hps = ps_s0 if pp == 0 else ps_s1
                        t.wait("f_ve", S.get("f_ve", 0))
                        t.wait("f_gel", max(0, S.get("f_gel", 0) - 1))
                        for rc in range(4):
                            t.matmul(hps[:, 0:512],
                                     wo1b[:, rc * 4096 + oc * 128: rc * 4096 + (oc + 1) * 128],
                                     z1b[:, rc * 512:(rc + 1) * 512],
                                     start=(rc == 0), stop=(rc == 3))
                        t.mark("f_mm", 1)
                        sc.wait("f_mm")
                        sc.activation(hb[:, oc * 512:(oc + 1) * 512], hps[:, 0:512],
                                      AF.Gelu).inc("f_gel", 1)
                    for rc in range(4):
                        sc.wait("f_gel", S.get("f_gel", 0))
                        t.wait("f_gel", S.get("f_gel", 0))
                        t.wait("f_ve", S.get("f_ve", 0))
                        for oc in range(32):
                            t.matmul(ps_s1[:, 0:512],
                                     w2b[:, oc * 512 + rc * 128: oc * 512 + (rc + 1) * 128],
                                     hb[:, oc * 512:(oc + 1) * 512],
                                     start=(oc == 0), stop=(oc == 31))
                        t.mark("f_mm", 1)
                        v.wait("f_mm")
                        v.scalar_tensor_tensor(z2b[:, rc * 512:(rc + 1) * 512],
                                               ps_s1[:, 0:512], b2c[:, rc:rc + 1],
                                               g2T[:, rc * 512:(rc + 1) * 512],
                                               ALU.add, ALU.mult).inc("f_ve", 1)
                    t.wait("f_yld", S.get("f_yld", 0))
                    v.wait("f_yld", S.get("f_yld", 0))
                    for tt in range(4):
                        ob = outsb[:, (tt % 2) * 1024:(tt % 2) * 1024 + 1024]
                        sp.wait("f_res", S.get("f_res", 0))
                        sp.wait("f_xold", S.get("f_xold", 0))
                        sp.dma_start(out=fxo[:],
                                     in_=x_own[(ti * 4 + tt) * 128:(ti * 4 + tt + 1) * 128, :]).inc("f_xold", 16)
                        for half in range(2):
                            pcmb = ps_av[:, half * 512:(half + 1) * 512]
                            t.wait("f_ve", S.get("f_ve", 0))
                            t.wait("f_res", max(0, S.get("f_res", 0) - 1))
                            for dc in range(8):
                                t.matmul(pcmb,
                                         fyn[:, dc * 512 + tt * 128: dc * 512 + tt * 128 + 128],
                                         woTb[:, dc * 1024 + half * 512: dc * 1024 + (half + 1) * 512],
                                         start=(dc == 0), stop=False)
                            for rc in range(4):
                                t.matmul(pcmb,
                                         z2b[:, rc * 512 + tt * 128: rc * 512 + tt * 128 + 128],
                                         wo2b[:, rc * 1024 + half * 512: rc * 1024 + (half + 1) * 512],
                                         start=False, stop=(rc == 3))
                            t.mark("f_mm", 1)
                            v.wait("f_mm")
                            v.wait("f_xold")
                            v.wait("f_ost", max(0, S.get("f_ost", 0) - 16))
                            v.tensor_tensor(ob[:, half * 512:(half + 1) * 512], pcmb,
                                            fxo[:, half * 512:(half + 1) * 512],
                                            ALU.add).inc("f_res", 1)
                        sp.wait("f_res")
                        sp.wait("f_ost", S.get("f_ost", 0))
                        sp.dma_start(out=y_out[(ti * 4 + tt) * 128:(ti * 4 + tt + 1) * 128, :],
                                     in_=ob).inc("f_ost", 16)
                sp.wait("f_ost")
                drain_dma()
                v.wait("f_ost")
                t.wait("f_ost")
                sc.wait("f_ost")
                g.wait("f_ost")

            ctxs = {e: Ctx(nc, semh, e, SEM_OWNER) for e in ENGS}
            nsec = 8 if phase < 2 else 11
            for sec_i in range(nsec):
                with nc.Block(f"sec{sec_i}") as blk:
                    for eng in ENGS:
                        def mk(engname, s):
                            def f(_eng):
                                program(ctxs[engname], s)
                            return f
                        getattr(blk, eng)(mk(eng, sec_i))
                nc.all_engine_barrier()

    return nc


def _consts():
    identb = np.eye(128, dtype=BF)
    identf = np.eye(128, dtype=np.float32)
    inv_freq = (1.0 / ROPE_BASE) ** (np.arange(0, DH, 2, dtype=np.float32) / DH)
    tpos = np.arange(T, dtype=np.float32)
    freqs = np.outer(tpos, inv_freq)
    ropeC = np.cos(freqs).astype(np.float32).reshape(NT, 128, 32)
    ropeS = np.sin(freqs).astype(np.float32).reshape(NT, 128, 32)
    qi = np.arange(512)[None, :]
    ki = np.arange(128)[:, None]
    dmaskv = np.stack([(qi >= ki + 128 * j) for j in range(4)]).astype(BF)
    emx = np.zeros((128, 56), dtype=np.float32)
    emx[:, 0:16] = 1.0 / np.arange(1, 17)[None, :]
    emx[:, 16:32] = np.arange(1, 17)[None, :]
    emx[:, 32:48] = np.arange(16)[None, :]
    emx[:, 48] = 1e-5
    emx[:, 49] = 1e-12
    emx[:, 50] = 1e-30
    emx[:, 53] = 0.0
    emx[:, 54] = np.log(0.75)
    auxmb = np.zeros((128, 512), dtype=np.float32)
    cc = np.arange(128)[None, :]
    pp = np.arange(128)[:, None]
    auxmb[:, 384:512] = np.where(cc <= pp, 0.0, -1e9)
    return dict(c_identb=identb, c_identf=identf, c_ropeC=ropeC, c_ropeS=ropeS,
                c_dmask=dmaskv, c_emx=emx, c_auxmb=auxmb)


def _stage_weights(ins):
    f = np.float32
    return dict(
        wabqT=np.ascontiguousarray(np.asarray(ins["L0_attn_W_abq"], f).T),
        wabkvT=np.ascontiguousarray(np.asarray(ins["L0_attn_W_abkv"], f).T),
        w_proj=np.ascontiguousarray(np.asarray(ins["L0_ffn_proj"], f)),
        w_c1=np.ascontiguousarray(np.asarray(ins["L0_ffn_C1"], f)),
        w_c2=np.ascontiguousarray(np.asarray(ins["L0_ffn_C2"], f)),
        w_in1=np.ascontiguousarray(np.asarray(ins["L0_ffn_Win1"], f)[0:D, :]),
        w_b1=np.ascontiguousarray(np.asarray(ins["L0_ffn_Win1"], f)[D:D + 1, :].T),
        w_out1T=np.ascontiguousarray(np.asarray(ins["L0_ffn_Wout1"], f).T),
        w_in2=np.ascontiguousarray(np.asarray(ins["L0_ffn_Win2"], f)[0:OH, :]),
        w_b2=np.ascontiguousarray(np.asarray(ins["L0_ffn_Win2"], f)[OH:OH + 1, :].T),
        w_out2T=np.ascontiguousarray(np.asarray(ins["L0_ffn_Wout2"], f).T),
        w_oT=np.ascontiguousarray(np.asarray(ins["L0_attn_W_o"], f).T),
    )


def make_in_maps(inputs):
    x = np.asarray(inputs["L0_attn_x"], np.float32)
    weights = _stage_weights(inputs)
    consts = _consts()
    in_maps = []
    for c in range(8):
        b, v = c // 4, c % 4
        xb = np.ascontiguousarray(x[b])
        xo = np.ascontiguousarray(
            np.concatenate([xb[1024 * v:1024 * (v + 1)],
                            xb[1024 * (7 - v):1024 * (8 - v)]], axis=0))
        m = {"x_full": xb, "x_own": xo}
        m.update(weights)
        m.update(consts)
        in_maps.append(m)
    return in_maps


def kernel(**inputs):
    nc = build_nc(phase=3, debug=False)
    in_maps = make_in_maps(inputs)
    res = run_bass_kernel_spmd(nc, in_maps, core_ids=list(range(8)))
    out = np.empty((B, T, D), np.float32)
    for c in range(8):
        b, v = c // 4, c % 4
        yo = res.results[c]["y_own_out"]
        out[b, 1024 * v:1024 * (v + 1)] = yo[0:1024]
        out[b, 1024 * (7 - v):1024 * (8 - v)] = yo[1024:2048]
    return out


if __name__ == "__main__":
    import reference as R
    ins = R.setup_inputs(0)
    out = kernel(**{k: np.asarray(v) for k, v in ins.items()})
    ref = np.asarray(R.reference(**ins))
    num = float(np.mean((out - ref) ** 2))
    den = float(np.mean(ref ** 2))
    print("resid var ratio:", num / den)


# version 18
# speedup vs baseline: 1.3985x; 1.3985x over previous
import contextlib
import numpy as np
import ml_dtypes
import concourse.bass as bass
import concourse.mybir as mybir
from concourse.bass_utils import run_bass_kernel_spmd

F32 = mybir.dt.float32
BF16 = mybir.dt.bfloat16
AF = mybir.ActivationFunctionType
ALU = mybir.AluOpType
AX = mybir.AxisListType.X
BF = ml_dtypes.bfloat16

B, T, D = 2, 8192, 1024
NH, DH, CH = 16, 64, 80
RQ = 6
NT = T // 128
OWN = 2048
NOT_ = OWN // 128
NST = 4
E16, RM, OH = 16, 512, 4096
ROPE_BASE = 10000.0
LN_EPS = 1e-5
ENGS = ("sync", "vector", "scalar", "tensor", "gpsimd")


def var_kv_tiles(v):
    return 8 * (8 - v)


def own_tile_global(v, o):
    return 8 * v + o if o < 8 else (56 - 8 * v) + (o - 8)


def stripe_nblk(v, s):
    return [8 * v + 4, 8 * v + 8, 60 - 8 * v, 64 - 8 * v][s]


_CHAINED = {"vector", "scalar", "gpsimd"}
_NOCHAIN_OPS = {"dma_start", "dma_start_transpose", "nop", "sem_inc",
                "partition_id", "collective_compute"}
DMA_SEMS = {"ld", "ngst", "wcast", "p_x", "p_kst", "p_vst0", "p_vst1",
            "p_xnst0", "p_xnst1", "q_ld", "q_st",
            "a_kv0", "a_kv1", "a_qt", "x_qb", "x_st", "dbst", "a_ynst",
            "f_wld", "f_xld", "f_xt",
            "f_yld", "f_xold", "f_ost"}
BAR_SEMS = {"bar_v", "bar_s", "bar_g", "bar_t", "rdone_vector", "rdone_scalar",
            "rdone_tensor", "rdone_gpsimd", "rdone_sync"}


def _is_virtual(sem, owner):
    return (owner.get(sem) in _CHAINED and sem not in DMA_SEMS
            and sem not in BAR_SEMS and not sem.startswith("ch_"))


class _Instr:
    __slots__ = ("ctx", "real", "chained", "eng")

    def __init__(self, ctx, real, chained, eng):
        self.ctx = ctx
        self.real = real
        self.chained = chained
        self.eng = eng

    def inc(self, sem, k=1):
        ctx = self.ctx
        ctx.sems[sem] = ctx.sems.get(sem, 0) + k
        if self.eng in _CHAINED and _is_virtual(sem, ctx.owner):
            ctx.hist.setdefault(sem, {})[ctx.sems[sem]] = ctx.sems.get(
                "ch_" + self.eng, 0)
        else:
            assert not self.chained, f"real sem {sem} inc on chained op"
            if self.real is not None:
                self.real.then_inc(ctx.semh[sem], k)
        return self


class _Prox:
    def __init__(self, ctx, name):
        self._ctx = ctx
        self._name = name

    def __getattr__(self, op):
        ctx = self._ctx
        name = self._name
        chain = name in _CHAINED and op not in _NOCHAIN_OPS
        csem = "ch_" + name

        def call(*a, **kw):
            if op == "nop" and "nofuse" not in kw:
                kw["nofuse"] = True
            real = None
            if chain:
                cur = ctx.sems.get(csem, 0)
                if cur > 0 and ctx.active == name and ctx.dry == 0:
                    getattr(ctx.nc, name).wait_ge(ctx.semh[csem], cur)
            if ctx.active == name and ctx.dry == 0:
                real = getattr(getattr(ctx.nc, name), op)(*a, **kw)
            ins = _Instr(ctx, real, chain, name)
            if chain:
                ctx.sems[csem] = ctx.sems.get(csem, 0) + 1
                if real is not None:
                    real.then_inc(ctx.semh[csem], 1)
            if name == "tensor" and op in ("matmul", "transpose"):
                ctx.last_tensor = ins
            return ins

        return call

    def mark(self, sem, k=1):
        assert self._name == "tensor"
        self._ctx.last_tensor.inc(sem, k)

    def wait(self, sem, val=None):
        ctx = self._ctx
        v = ctx.sems.get(sem, 0) if val is None else val
        if v <= 0:
            return
        own = ctx.owner.get(sem)
        if _is_virtual(sem, ctx.owner):
            if v <= ctx.hist_floor.get(sem, 0):
                return
            cc = ctx.hist.get(sem, {}).get(v)
            if cc is None:
                raise RuntimeError(f"wait {sem}>={v}: no hist entry "
                                   f"(have {sorted(ctx.hist.get(sem, {}))[-4:]})")
            sem, v = "ch_" + own, cc
        key = (self._name, sem)
        if ctx.lastwait.get(key, -1) >= v:
            return
        ctx.lastwait[key] = v
        if ctx.active == self._name and ctx.dry == 0:
            getattr(ctx.nc, self._name).wait_ge(ctx.semh[sem], v)

    def pad_sem(self, sem, target):
        ctx = self._ctx
        cur = ctx.sems.get(sem, 0)
        if target > cur:
            if ctx.active == self._name and ctx.dry == 0:
                eng = getattr(ctx.nc, self._name)
                step = 16 if sem in DMA_SEMS else 1
                n = target - cur
                assert n % step == 0, (sem, cur, target)
                for _ in range(n // step):
                    eng.nop(cycle_cnt=4, nofuse=True).then_inc(ctx.semh[sem], step)
            ctx.sems[sem] = target


class Ctx:
    def __init__(self, nc, semh, active, owner):
        self.nc = nc
        self.semh = semh
        self.active = active
        self.owner = owner
        self.sems = {}
        self.hist = {}
        self.hist_floor = {}
        self.lastwait = {}
        self.dry = 0
        self.last_tensor = _Instr(self, None, False, 'tensor')
        self.sp = _Prox(self, "sync")
        self.v = _Prox(self, "vector")
        self.sc = _Prox(self, "scalar")
        self.t = _Prox(self, "tensor")
        self.g = _Prox(self, "gpsimd")
        self.prox = {"sync": self.sp, "vector": self.v, "scalar": self.sc,
                     "tensor": self.t, "gpsimd": self.g}

    def _snap(self):
        return (dict(self.sems), {k: dict(v) for k, v in self.hist.items()},
                dict(self.hist_floor), dict(self.lastwait))

    def _restore(self, snap):
        self.sems.clear()
        self.sems.update(snap[0])
        self.hist.clear()
        self.hist.update({k: dict(v) for k, v in snap[1].items()})
        self.hist_floor.clear()
        self.hist_floor.update(snap[2])
        self.lastwait.clear()
        self.lastwait.update(snap[3])

    def variant_region(self, engines, body_fn):
        snap = self._snap()
        ends = []
        for v in range(4):
            self._restore(snap)
            self.dry += 1
            body_fn(v)
            self.dry -= 1
            ends.append(self.sems)
        allsems = set()
        for e in ends:
            allsems |= set(e)
        incremented = sorted(k for k in allsems
                             if max(e.get(k, 0) for e in ends) > snap[0].get(k, 0))
        for k in incremented:
            if self.owner.get(k) not in engines:
                raise RuntimeError(
                    f"sem {k} (owner {self.owner.get(k)}) incremented in "
                    f"region with engines {engines}")

        def epilogue():
            for k in incremented:
                own = self.owner[k]
                if own in engines:
                    self.prox[own].wait(k)
            for e in engines:
                if e in _CHAINED:
                    self.prox[e].wait("ch_" + e)
                self.prox[e].drain()

        if self.active in engines and self.dry == 0:
            eng = getattr(self.nc, self.active)
            pid = eng.partition_id()
            idx = pid & 3
            for arm in eng.Switch(idx, 4):
                self._restore(snap)
                body_fn(arm)
                epilogue()
        else:
            self._restore(snap)
            self.dry += 1
            body_fn(0)
            epilogue()
            self.dry -= 1
        self._restore(snap)
        self.pending_region = (engines, incremented)

    def clear_region(self):
        engines, incremented = self.pending_region
        for k in incremented:
            own = self.owner[k]
            if self.active == own and self.dry == 0:
                getattr(self.nc, own).sem_clear(self.semh[k])
            self.sems[k] = 0
            self.hist.pop(k, None)
            self.hist_floor.pop(k, None)
            for kk in list(self.lastwait):
                if kk[1] == k:
                    del self.lastwait[kk]
        for e in engines:
            if e in _CHAINED:
                if self.active == e and self.dry == 0:
                    getattr(self.nc, e).sem_clear(self.semh["ch_" + e])
                self.sems["ch_" + e] = 0
                for kk in list(self.lastwait):
                    if kk[1] == "ch_" + e:
                        del self.lastwait[kk]
                for k in list(self.hist):
                    if self.owner.get(k) == e:
                        self.hist.pop(k)
                        self.hist_floor[k] = self.sems.get(k, 0)


SEM_OWNER = {
    "ld": "sync", "ngst": "sync", "ng": "vector",
    "wcast": "gpsimd", "vset": "gpsimd",
    "ch_vector": "vector", "ch_scalar": "scalar", "ch_gpsimd": "gpsimd",
    "p_x": "scalar", "p_stats": "vector", "p_rstd": "scalar", "p_xnt": "tensor",
    "p_xne": "scalar", "p_mm": "tensor", "p_pev": "scalar", "p_rot": "gpsimd",
    "p_kprod": "vector", "p_ktr": "tensor", "p_kte": "scalar", "p_kst": "scalar",
    "p_vprod": "gpsimd", "p_vst0": "gpsimd", "p_vst1": "gpsimd",
    "p_xnst0": "scalar", "p_xnst1": "scalar",
    "q_ld": "scalar", "q_mm": "tensor", "q_ev": "scalar", "q_prod": "vector",
    "q_tr": "tensor", "q_te": "scalar", "q_st": "scalar",
    "bar_v": "vector", "bar_t": "tensor", "bar_s": "scalar", "bar_g": "gpsimd",
    "rdone_vector": "vector", "rdone_scalar": "scalar", "rdone_tensor": "tensor",
    "rdone_gpsimd": "gpsimd", "rdone_sync": "sync",
    "a_kv0": "scalar", "a_kv1": "scalar",
    "a_qt": "scalar", "x_qb": "scalar", "x_st": "scalar", "dbst": "scalar",
    "x_mm": "tensor", "x_mx": "vector", "x_ls": "scalar",
    "x_tr": "tensor", "x_ev": "vector",
    "a_qk": "tensor", "a_exp": "scalar",
    "a_msk": "gpsimd", "a_av": "tensor", "a_evac": "vector",
    "a_ftr": "tensor", "a_fv": "vector", "a_ftr2": "tensor", "a_fe": "vector",
    "a_ynst": "scalar",
    "f_wld": "gpsimd", "f_xld": "scalar", "f_xt": "sync", "f_yld": "sync",
    "f_xold": "sync",
    "f_mm": "tensor", "f_ve": "vector", "f_act": "scalar", "f_ost": "sync", "f_res": "vector",
    "f_gel": "scalar",
}


def build_nc(phase=3, debug=False):
    nc = bass.Bass("TRN2", target_bir_lowering=False)

    def inp(name, shape, dt=F32):
        return nc.dram_tensor(name, shape, dt, kind="ExternalInput").ap()

    x_full = inp("x_full", [T, D])
    x_own = inp("x_own", [OWN, D])
    wabqT = inp("wabqT", [D, RQ * CH])
    wabkvT = inp("wabkvT", [D, 4 * CH])
    w_proj = inp("w_proj", [D, E16])
    w_c1 = inp("w_c1", [E16, RM])
    w_c2 = inp("w_c2", [E16, RM])
    w_in1 = inp("w_in1", [D, RM])
    w_b1 = inp("w_b1", [RM, 1])
    w_out1T = inp("w_out1T", [RM, OH])
    w_in2 = inp("w_in2", [OH, RM])
    w_b2 = inp("w_b2", [RM, 1])
    w_out2T = inp("w_out2T", [RM, D])
    w_oT = inp("w_oT", [D, D])
    c_identb = inp("c_identb", [128, 128], BF16)
    c_identf = inp("c_identf", [128, 128], F32)
    c_ropeC = inp("c_ropeC", [NT, 128, 32])
    c_ropeS = inp("c_ropeS", [NT, 128, 32])
    c_dmask = inp("c_dmask", [4, 128, 512], BF16)
    c_emx = inp("c_emx", [128, 56])
    c_auxmb = inp("c_auxmb", [128, 512])

    y_out = nc.dram_tensor("y_own_out", [OWN, D], F32, kind="ExternalOutput").ap()

    kv_k = nc.dram_tensor("kv_k", [NH, 65, T], BF16, kind="Internal").ap()
    kv_v = nc.dram_tensor("kv_v", [T, NH * 65], BF16, kind="Internal").ap()
    xnT_all = nc.dram_tensor("xnT_all", [8, 128, T], BF16, kind="Internal").ap()
    xnT_own = nc.dram_tensor("xnT_own", [8, 128, OWN], BF16, kind="Internal").ap()
    qT_d = nc.dram_tensor("qT_d", [8, 128, OWN], BF16, kind="Internal").ap()
    ynT_d = nc.dram_tensor("ynT_d", [8, 128, OWN], BF16, kind="Internal").ap()
    mu_d = nc.dram_tensor("mu_d", [64, 128], BF16, kind="Internal").ap()
    muT_d = nc.dram_tensor("muT_d", [16, OWN], BF16, kind="Internal").ap()

    dbg = {}
    if debug:
        def dout(name, shape, dt):
            dbg[name] = nc.dram_tensor("dbg_" + name, shape, dt, kind="ExternalOutput").ap()
        dout("xnT", [8, 128, T], BF16)
        dout("kvk", [NH, 65, T], BF16)
        dout("kvv", [T, NH * 65], BF16)
        dout("qT", [8, 128, OWN], BF16)
        dout("ynT", [8, 128, OWN], BF16)
        dout("yacc", [NST, 65, 16 * 512], F32)
        dout("mu", [NST, 1, 16 * 512], BF16)

    with contextlib.ExitStack() as stack:
        semh = {s: stack.enter_context(nc.semaphore(s)) for s in SEM_OWNER}

        def sb(name, shape, dt):
            return stack.enter_context(nc.sbuf_tensor(name, shape, dt)).ap()

        identb = sb("identb", [128, 128], BF16)
        identf = sb("identf", [128, 128], F32)
        dmask = sb("dmask", [128, 4 * 512], BF16)
        emx = sb("emx", [128, 56], F32)
        auxmb = sb("auxmb", [128, 512], F32)
        cols = sb("cols", [128, 16], F32)
        negrow = sb("negrow", [128, 1024], BF16)

        ARENA = 196 * 1024
        U8 = mybir.dt.uint8
        arena = sb("arena", [128, ARENA], U8)
        _esize = {F32: 4, BF16: 2}

        def mkregion():
            off = [0]

            def alloc(shape, dt):
                nb = shape[1] * _esize[dt]
                o = (off[0] + 31) // 32 * 32
                assert o + nb <= ARENA, f"arena overflow {o + nb}"
                off[0] = o + nb
                vw = arena[:, o:o + nb].bitcast(dt)
                if shape[0] != 128:
                    vw = vw[0:shape[0], :]
                return vw
            return alloc

        P = mkregion()
        ropeC = P([128, NT * 32], F32)
        ropeS = P([128, NT * 32], F32)
        wabq_b = P([128, 8 * 480], BF16)
        wabkv_b = P([128, 8 * 320], BF16)
        xt = P([128, 2 * 1024], F32)
        scr = P([128, 1024], F32)
        rscr = P([128, 2 * 256], F32)
        xnb = P([128, 1024], BF16)
        xnT_sb = P([128, 2 * 1024], BF16)
        akv = P([128, 2 * 64], F32)
        bkv = P([128, 2 * 256], F32)
        bkvr = P([128, 2 * 256], F32)
        Pt = P([128, 2 * 2048 + 1024], F32)
        ptb = P([128, 2 * 1024], BF16)
        Pvt = P([128, 1024], BF16)
        vstage = P([128, 2 * 1040], BF16)
        kT_sb = P([128, 2 * 1024], BF16)
        xq_t = P([128, 2 * 1024], BF16)
        aq = P([128, 2 * 96], F32)
        bq = P([128, 2 * 384], F32)
        qT_sb = P([128, 2 * 1024], BF16)
        kq = P([128, 1024], F32)
        ksq = P([128, 16], F32)
        kmax = P([128, 16], F32)
        kmaxT = P([16, 128], F32)
        km1 = P([16, 1], F32)
        km1r = P([1, 16], F32)
        kbr = P([128, 16], F32)
        ones1 = P([1, 128], F32)
        qsq = P([128, 16], F32)
        mu2 = P([128, 16], F32)
        muv = P([128, 16], F32)
        muT_sb = P([16, 128], BF16)

        A = mkregion()
        kr_k = A([65, 2 * 16 * 1024], BF16)
        kr_v = A([128, 2 * 8 * 1040], BF16)
        qbuf = A([64, 16 * 512], BF16)
        qT_ext = A([65, 2 * 16 * 512], BF16)
        rmx = A([128, 64], F32)
        mstage = A([64, 128], BF16)
        mtmp = A([128, 1], F32)
        junk = A([128, 512], F32)
        junk2 = A([128, 512], BF16)
        lsb = A([128, 512], F32)
        red32 = A([128, 32], F32)
        pT0 = A([128, 1024], BF16)
        pT1 = A([128, 1024], BF16)
        y_acc = A([65, 16 * 512], F32)
        ytokn = A([128, 4 * 64], BF16)
        ynT_sb = A([64, 512], BF16)
        recipS = A([128, 4], F32)

        F = mkregion()
        w1b = F([128, 8 * 512], BF16)
        wo1b = F([128, 4 * 4096], BF16)
        w2b = F([128, 32 * 512], BF16)
        wo2b = F([128, 4 * 1024], BF16)
        woTb = F([128, 8 * 1024], BF16)
        projb = F([128, 8 * 16], BF16)
        c1b = F([16, 512], BF16)
        c2b = F([16, 512], BF16)
        b1c = F([128, 4], F32)
        b2c = F([128, 4], F32)
        fxn = F([128, 8 * 512], BF16)
        fyn = F([128, 8 * 512], BF16)
        fxo = F([128, 1024], F32)
        zproj = F([16, 512], F32)
        ztok = F([128, 64], F32)
        ecols = F([128, 24], F32)
        ezs = F([128, 64], F32)
        ez3 = F([128, 64], F32)
        etmp = F([128, 64], F32)
        etmp2 = F([128, 64], F32)
        ecs = F([128, 64], F32)
        ecs2 = F([128, 64], F32)
        ab16 = F([128, 64], BF16)
        aT_all = F([16, 4 * 512], BF16)
        g1T = F([128, 4 * 512], F32)
        g2T = F([128, 4 * 512], F32)
        z1b = F([128, 4 * 512], BF16)
        hb = F([128, 32 * 512], BF16)
        z2b = F([128, 4 * 512], BF16)
        outsb = F([128, 2 * 1024], F32)

        ps_s0 = stack.enter_context(nc.psum_tensor("ps_s0", [128, 1024], F32)).ap()
        ps_s1 = stack.enter_context(nc.psum_tensor("ps_s1", [128, 1024], F32)).ap()
        ps_av = stack.enter_context(nc.psum_tensor("ps_av", [128, 1024], F32)).ap()
        ps_u = stack.enter_context(nc.psum_tensor("ps_u", [128, 512], F32)).ap()
        ps_x = stack.enter_context(nc.psum_tensor("ps_x", [128, 512], F32)).ap()

        def bfv(ps, bank):
            return ps[:, bank * 512:(bank + 1) * 512].bitcast(BF16)

        if True:
            def program(ctx: Ctx, sec: int):
                sp, v, sc, t, g = ctx.sp, ctx.v, ctx.sc, ctx.t, ctx.g
                S = ctx.sems

                def drain_dma():
                    for s_ in sorted(DMA_SEMS):
                        sp.wait(s_)

                if sec == 0:
                    run_init(ctx)
                elif sec == 1:
                    run_prep(ctx)
                elif sec == 2:
                    ctx.clear_region()
                elif sec == 3:
                    run_kfin(ctx)
                elif sec == 4:
                    run_qprep(ctx)
                elif sec == 5:
                    ctx.clear_region()
                elif sec == 6:
                    run_bias(ctx)
                elif sec == 7:
                    if phase < 2:
                        run_placeholder_out(ctx)
                    else:
                        run_attention(ctx)
                elif sec == 8:
                    ctx.clear_region()
                elif sec == 9:
                    run_postattn(ctx)
                elif sec == 10:
                    if phase < 3:
                        run_placeholder_out(ctx)
                    else:
                        run_ffn(ctx)
                else:
                    raise RuntimeError(sec)

            def run_init(ctx: Ctx):
                sp, v, sc, t, g = ctx.sp, ctx.v, ctx.sc, ctx.t, ctx.g
                S = ctx.sems
                sp.dma_start(out=identb[:], in_=c_identb[:]).inc("ld", 16)
                sp.dma_start(out=identf[:], in_=c_identf[:]).inc("ld", 16)
                sp.dma_start(out=ropeC.rearrange("p (g c) -> p g c", c=32),
                             in_=c_ropeC.rearrange("g p c -> p g c")).inc("ld", 16)
                sp.dma_start(out=ropeS.rearrange("p (g c) -> p g c", c=32),
                             in_=c_ropeS.rearrange("g p c -> p g c")).inc("ld", 16)
                sp.dma_start(out=dmask.rearrange("p (j c) -> p j c", c=512),
                             in_=c_dmask.rearrange("j p c -> p j c")).inc("ld", 16)
                sp.dma_start(out=emx[:], in_=c_emx[:]).inc("ld", 16)
                sp.dma_start(out=auxmb[:], in_=c_auxmb[:]).inc("ld", 16)
                g.dma_start(out=wabq_b.rearrange("p (c f) -> p c f", c=8),
                            in_=wabqT.rearrange("(c p) f -> p c f", p=128)).inc("wcast", 16)
                g.dma_start(out=wabkv_b.rearrange("p (c f) -> p c f", c=8),
                            in_=wabkvT.rearrange("(c p) f -> p c f", p=128)).inc("wcast", 16)
                g.memset(vstage.rearrange("p (b h c) -> p b h c", b=2, h=16)[:, :, :, 64:65],
                         1.0).inc("vset", 1)
                v.memset(ones1[0:1, :], 1.0)
                v.memset(negrow[:], -1.0).inc("ng", 1)
                sp.wait("ng")
                sp.dma_start(
                    out=kv_k[:, 64:65, :].rearrange("h o (p t) -> p h o t", p=128),
                    in_=negrow.rearrange("p (h o t) -> p h o t", h=16, o=1)
                ).inc("ngst", 16)

            def run_prep(ctx: Ctx):
                sp, v, sc, t, g = ctx.sp, ctx.v, ctx.sc, ctx.t, ctx.g
                S = ctx.sems
                def kv_prep(vv):
                    ntl = var_kv_tiles(vv)
                    v.memset(kmax[:, 0:16], 0.0)

                    def stage1(i):
                        pb = i % 2
                        xti = xt[:, pb * 1024:(pb + 1) * 1024]
                        sp.wait("p_stats", max(0, 2 * (i - 1)))
                        sp.dma_start(out=xti, in_=x_full[i * 128:(i + 1) * 128, :]).inc("p_x", 16)
                        v.wait("p_x")
                        if i == 0:
                            v.wait("wcast", 32)
                            v.wait("vset", 1)
                            v.wait("ld", 112)
                        v.reduce_sum(cols[:, 0:1], xti, axis=AX)
                        v.scalar_tensor_tensor(scr[:], xti, 1.0, xti, ALU.mult, ALU.mult,
                                               accum_out=cols[:, 1:2])
                        v.tensor_scalar(cols[:, 2:3], cols[:, 0:1], 1.0 / D, None, ALU.mult)
                        v.tensor_scalar(cols[:, 3:4], cols[:, 1:2], 1.0 / D, None, ALU.mult)
                        v.tensor_mul(cols[:, 4:5], cols[:, 2:3], cols[:, 2:3])
                        v.tensor_sub(cols[:, 5:6], cols[:, 3:4], cols[:, 4:5]).inc("p_stats", 1)
                        sc.wait("p_stats", 2 * i + 1)
                        sc.activation(cols[:, 6:7], cols[:, 5:6], AF.Ln, bias=emx[:, 48:49])
                        sc.activation(cols[:, 7:8], cols[:, 6:7], AF.Exp, scale=-0.5).inc("p_rstd", 1)
                        v.wait("p_rstd")
                        v.tensor_scalar(xnb[:], xti, cols[:, 2:3], cols[:, 7:8],
                                        ALU.subtract, ALU.mult).inc("p_stats", 1)
                        xnt_ps = bfv(ps_u if i % 2 == 0 else ps_x, 0)
                        t.wait("p_stats", 2 * i + 2)
                        t.wait("p_xne", max(0, i - 1))
                        for fc in range(8):
                            t.matmul(xnt_ps[:, fc * 128:(fc + 1) * 128],
                                     xnb[:, fc * 128:(fc + 1) * 128], identb[:],
                                     is_transpose=True, start=True, stop=True)
                        t.mark("p_xnt", 1)
                        xnti = xnT_sb[:, pb * 1024:(pb + 1) * 1024]
                        sc.wait("p_xnt")
                        sc.wait("p_mm", max(0, i - 1))
                        sc.wait("p_xnst0" if pb == 0 else "p_xnst1")
                        sc.activation(xnti, xnt_ps, AF.Copy, bias=0.0).inc("p_xne", 1)
                        sp.wait("p_xne")
                        sp.dma_start(out=xnT_all.rearrange("c p t -> p c t")[:, :, i * 128:(i + 1) * 128],
                                     in_=xnti.rearrange("p (c t) -> p c t", c=8)).inc("p_xnst0" if pb == 0 else "p_xnst1", 16)
                        t.wait("p_xne")
                        t.wait("p_pev", max(0, i - 1))
                        s0c = ps_s0[:, pb * 512:pb * 512 + 320]
                        for dc in range(8):
                            t.matmul(s0c, xnti[:, dc * 128:(dc + 1) * 128],
                                     wabkv_b[:, dc * 320:(dc + 1) * 320],
                                     start=(dc == 0), stop=(dc == 7))
                        t.mark("p_mm", 1)
                        akv_i = akv[:, pb * 64:(pb + 1) * 64]
                        bkv_i = bkv[:, pb * 256:(pb + 1) * 256]
                        bkvr_i = bkvr[:, pb * 256:(pb + 1) * 256]
                        sc.wait("p_mm")
                        sc.wait("p_vprod", max(0, i - 1))
                        sc.wait("p_rot", max(0, i - 1))
                        aps = s0c.rearrange("p (r c) -> p r c", r=4)
                        sc.activation(akv_i.rearrange("p (r h) -> p r h", r=4),
                                      aps[:, :, 0:16], AF.Copy, scale=0.5, bias=0.0)
                        sc.activation(bkv_i.rearrange("p (r c) -> p r c", r=4),
                                      aps[:, :, 16:80], AF.Copy, bias=0.0).inc("p_pev", 1)

                    def stage2(i):
                        pb = i % 2
                        akv_i = akv[:, pb * 64:(pb + 1) * 64]
                        bkv_i = bkv[:, pb * 256:(pb + 1) * 256]
                        bkvr_i = bkvr[:, pb * 256:(pb + 1) * 256]
                        g.wait("p_pev", i + 1)
                        g.wait("p_kprod", max(0, i - 1))
                        cosr = ropeC[:, i * 32:(i + 1) * 32].rearrange(
                            "p (o c) -> p o c", o=1).broadcast_to([128, 4, 32])
                        sinr = ropeS[:, i * 32:(i + 1) * 32].rearrange(
                            "p (o c) -> p o c", o=1).broadcast_to([128, 4, 32])
                        bk4 = bkv_i.rearrange("p (r c) -> p r c", r=4)
                        br4 = bkvr_i.rearrange("p (r c) -> p r c", r=4)
                        rs4 = rscr[:, pb * 256:(pb + 1) * 256].rearrange(
                            "p (r c) -> p r c", r=8)
                        g.tensor_tensor(br4[:, :, 0:32], bk4[:, :, 0:32], cosr, ALU.mult)
                        g.tensor_tensor(rs4[:, 0:4, :], bk4[:, :, 32:64], sinr, ALU.mult)
                        g.tensor_tensor(br4[:, :, 0:32], br4[:, :, 0:32], rs4[:, 0:4, :], ALU.add)
                        g.tensor_tensor(br4[:, :, 32:64], bk4[:, :, 32:64], cosr, ALU.mult)
                        g.tensor_tensor(rs4[:, 4:8, :], bk4[:, :, 0:32], sinr, ALU.mult)
                        g.tensor_tensor(br4[:, :, 32:64], br4[:, :, 32:64], rs4[:, 4:8, :],
                                        ALU.subtract).inc("p_rot", 1)
                        ptv = Pt[:, pb * 2048:(pb + 1) * 2048]
                        v.wait("p_rot", i + 1)
                        v.wait("p_ktr", max(0, i - 1))
                        for s_ in range(2):
                            a_b = akv_i[:, s_ * 16:(s_ + 1) * 16].rearrange(
                                "p (o h) -> p h o", o=1).broadcast_to([128, 16, 64])
                            b_b = bkvr_i[:, s_ * 64:(s_ + 1) * 64].rearrange(
                                "p (o c) -> p o c", o=1).broadcast_to([128, 16, 64])
                            v.tensor_tensor(
                                ptv[:, s_ * 1024:(s_ + 1) * 1024].rearrange("p (h c) -> p h c", h=16),
                                a_b, b_b, ALU.mult)
                        v.tensor_tensor(ptb[:, pb * 1024:(pb + 1) * 1024],
                                        ptv[:, 0:1024], ptv[:, 1024:2048],
                                        ALU.add)
                        v.nop().inc("p_kprod", 1)
                        ptbk = ptb[:, pb * 1024:(pb + 1) * 1024]
                        v.scalar_tensor_tensor(kq[:], ptbk, 1.0, ptbk,
                                               ALU.mult, ALU.mult)
                        v.tensor_reduce(ksq[:, 0:16],
                                        kq.rearrange("p (h c) -> p h c", h=16),
                                        axis=AX, op=ALU.add)
                        v.tensor_tensor(kmax[:, 0:16], kmax[:, 0:16],
                                        ksq[:, 0:16], ALU.max)
                        kt_ps = bfv(ps_s1 if pb == 0 else ps_av, 0)
                        t.wait("p_kprod", i + 1)
                        t.wait("p_kte", max(0, i - 1))
                        for fc in range(8):
                            t.matmul(kt_ps[:, fc * 128:(fc + 1) * 128],
                                     ptb[:, pb * 1024 + fc * 128:
                                         pb * 1024 + (fc + 1) * 128],
                                     identb[:], is_transpose=True,
                                     start=True, stop=True)
                        t.mark("p_ktr", 1)
                        kT_i = kT_sb[:, pb * 1024:(pb + 1) * 1024]
                        sc.wait("p_ktr", i + 1)
                        sc.wait("p_kst", 32 * i)
                        sc.activation(kT_i, kt_ps[:, 0:1024], AF.Copy,
                                      bias=0.0).inc("p_kte", 1)
                        sp.wait("p_kte", i + 1)
                        kv2 = kv_k.rearrange("(c two) r t -> two r c t", two=2)
                        for two in range(2):
                            sp.dma_start(
                                out=kv2[two][0:64, :, i * 128:(i + 1) * 128],
                                in_=kT_i[64 * two:64 * two + 64, :].rearrange(
                                    "p (c t) -> p c t", c=8)).inc("p_kst", 16)
                        vst = vstage[:, pb * 1040:(pb + 1) * 1040]
                        vsh = vst.rearrange("p (h c) -> p h c", h=16)[:, :, 0:64]
                        g.wait("p_vst0" if pb == 0 else "p_vst1")
                        for s_ in (2, 3):
                            a_b = akv_i[:, s_ * 16:(s_ + 1) * 16].rearrange(
                                "p (o h) -> p h o", o=1).broadcast_to([128, 16, 64])
                            b_b = bkvr_i[:, s_ * 64:(s_ + 1) * 64].rearrange(
                                "p (o c) -> p o c", o=1).broadcast_to([128, 16, 64])
                            if s_ == 2:
                                g.tensor_tensor(vsh, a_b, b_b, ALU.mult)
                            else:
                                g.tensor_tensor(Pvt.rearrange("p (h c) -> p h c", h=16),
                                                a_b, b_b, ALU.mult)
                                g.tensor_tensor(vsh, vsh,
                                                Pvt.rearrange("p (h c) -> p h c", h=16),
                                                ALU.add)
                        g.nop().inc("p_vprod", 1)
                        sp.wait("p_vprod", i + 1)
                        sp.dma_start(out=kv_v[i * 128:(i + 1) * 128, :], in_=vst).inc("p_vst0" if pb == 0 else "p_vst1", 16)

                    for i in range(ntl + 1):
                        if i < ntl:
                            stage1(i)
                        if i >= 1:
                            stage2(i - 1)

                ctx.variant_region(("sync", "vector", "scalar", "tensor", "gpsimd"),
                                   kv_prep)


            def run_kfin(ctx: Ctx):
                pass

            def run_qprep(ctx: Ctx):
                sp, v, sc, t, g = ctx.sp, ctx.v, ctx.sc, ctx.t, ctx.g
                S = ctx.sems
                def q_prep(vv):
                    qtr_n = [S.get("q_tr", 0)]
                    t.matmul(ps_u[0:16, 0:128], kmax[:, 0:16], identf[:],
                             is_transpose=True, start=True, stop=True)
                    t.mark("x_tr", 1)
                    v.wait("x_tr", S.get("x_tr", 0))
                    v.tensor_copy(kmaxT[0:16, :], ps_u[0:16, 0:128])
                    v.tensor_reduce(km1[0:16, 0:1], kmaxT[0:16, :], axis=AX,
                                    op=ALU.max)
                    v.nop().inc("x_mx", 1)
                    t.wait("x_mx", S.get("x_mx", 0))
                    t.matmul(ps_u[0:1, 128:144], km1[0:16, 0:1],
                             identf[0:16, 0:16], is_transpose=True,
                             start=True, stop=True)
                    t.mark("x_tr", 1)
                    v.wait("x_tr", S.get("x_tr", 0))
                    v.tensor_copy(km1r[0:1, :], ps_u[0:1, 128:144])
                    v.nop().inc("x_mx", 1)
                    t.wait("x_mx", S.get("x_mx", 0))
                    t.matmul(ps_u[0:128, 256:272], ones1[0:1, :], km1r[0:1, :],
                             start=True, stop=True)
                    t.mark("x_tr", 1)
                    v.wait("x_tr", S.get("x_tr", 0))
                    v.tensor_copy(kbr[:, 0:16], ps_u[:, 256:272])
                    v.nop().inc("x_mx", 1)

                    def stage1(o):
                        gt = own_tile_global(vv, o)
                        pb = o % 2
                        xq_i = xq_t[:, pb * 1024:(pb + 1) * 1024]
                        sc.wait("q_mm", max(0, o - 1))
                        sc.dma_start(out=xq_i.rearrange("p (c t) -> p c t", c=8),
                                     in_=xnT_all.rearrange("c p t -> p c t")[:, :, gt * 128:(gt + 1) * 128]).inc("q_ld", 16)
                        sc.dma_start(out=xnT_own[:, :, o * 128:(o + 1) * 128],
                                     in_=xnT_all[:, :, gt * 128:(gt + 1) * 128]).inc("q_ld", 16)
                        t.wait("q_ld")
                        t.wait("q_ev", max(0, o - 1))
                        s0c = ps_s0[:, pb * 512:pb * 512 + 480]
                        for dc in range(8):
                            t.matmul(s0c, xq_i[:, dc * 128:(dc + 1) * 128],
                                     wabq_b[:, dc * 480:(dc + 1) * 480],
                                     start=(dc == 0), stop=(dc == 7))
                        t.mark("q_mm", 1)
                        sc.wait("q_mm")
                        sc.wait("q_prod", max(0, o - 1))
                        apq = s0c.rearrange("p (r c) -> p r c", r=6)
                        sc.activation(aq[:, pb * 96:(pb + 1) * 96].rearrange(
                                          "p (r h) -> p r h", r=6),
                                      apq[:, :, 0:16], AF.Copy, scale=1.0 / RQ,
                                      bias=0.0)
                        sc.activation(bq[:, pb * 384:(pb + 1) * 384].rearrange(
                                          "p (r c) -> p r c", r=6),
                                      apq[:, :, 16:80], AF.Copy, bias=0.0).inc("q_ev", 1)

                    qacc = Pt[:, 4096:5120]
                    qacc4 = qacc.rearrange("p (h c) -> p h c", h=16)

                    def stage2(o):
                        gt = own_tile_global(vv, o)
                        pb = o % 2
                        qt_ps = bfv(ps_s1 if pb == 0 else ps_av, 0)
                        ptbq = ptb[:, pb * 1024:(pb + 1) * 1024]
                        v.wait("q_ev", o + 1)
                        for r in range(6):
                            pr = r % 2
                            a_b = aq[:, pb * 96 + r * 16:pb * 96 + (r + 1) * 16].rearrange(
                                "p (o h) -> p h o", o=1).broadcast_to([128, 16, 64])
                            b_b = bq[:, pb * 384 + r * 64:pb * 384 + (r + 1) * 64].rearrange(
                                "p (o c) -> p o c", o=1).broadcast_to([128, 16, 64])
                            if r == 0:
                                v.tensor_tensor(qacc4, a_b, b_b, ALU.mult)
                            else:
                                v.tensor_tensor(
                                    Pt[:, pr * 1024:(pr + 1) * 1024].rearrange(
                                        "p (h c) -> p h c", h=16),
                                    a_b, b_b, ALU.mult)
                                if r < 5:
                                    v.scalar_tensor_tensor(
                                        qacc, Pt[:, pr * 1024:(pr + 1) * 1024],
                                        1.0, qacc, ALU.mult, ALU.add)
                                else:
                                    v.wait("q_tr", max(0, o - 1))
                                    v.scalar_tensor_tensor(
                                        ptbq, Pt[:, pr * 1024:(pr + 1) * 1024],
                                        1.0, qacc, ALU.mult, ALU.add)
                        v.nop().inc("q_prod", 1)
                        t.wait("q_prod", o + 1)
                        t.wait("q_te", max(0, o - 1))
                        for fc in range(8):
                            t.matmul(qt_ps[:, fc * 128:(fc + 1) * 128],
                                     ptbq[:, fc * 128:(fc + 1) * 128],
                                     identb[:], is_transpose=True,
                                     start=True, stop=True)
                        t.mark("q_tr", 1)
                        qT_i = qT_sb[:, pb * 1024:(pb + 1) * 1024]
                        sc.wait("q_tr", o + 1)
                        sc.wait("q_st", 16 * o)
                        sc.activation(qT_i, qt_ps[:, 0:1024], AF.Copy,
                                      bias=0.0).inc("q_te", 1)
                        sc.wait("q_te", o + 1)
                        sc.dma_start(out=qT_d.rearrange("c p t -> p c t")[:, :, o * 128:(o + 1) * 128],
                                     in_=qT_i.rearrange("p (c t) -> p c t", c=8)).inc("q_st", 16)
                        v.wait("x_ls", S.get("x_ls", 0))
                        v.scalar_tensor_tensor(kq[:], ptbq, 1.0, ptbq,
                                               ALU.mult, ALU.mult)
                        v.tensor_reduce(qsq[:, 0:16],
                                        kq.rearrange("p (h c) -> p h c", h=16),
                                        axis=AX, op=ALU.add)
                        v.tensor_tensor(mu2[:, 0:16], qsq[:, 0:16],
                                        kbr[:, 0:16], ALU.mult)
                        v.nop().inc("x_mx", 1)
                        sc.wait("x_mx", S.get("x_mx", 0))
                        sc.activation(qsq[:, 0:16], mu2[:, 0:16], AF.Ln,
                                      bias=emx[:, 49:50])
                        sc.activation(muv[:, 0:16], qsq[:, 0:16], AF.Exp,
                                      scale=0.5, bias=emx[:, 54:55]).inc("x_ls", 1)
                        t.wait("x_ls", S.get("x_ls", 0))
                        t.wait("x_mx", S.get("x_mx", 0))
                        t.matmul(ps_u[0:16, 0:128], muv[:, 0:16], identf[:],
                                 is_transpose=True, start=True, stop=True)
                        t.mark("x_tr", 1)
                        sc.wait("x_tr", S.get("x_tr", 0))
                        sc.wait("x_st", S.get("x_st", 0))
                        sc.activation(muT_sb[0:16, :], ps_u[0:16, 0:128],
                                      AF.Copy, bias=0.0).inc("x_ls", 1)
                        sc.dma_start(out=muT_d[:, o * 128:(o + 1) * 128],
                                     in_=muT_sb[0:16, :]).inc("x_st", 16)

                    for o in range(NOT_ + 1):
                        if o < NOT_:
                            stage1(o)
                        if o >= 1:
                            stage2(o - 1)

                ctx.variant_region(("vector", "scalar", "tensor"), q_prep)

            def run_bias(ctx: Ctx):
                sp, v, sc, t, g = ctx.sp, ctx.v, ctx.sc, ctx.t, ctx.g
                if debug:
                    sp.wait("ngst", ctx.sems.get("ngst", 0))
                    sp.wait("ld", ctx.sems.get("ld", 0))
                    sp.dma_start(out=dbg["xnT"][:], in_=xnT_all[:]).inc("ld", 16)
                    sp.dma_start(out=dbg["kvk"][:], in_=kv_k[:]).inc("ld", 16)
                    sp.dma_start(out=dbg["kvv"][:], in_=kv_v[:]).inc("ld", 16)
                    sp.dma_start(out=dbg["qT"][:], in_=qT_d[:]).inc("ld", 16)

            def run_placeholder_out(ctx: Ctx):
                sp, v, sc, t, g = ctx.sp, ctx.v, ctx.sc, ctx.t, ctx.g
                S = ctx.sems

                def drain_dma():
                    for s_ in sorted(DMA_SEMS):
                        sp.wait(s_)
                v.memset(outsb[:, 0:1024], 0.0).inc("f_res", 1)
                sp.wait("f_res")
                for ti in range(NOT_):
                    sp.dma_start(out=y_out[ti * 128:(ti + 1) * 128, :],
                                 in_=outsb[:, 0:1024]).inc("f_ost", 16)
                sp.wait("f_ost")
                drain_dma()
                v.wait("f_ost")
                t.wait("f_ost")
                sc.wait("f_ost")
                g.wait("f_ost")

            def run_attention(ctx: Ctx):
                sp, v, sc, t, g = ctx.sp, ctx.v, ctx.sc, ctx.t, ctx.g
                S = ctx.sems
                def attention(vv):
                    nblk = [stripe_nblk(vv, s_) for s_ in range(NST)]
                    CC = [(n + 7) // 8 for n in nblk]
                    gts = {(s_, qt): own_tile_global(vv, 4 * s_ + qt)
                           for s_ in range(NST) for qt in range(4)}
                    avcnt = [0]
                    ucnt = [0]
                    pend_av = []
                    last_expS = {0: 0, 1: 0}
                    last_avpT = {0: 0, 1: 0}
                    kbusy = {}
                    vbusy = {}
                    akv_end = {}
                    qk_end = {}

                    def kvload(ci, nbig, with_v, nv=0):
                        blocks = min(8, nbig - 8 * ci)
                        slot = ci % 2
                        sem = "a_kv%d" % slot
                        sc.wait(sem, S.get(sem, 0))
                        if slot in kbusy:
                            sc.wait("a_qk", kbusy[slot])
                        g0 = ci * 1024
                        sc.dma_start(
                            out=kr_k[:, slot * 16384:(slot + 1) * 16384].rearrange(
                                "r (h t) -> r h t", h=16)[:, :, 0:blocks * 128],
                            in_=kv_k.rearrange("h r t -> r h t")[:, :, g0:g0 + blocks * 128]
                        ).inc(sem, 16)
                        if with_v:
                            bv = min(8, nv - 8 * ci)
                            if slot in vbusy:
                                sc.wait("a_av", vbusy[slot])
                            sc.dma_start(
                                out=kr_v[:, slot * 8320:slot * 8320 + bv * 1040].rearrange(
                                    "p (b c) -> p b c", b=bv),
                                in_=kv_v[g0:g0 + bv * 128, :].rearrange(
                                    "(b p) c -> p b c", p=128)).inc(sem, 16)
                        akv_end[ci] = (sem, S.get(sem, 0))

                    def load_qext_rows(sa):
                        sc.wait("a_qt", S.get("a_qt", 0))
                        if sa > 1:
                            sc.wait("a_qk", qk_end[sa - 2])
                        par = sa % 2
                        for h in range(16):
                            sc.dma_start(
                                out=qT_ext[0:64, par * 8192 + h * 512:
                                           par * 8192 + (h + 1) * 512],
                                in_=qT_d[h // 2][64 * (h % 2):64 * (h % 2) + 64,
                                                 sa * 512:(sa + 1) * 512]).inc("a_qt", 16)
                        sc.wait("x_st", S.get("x_st", 0))
                        sc.dma_start(
                            out=qT_ext[64:65, par * 8192:(par + 1) * 8192].rearrange(
                                "o (h t) -> o h t", h=16),
                            in_=muT_d[:, sa * 512:(sa + 1) * 512].rearrange(
                                "(o h) t -> o h t", o=1)).inc("x_st", 16)

                    def emit_av(pa):
                        (s_, ci, h, bp, blocks, slot, up, expn, mskn) = pa
                        first = (bp == 0)
                        avs = avcnt[0] % 2
                        pT = pT0 if up == 0 else pT1
                        t.wait("a_exp", expn)
                        if mskn:
                            t.wait("a_msk", mskn)
                        if first:
                            t.wait("a_evac", max(0, avcnt[0] - 1))
                            if ci == 0 and h == 0:
                                t.wait("a_fv", S.get("a_fv", 0))
                        b0 = 2 * bp
                        for j in (0, 1):
                            vco = slot * 8320 + (b0 + j) * 1040 + h * 65
                            t.matmul(ps_av[0:65, avs * 512:(avs + 1) * 512],
                                     kr_v[:, vco:vco + 65],
                                     pT[:, j * 512:(j + 1) * 512],
                                     start=(first and j == 0),
                                     stop=(bp == blocks // 2 - 1 and j == 1))
                        t.mark("a_av", 1)
                        last_avpT[up] = S["a_av"]
                        if bp == blocks // 2 - 1:
                            v.wait("a_av")
                            ya = y_acc[:, h * 512:(h + 1) * 512]
                            if ci == 0:
                                if debug and h == 0:
                                    v.wait("dbst", S.get("dbst", 0))
                                v.tensor_copy(ya, ps_av[0:65, avs * 512:(avs + 1) * 512])
                            else:
                                v.scalar_tensor_tensor(
                                    ya, ps_av[0:65, avs * 512:(avs + 1) * 512],
                                    1.0, ya, ALU.mult, ALU.add)
                            v.nop().inc("a_evac", 1)
                            avcnt[0] += 1

                    def flush_av(all_=True):
                        while pend_av and (all_ or len(pend_av) > 1):
                            emit_av(pend_av.pop(0))

                    def main_chunk(s_, ci):
                        blocks = min(8, nblk[s_] - 8 * ci)
                        slot = ci % 2
                        N = nblk[s_]
                        for h in range(16):
                            for bp in range(blocks // 2):
                                up = ucnt[0] % 2
                                ucnt[0] += 1
                                Sps = ps_s0 if up == 0 else ps_s1
                                pT = pT0 if up == 0 else pT1
                                b0 = 2 * bp
                                t.wait(*akv_end[ci])
                                if ci == 0 and h == 0 and bp == 0:
                                    t.wait("a_qt", S.get("a_qt", 0))
                                    t.wait("x_st", S.get("x_st", 0))
                                if ci == 0 and h == 0 and bp <= 1:
                                    t.wait("a_fe", S.get("a_fe", 0))
                                t.wait("a_exp", last_expS[up])
                                qpar = (s_ % 2) * 8192
                                for j in (0, 1):
                                    co = slot * 16384 + h * 1024 + (b0 + j) * 128
                                    t.matmul(Sps[:, j * 512:(j + 1) * 512],
                                             kr_k[0:65, co:co + 128],
                                             qT_ext[0:65, qpar + h * 512:
                                                    qpar + (h + 1) * 512],
                                             start=True, stop=True)
                                t.mark("a_qk", 1)
                                flush_av(all_=False)
                                sc.wait("a_qk")
                                sc.wait("a_av", last_avpT[up])
                                sc.activation(pT[:], Sps[:], AF.Exp,
                                              bias=emx[:, 53:54],
                                              scale=0.125).inc("a_exp", 1)
                                last_expS[up] = S["a_exp"]
                                ndiag = 0
                                for j in (0, 1):
                                    blk = ci * 8 + b0 + j
                                    dj = blk - (N - 4)
                                    if dj >= 0:
                                        g.wait("a_exp")
                                        g.tensor_tensor(pT[:, j * 512:(j + 1) * 512],
                                                        pT[:, j * 512:(j + 1) * 512],
                                                        dmask[:, dj * 512:(dj + 1) * 512],
                                                        ALU.mult)
                                        ndiag += 1
                                if ndiag:
                                    g.nop().inc("a_msk", 1)
                                pend_av.append(
                                    (s_, ci, h, bp, blocks, slot, up,
                                     S["a_exp"], S.get("a_msk", 0) if ndiag else 0))
                                yield

                    def y_finalize(s_):
                        if debug:
                            sc.wait("dbst", S.get("dbst", 0))
                            sc.wait("a_evac", S.get("a_evac", 0))
                            sc.dma_start(out=dbg["yacc"][s_], in_=y_acc[:]).inc("dbst", 16)
                        for h in range(16):
                            ya = y_acc[:, h * 512:(h + 1) * 512]
                            t.wait("a_evac")
                            t.wait("a_fv", S.get("a_fv", 0))
                            for q4 in range(4):
                                t.matmul(ps_av[:, q4 * 128:q4 * 128 + 65],
                                         ya[:, q4 * 128:(q4 + 1) * 128], identf[0:65, 0:65],
                                         is_transpose=True, start=True, stop=True)
                            t.mark("a_ftr", 1)
                            v.wait("a_ftr")
                            for q4 in range(4):
                                pu = ps_av[:, q4 * 128:q4 * 128 + 65]
                                v.reciprocal(recipS[:, q4:q4 + 1], pu[:, 64:65])
                                v.tensor_scalar(ytokn[:, q4 * 64:(q4 + 1) * 64],
                                                pu[:, 0:64], recipS[:, q4:q4 + 1],
                                                None, ALU.mult)
                            v.nop().inc("a_fv", 1)
                            t.wait("a_fv")
                            t.wait("a_fe", S.get("a_fe", 0))
                            ynps = bfv(ps_s1, 1)
                            for q4 in range(4):
                                t.matmul(ynps[0:64, q4 * 128:(q4 + 1) * 128],
                                         ytokn[:, q4 * 64:(q4 + 1) * 64], identb[:],
                                         is_transpose=True, start=True, stop=True)
                            t.mark("a_ftr2", 1)
                            v.wait("a_ftr2")
                            v.wait("a_ynst", max(0, S.get("a_ynst", 0)))
                            v.tensor_copy(ynT_sb[:], ynps[0:64, 0:512]).inc("a_fe", 1)
                            sc.wait("a_fe")
                            sc.wait("a_ynst", S.get("a_ynst", 0))
                            sc.dma_start(out=ynT_d[h // 2][64 * (h % 2):64 * (h % 2) + 64,
                                                           s_ * 512:(s_ + 1) * 512],
                                         in_=ynT_sb[:]).inc("a_ynst", 16)

                    sc.wait("ld", S.get("ld", 0))
                    sc.wait("ngst", S.get("ngst", 0))
                    load_qext_rows(0)
                    kvload(0, nblk[0], True, nblk[0])
                    for s_ in range(NST):
                        for ci in range(CC[s_]):
                            if ci + 1 < CC[s_]:
                                kvload(ci + 1, nblk[s_], True, nblk[s_])
                            for _ in main_chunk(s_, ci):
                                pass
                            flush_av()
                            kbusy[ci % 2] = S.get("a_qk", 0)
                            vbusy[ci % 2] = S.get("a_av", 0)
                        qk_end[s_] = S.get("a_qk", 0)
                        if debug:
                            sc.wait("dbst", S.get("dbst", 0))
                            sc.dma_start(out=dbg["mu"][s_],
                                         in_=qT_ext[64:65, (s_ % 2) * 8192:
                                                    (s_ % 2 + 1) * 8192]).inc("dbst", 16)
                        if s_ < 3:
                            load_qext_rows(s_ + 1)
                            kvload(0, nblk[s_ + 1], True, nblk[s_ + 1])
                        y_finalize(s_)

                ctx.variant_region(("vector", "scalar", "tensor", "gpsimd"), attention)

            def run_postattn(ctx: Ctx):
                sp, v, sc, t, g = ctx.sp, ctx.v, ctx.sc, ctx.t, ctx.g
                if debug:
                    sp.dma_start(out=dbg["ynT"][:], in_=ynT_d[:]).inc("ld", 16)

            def run_ffn(ctx: Ctx):
                sp, v, sc, t, g = ctx.sp, ctx.v, ctx.sc, ctx.t, ctx.g
                S = ctx.sems

                def drain_dma():
                    for s_ in sorted(DMA_SEMS):
                        sp.wait(s_)
                def wld(out, in_, **kw):
                    g.wait("f_wld", S.get("f_wld", 0))
                    g.dma_start(out=out, in_=in_, **kw).inc("f_wld", 16)
                wld(projb.rearrange("p (c f) -> p c f", c=8),
                    w_proj.rearrange("(c p) f -> p c f", p=128))
                wld(c1b[:], w_c1[:])
                wld(c2b[:], w_c2[:])
                wld(w1b.rearrange("p (c f) -> p c f", c=8),
                    w_in1.rearrange("(c p) f -> p c f", p=128))
                wld(wo1b.rearrange("p (c f) -> p c f", c=4),
                    w_out1T.rearrange("(c p) f -> p c f", p=128))
                wld(w2b.rearrange("p (c f) -> p c f", c=32),
                    w_in2.rearrange("(c p) f -> p c f", p=128))
                wld(wo2b.rearrange("p (c f) -> p c f", c=4),
                    w_out2T.rearrange("(c p) f -> p c f", p=128))
                wld(woTb.rearrange("p (c f) -> p c f", c=8),
                    w_oT.rearrange("(c p) f -> p c f", p=128))
                wld(b1c.rearrange("p c -> p c"),
                    w_b1.rearrange("(c p) o -> p (c o)", p=128),
                    allow_slow_non_contiguous=True)
                wld(b2c.rearrange("p c -> p c"),
                    w_b2.rearrange("(c p) o -> p (c o)", p=128),
                    allow_slow_non_contiguous=True)

                inv_k = emx[:, 0:16]
                kvec = emx[:, 16:32]
                iota16 = emx[:, 32:48]

                for ti in range(4):
                    sc.wait("f_act", S.get("f_act", 0))
                    sc.wait("f_xld", S.get("f_xld", 0))
                    sc.wait("f_mm", S.get("f_mm", 0))
                    sc.dma_start(out=fxn.rearrange("p (c f) -> p c f", c=8),
                                 in_=xnT_own.rearrange("c p t -> p c t")[:, :, ti * 512:(ti + 1) * 512]).inc("f_xld", 16)
                    t.wait("f_xld", 16 * (ti + 1))
                    if ti == 0:
                        t.wait("f_wld", 16 * 3)
                    t.wait("f_ve", S.get("f_ve", 0))
                    for dc in range(8):
                        t.matmul(ps_u[0:16, 0:512], projb[:, dc * 16:(dc + 1) * 16],
                                 fxn[:, dc * 512:(dc + 1) * 512],
                                 start=(dc == 0), stop=(dc == 7))
                    t.mark("f_mm", 1)
                    v.wait("f_mm")
                    v.tensor_copy(zproj[:], ps_u[0:16, 0:512]).inc("f_ve", 1)
                    t.wait("f_ve")
                    for q4 in range(4):
                        t.matmul(ps_s0[:, q4 * 128:q4 * 128 + 16],
                                 zproj[0:16, q4 * 128:(q4 + 1) * 128], identf[0:16, 0:16],
                                 is_transpose=True, start=True, stop=True)
                    t.mark("f_mm", 1)
                    v.wait("f_mm")
                    for q4 in range(4):
                        v.tensor_copy(ztok[:, q4 * 16:(q4 + 1) * 16],
                                      ps_s0[:, q4 * 128:q4 * 128 + 16])
                    v.nop().inc("f_ve", 1)
                    z4 = ztok.rearrange("p (g e) -> p g e", g=4)
                    v.tensor_reduce(ecols[:, 0:4], z4, axis=AX, op=ALU.add)
                    v.tensor_scalar(ecols[:, 0:4], ecols[:, 0:4], 1.0 / 16, None, ALU.mult)
                    mb = ecols[:, 0:4].rearrange("p (g e) -> p g e", e=1).broadcast_to([128, 4, 16])
                    ez4 = ezs.rearrange("p (g e) -> p g e", g=4)
                    v.tensor_tensor(ez4, z4, mb, ALU.subtract)
                    t4_ = etmp.rearrange("p (g e) -> p g e", g=4)
                    v.tensor_tensor(t4_, ez4, ez4, ALU.mult)
                    v.tensor_reduce(ecols[:, 4:8], t4_, axis=AX, op=ALU.add)
                    v.tensor_scalar(ecols[:, 4:8], ecols[:, 4:8], 1.0 / 16, None,
                                    ALU.mult).inc("f_ve", 1)
                    sc.wait("f_ve")
                    sc.activation(ecols[:, 8:12], ecols[:, 4:8], AF.Ln, bias=emx[:, 48:49])
                    sc.activation(ecols[:, 12:16], ecols[:, 8:12], AF.Exp,
                                  scale=-0.5).inc("f_act", 1)
                    v.wait("f_act")
                    rb = ecols[:, 12:16].rearrange("p (g e) -> p g e", e=1).broadcast_to([128, 4, 16])
                    v.tensor_tensor(ez4, ez4, rb, ALU.mult)
                    v.tensor_scalar(ezs[:], ezs[:], 0.5, None, ALU.mult)
                    v.tensor_reduce(ecols[:, 16:20], ez4, axis=AX, op=ALU.max)
                    xb_ = ecols[:, 16:20].rearrange("p (g e) -> p g e", e=1).broadcast_to([128, 4, 16])
                    v.tensor_tensor(ez4, ez4, xb_, ALU.subtract)
                    v.tensor_copy(ez3[:], ezs[:])
                    for rnd in range(16):
                        off = rnd % 2
                        n = (16 - off) // 2
                        src = ezs.rearrange("p (g e) -> p g e", g=4)[:, :, off:off + 2 * n]
                        pa = src.rearrange("p g (e2 two) -> p g e2 two", two=2)[:, :, :, 0:1]
                        pb_ = src.rearrange("p g (e2 two) -> p g e2 two", two=2)[:, :, :, 1:2]
                        ta = etmp.rearrange("p (g e) -> p g e", g=4)[:, :, 0:n].rearrange(
                            "p g (e2 o) -> p g e2 o", o=1)
                        v.tensor_tensor(ta, pa, pb_, ALU.max)
                        v.tensor_tensor(pb_, pa, pb_, ALU.min)
                        v.tensor_copy(pa, ta)
                    cs4 = ecs.rearrange("p (g e) -> p g e", g=4)
                    cq4 = ecs2.rearrange("p (g e) -> p g e", g=4)
                    v.tensor_tensor(cq4, ez4, ez4, ALU.mult)
                    v.tensor_copy(ecs[:], ezs[:])
                    for d_ in (1, 2, 4, 8):
                        for buf in (ecs, ecs2):
                            c4 = buf.rearrange("p (g e) -> p g e", g=4)
                            tt4 = etmp2.rearrange("p (g e) -> p g e", g=4)
                            v.tensor_copy(tt4[:, :, 0:16 - d_], c4[:, :, 0:16 - d_])
                            v.tensor_tensor(c4[:, :, d_:16], c4[:, :, d_:16],
                                            tt4[:, :, 0:16 - d_], ALU.add)
                    ik4 = inv_k.rearrange("p (o e) -> p o e", o=1).broadcast_to([128, 4, 16])
                    kv4 = kvec.rearrange("p (o e) -> p o e", o=1).broadcast_to([128, 4, 16])
                    v.tensor_tensor(cs4, cs4, ik4, ALU.mult)
                    v.tensor_tensor(cq4, cq4, ik4, ALU.mult)
                    v.tensor_tensor(t4_, cs4, cs4, ALU.mult)
                    v.tensor_tensor(t4_, cq4, t4_, ALU.subtract)
                    v.tensor_tensor(t4_, t4_, kv4, ALU.mult)
                    v.tensor_scalar(etmp[:], etmp[:], -1.0, 1.0, ALU.mult, ALU.add)
                    v.tensor_tensor(t4_, t4_, ik4, ALU.mult)
                    v.tensor_scalar(etmp[:], etmp[:], 0.0, None, ALU.max).inc("f_ve", 1)
                    sc.wait("f_ve")
                    sc.activation(etmp2[:], etmp[:], AF.Ln, bias=emx[:, 50:51])
                    sc.activation(etmp2[:], etmp2[:], AF.Exp, scale=0.5).inc("f_act", 1)
                    v.wait("f_act")
                    t24 = etmp2.rearrange("p (g e) -> p g e", g=4)
                    v.tensor_tensor(t24, cs4, t24, ALU.subtract)
                    s4_ = cs4
                    v.tensor_tensor(s4_, t24, ez4, ALU.is_le)
                    v.tensor_reduce(ecols[:, 20:24], s4_, axis=AX, op=ALU.add)
                    v.tensor_scalar(ecols[:, 20:24], ecols[:, 20:24], -1.0, None, ALU.add)
                    io4 = iota16.rearrange("p (o e) -> p o e", o=1).broadcast_to([128, 4, 16])
                    sbx = ecols[:, 20:24].rearrange("p (g e) -> p g e", e=1).broadcast_to([128, 4, 16])
                    v.tensor_tensor(s4_, io4, sbx, ALU.is_equal)
                    v.tensor_tensor(s4_, s4_, t24, ALU.mult)
                    v.tensor_reduce(ecols[:, 16:20], s4_, axis=AX, op=ALU.add)
                    tb_ = ecols[:, 16:20].rearrange("p (g e) -> p g e", e=1).broadcast_to([128, 4, 16])
                    e34 = ez3.rearrange("p (g e) -> p g e", g=4)
                    v.tensor_tensor(e34, e34, tb_, ALU.subtract)
                    v.tensor_scalar(ez3[:], ez3[:], 0.0, None, ALU.max)
                    v.tensor_tensor(e34, e34, e34, ALU.mult)
                    v.tensor_copy(ab16[:], ez3[:]).inc("f_ve", 1)
                    t.wait("f_ve")
                    aps_ = bfv(ps_u, 0)
                    for q4 in range(4):
                        t.matmul(aps_[0:16, q4 * 128:(q4 + 1) * 128],
                                 ab16[:, q4 * 16:(q4 + 1) * 16], identb[:],
                                 is_transpose=True, start=True, stop=True)
                    t.mark("f_mm", 1)
                    v.wait("f_mm")
                    v.tensor_copy(aT_all[0:16, ti * 512:(ti + 1) * 512],
                                  aps_[0:16, 0:512]).inc("f_ve", 1)

                for ti in range(4):
                    sp.wait("f_ost", max(0, S.get("f_ost", 0) - 16))
                    sp.wait("f_xt", S.get("f_xt", 0))
                    sp.wait("f_yld", S.get("f_yld", 0))
                    if ti == 0:
                        sp.wait("f_xld", S.get("f_xld", 0))
                    sp.wait("f_mm", S.get("f_mm", 0))
                    sp.dma_start(out=fxn.rearrange("p (c f) -> p c f", c=8),
                                 in_=xnT_own.rearrange("c p t -> p c t")[:, :, ti * 512:(ti + 1) * 512]).inc("f_xt", 16)
                    sp.dma_start(out=fyn.rearrange("p (c f) -> p c f", c=8),
                                 in_=ynT_d.rearrange("c p t -> p c t")[:, :, ti * 512:(ti + 1) * 512]).inc("f_yld", 16)
                    t.wait("f_wld", 16 * 10)
                    if ti == 0:
                        v.wait("f_wld", 16 * 10)
                    for rc in range(4):
                        t.wait("f_ve", S.get("f_ve", 0))
                        t.matmul(ps_s0[:, 0:512], c1b[0:16, rc * 128:(rc + 1) * 128],
                                 aT_all[0:16, ti * 512:(ti + 1) * 512], start=True, stop=True)
                        t.mark("f_mm", 1)
                        v.wait("f_mm")
                        v.tensor_copy(g1T[:, rc * 512:(rc + 1) * 512], ps_s0[:, 0:512]).inc("f_ve", 1)
                        t.wait("f_ve")
                        t.matmul(ps_s0[:, 0:512], c2b[0:16, rc * 128:(rc + 1) * 128],
                                 aT_all[0:16, ti * 512:(ti + 1) * 512], start=True, stop=True)
                        t.mark("f_mm", 1)
                        v.wait("f_mm")
                        v.tensor_copy(g2T[:, rc * 512:(rc + 1) * 512], ps_s0[:, 0:512]).inc("f_ve", 1)
                    t.wait("f_xt", S.get("f_xt", 0))
                    for rc in range(4):
                        t.wait("f_ve", S.get("f_ve", 0))
                        for dc in range(8):
                            t.matmul(ps_s1[:, 0:512],
                                     w1b[:, dc * 512 + rc * 128: dc * 512 + (rc + 1) * 128],
                                     fxn[:, dc * 512:(dc + 1) * 512],
                                     start=(dc == 0), stop=(dc == 7))
                        t.mark("f_mm", 1)
                        v.wait("f_mm")
                        v.scalar_tensor_tensor(z1b[:, rc * 512:(rc + 1) * 512],
                                               ps_s1[:, 0:512], b1c[:, rc:rc + 1],
                                               g1T[:, rc * 512:(rc + 1) * 512],
                                               ALU.add, ALU.mult).inc("f_ve", 1)
                    for oc in range(32):
                        pp = oc % 2
                        hps = ps_s0 if pp == 0 else ps_s1
                        t.wait("f_ve", S.get("f_ve", 0))
                        t.wait("f_gel", max(0, S.get("f_gel", 0) - 1))
                        for rc in range(4):
                            t.matmul(hps[:, 0:512],
                                     wo1b[:, rc * 4096 + oc * 128: rc * 4096 + (oc + 1) * 128],
                                     z1b[:, rc * 512:(rc + 1) * 512],
                                     start=(rc == 0), stop=(rc == 3))
                        t.mark("f_mm", 1)
                        sc.wait("f_mm")
                        sc.activation(hb[:, oc * 512:(oc + 1) * 512], hps[:, 0:512],
                                      AF.Gelu).inc("f_gel", 1)
                    for rc in range(4):
                        sc.wait("f_gel", S.get("f_gel", 0))
                        t.wait("f_gel", S.get("f_gel", 0))
                        t.wait("f_ve", S.get("f_ve", 0))
                        for oc in range(32):
                            t.matmul(ps_s1[:, 0:512],
                                     w2b[:, oc * 512 + rc * 128: oc * 512 + (rc + 1) * 128],
                                     hb[:, oc * 512:(oc + 1) * 512],
                                     start=(oc == 0), stop=(oc == 31))
                        t.mark("f_mm", 1)
                        v.wait("f_mm")
                        v.scalar_tensor_tensor(z2b[:, rc * 512:(rc + 1) * 512],
                                               ps_s1[:, 0:512], b2c[:, rc:rc + 1],
                                               g2T[:, rc * 512:(rc + 1) * 512],
                                               ALU.add, ALU.mult).inc("f_ve", 1)
                    t.wait("f_yld", S.get("f_yld", 0))
                    v.wait("f_yld", S.get("f_yld", 0))
                    for tt in range(4):
                        ob = outsb[:, (tt % 2) * 1024:(tt % 2) * 1024 + 1024]
                        sp.wait("f_res", S.get("f_res", 0))
                        sp.wait("f_xold", S.get("f_xold", 0))
                        sp.dma_start(out=fxo[:],
                                     in_=x_own[(ti * 4 + tt) * 128:(ti * 4 + tt + 1) * 128, :]).inc("f_xold", 16)
                        for half in range(2):
                            pcmb = ps_av[:, half * 512:(half + 1) * 512]
                            t.wait("f_ve", S.get("f_ve", 0))
                            t.wait("f_res", max(0, S.get("f_res", 0) - 1))
                            for dc in range(8):
                                t.matmul(pcmb,
                                         fyn[:, dc * 512 + tt * 128: dc * 512 + tt * 128 + 128],
                                         woTb[:, dc * 1024 + half * 512: dc * 1024 + (half + 1) * 512],
                                         start=(dc == 0), stop=False)
                            for rc in range(4):
                                t.matmul(pcmb,
                                         z2b[:, rc * 512 + tt * 128: rc * 512 + tt * 128 + 128],
                                         wo2b[:, rc * 1024 + half * 512: rc * 1024 + (half + 1) * 512],
                                         start=False, stop=(rc == 3))
                            t.mark("f_mm", 1)
                            v.wait("f_mm")
                            v.wait("f_xold")
                            v.wait("f_ost", max(0, S.get("f_ost", 0) - 16))
                            v.tensor_tensor(ob[:, half * 512:(half + 1) * 512], pcmb,
                                            fxo[:, half * 512:(half + 1) * 512],
                                            ALU.add).inc("f_res", 1)
                        sp.wait("f_res")
                        sp.wait("f_ost", S.get("f_ost", 0))
                        sp.dma_start(out=y_out[(ti * 4 + tt) * 128:(ti * 4 + tt + 1) * 128, :],
                                     in_=ob).inc("f_ost", 16)
                sp.wait("f_ost")
                drain_dma()
                v.wait("f_ost")
                t.wait("f_ost")
                sc.wait("f_ost")
                g.wait("f_ost")

            ctxs = {e: Ctx(nc, semh, e, SEM_OWNER) for e in ENGS}
            nsec = 8 if phase < 2 else 11
            for sec_i in range(nsec):
                with nc.Block(f"sec{sec_i}") as blk:
                    for eng in ENGS:
                        def mk(engname, s):
                            def f(_eng):
                                program(ctxs[engname], s)
                            return f
                        getattr(blk, eng)(mk(eng, sec_i))
                nc.all_engine_barrier()

    return nc


def _consts():
    identb = np.eye(128, dtype=BF)
    identf = np.eye(128, dtype=np.float32)
    inv_freq = (1.0 / ROPE_BASE) ** (np.arange(0, DH, 2, dtype=np.float32) / DH)
    tpos = np.arange(T, dtype=np.float32)
    freqs = np.outer(tpos, inv_freq)
    ropeC = np.cos(freqs).astype(np.float32).reshape(NT, 128, 32)
    ropeS = np.sin(freqs).astype(np.float32).reshape(NT, 128, 32)
    qi = np.arange(512)[None, :]
    ki = np.arange(128)[:, None]
    dmaskv = np.stack([(qi >= ki + 128 * j) for j in range(4)]).astype(BF)
    emx = np.zeros((128, 56), dtype=np.float32)
    emx[:, 0:16] = 1.0 / np.arange(1, 17)[None, :]
    emx[:, 16:32] = np.arange(1, 17)[None, :]
    emx[:, 32:48] = np.arange(16)[None, :]
    emx[:, 48] = 1e-5
    emx[:, 49] = 1e-12
    emx[:, 50] = 1e-30
    emx[:, 53] = 0.0
    emx[:, 54] = np.log(0.75)
    auxmb = np.zeros((128, 512), dtype=np.float32)
    cc = np.arange(128)[None, :]
    pp = np.arange(128)[:, None]
    auxmb[:, 384:512] = np.where(cc <= pp, 0.0, -1e9)
    return dict(c_identb=identb, c_identf=identf, c_ropeC=ropeC, c_ropeS=ropeS,
                c_dmask=dmaskv, c_emx=emx, c_auxmb=auxmb)


def _stage_weights(ins):
    f = np.float32
    return dict(
        wabqT=np.ascontiguousarray(np.asarray(ins["L0_attn_W_abq"], f).T),
        wabkvT=np.ascontiguousarray(np.asarray(ins["L0_attn_W_abkv"], f).T),
        w_proj=np.ascontiguousarray(np.asarray(ins["L0_ffn_proj"], f)),
        w_c1=np.ascontiguousarray(np.asarray(ins["L0_ffn_C1"], f)),
        w_c2=np.ascontiguousarray(np.asarray(ins["L0_ffn_C2"], f)),
        w_in1=np.ascontiguousarray(np.asarray(ins["L0_ffn_Win1"], f)[0:D, :]),
        w_b1=np.ascontiguousarray(np.asarray(ins["L0_ffn_Win1"], f)[D:D + 1, :].T),
        w_out1T=np.ascontiguousarray(np.asarray(ins["L0_ffn_Wout1"], f).T),
        w_in2=np.ascontiguousarray(np.asarray(ins["L0_ffn_Win2"], f)[0:OH, :]),
        w_b2=np.ascontiguousarray(np.asarray(ins["L0_ffn_Win2"], f)[OH:OH + 1, :].T),
        w_out2T=np.ascontiguousarray(np.asarray(ins["L0_ffn_Wout2"], f).T),
        w_oT=np.ascontiguousarray(np.asarray(ins["L0_attn_W_o"], f).T),
    )


def make_in_maps(inputs):
    x = np.asarray(inputs["L0_attn_x"], np.float32)
    weights = _stage_weights(inputs)
    consts = _consts()
    in_maps = []
    for c in range(8):
        b, v = c // 4, c % 4
        xb = np.ascontiguousarray(x[b])
        xo = np.ascontiguousarray(
            np.concatenate([xb[1024 * v:1024 * (v + 1)],
                            xb[1024 * (7 - v):1024 * (8 - v)]], axis=0))
        m = {"x_full": xb, "x_own": xo}
        m.update(weights)
        m.update(consts)
        in_maps.append(m)
    return in_maps


def kernel(**inputs):
    nc = build_nc(phase=3, debug=False)
    in_maps = make_in_maps(inputs)
    res = run_bass_kernel_spmd(nc, in_maps, core_ids=list(range(8)))
    out = np.empty((B, T, D), np.float32)
    for c in range(8):
        b, v = c // 4, c % 4
        yo = res.results[c]["y_own_out"]
        out[b, 1024 * v:1024 * (v + 1)] = yo[0:1024]
        out[b, 1024 * (7 - v):1024 * (8 - v)] = yo[1024:2048]
    return out


if __name__ == "__main__":
    import reference as R
    ins = R.setup_inputs(0)
    out = kernel(**{k: np.asarray(v) for k, v in ins.items()})
    ref = np.asarray(R.reference(**ins))
    num = float(np.mean((out - ref) ** 2))
    den = float(np.mean(ref ** 2))
    print("resid var ratio:", num / den)


# version 19 (final)
# speedup vs baseline: 1.4133x; 1.0106x over previous
import contextlib
import numpy as np
import ml_dtypes
import concourse.bass as bass
import concourse.mybir as mybir
from concourse.bass_utils import run_bass_kernel_spmd

F32 = mybir.dt.float32
BF16 = mybir.dt.bfloat16
AF = mybir.ActivationFunctionType
ALU = mybir.AluOpType
AX = mybir.AxisListType.X
BF = ml_dtypes.bfloat16

B, T, D = 2, 8192, 1024
NH, DH, CH = 16, 64, 80
RQ = 6
NT = T // 128
OWN = 2048
NOT_ = OWN // 128
NST = 4
E16, RM, OH = 16, 512, 4096
ROPE_BASE = 10000.0
LN_EPS = 1e-5
ENGS = ("sync", "vector", "scalar", "tensor", "gpsimd")


def var_kv_tiles(v):
    return 8 * (8 - v)


def own_tile_global(v, o):
    return 8 * v + o if o < 8 else (56 - 8 * v) + (o - 8)


def stripe_nblk(v, s):
    return [8 * v + 4, 8 * v + 8, 60 - 8 * v, 64 - 8 * v][s]


_CHAINED = {"vector", "scalar", "gpsimd"}
_NOCHAIN_OPS = {"dma_start", "dma_start_transpose", "nop", "sem_inc",
                "partition_id", "collective_compute"}
DMA_SEMS = {"ld", "ngst", "wcast", "p_x", "p_kst", "p_vst0", "p_vst1",
            "p_xnst0", "p_xnst1", "q_ld", "q_st",
            "a_kv0", "a_kv1", "a_qt", "x_qb", "x_st", "dbst", "a_ynst",
            "f_wld", "f_xld", "f_xt",
            "f_yld", "f_xold", "f_ost"}
BAR_SEMS = {"bar_v", "bar_s", "bar_g", "bar_t", "rdone_vector", "rdone_scalar",
            "rdone_tensor", "rdone_gpsimd", "rdone_sync"}


def _is_virtual(sem, owner):
    return (owner.get(sem) in _CHAINED and sem not in DMA_SEMS
            and sem not in BAR_SEMS and not sem.startswith("ch_"))


class _Instr:
    __slots__ = ("ctx", "real", "chained", "eng")

    def __init__(self, ctx, real, chained, eng):
        self.ctx = ctx
        self.real = real
        self.chained = chained
        self.eng = eng

    def inc(self, sem, k=1):
        ctx = self.ctx
        ctx.sems[sem] = ctx.sems.get(sem, 0) + k
        if self.eng in _CHAINED and _is_virtual(sem, ctx.owner):
            ctx.hist.setdefault(sem, {})[ctx.sems[sem]] = ctx.sems.get(
                "ch_" + self.eng, 0)
        else:
            assert not self.chained, f"real sem {sem} inc on chained op"
            if self.real is not None:
                self.real.then_inc(ctx.semh[sem], k)
        return self


class _Prox:
    def __init__(self, ctx, name):
        self._ctx = ctx
        self._name = name

    def __getattr__(self, op):
        ctx = self._ctx
        name = self._name
        chain = name in _CHAINED and op not in _NOCHAIN_OPS
        csem = "ch_" + name

        def call(*a, **kw):
            if op == "nop" and "nofuse" not in kw:
                kw["nofuse"] = True
            real = None
            if chain:
                cur = ctx.sems.get(csem, 0)
                if cur > 0 and ctx.active == name and ctx.dry == 0:
                    getattr(ctx.nc, name).wait_ge(ctx.semh[csem], cur)
            if ctx.active == name and ctx.dry == 0:
                real = getattr(getattr(ctx.nc, name), op)(*a, **kw)
            ins = _Instr(ctx, real, chain, name)
            if chain:
                ctx.sems[csem] = ctx.sems.get(csem, 0) + 1
                if real is not None:
                    real.then_inc(ctx.semh[csem], 1)
            if name == "tensor" and op in ("matmul", "transpose"):
                ctx.last_tensor = ins
            return ins

        return call

    def mark(self, sem, k=1):
        assert self._name == "tensor"
        self._ctx.last_tensor.inc(sem, k)

    def wait(self, sem, val=None):
        ctx = self._ctx
        v = ctx.sems.get(sem, 0) if val is None else val
        if v <= 0:
            return
        own = ctx.owner.get(sem)
        if _is_virtual(sem, ctx.owner):
            if v <= ctx.hist_floor.get(sem, 0):
                return
            cc = ctx.hist.get(sem, {}).get(v)
            if cc is None:
                raise RuntimeError(f"wait {sem}>={v}: no hist entry "
                                   f"(have {sorted(ctx.hist.get(sem, {}))[-4:]})")
            sem, v = "ch_" + own, cc
        key = (self._name, sem)
        if ctx.lastwait.get(key, -1) >= v:
            return
        ctx.lastwait[key] = v
        if ctx.active == self._name and ctx.dry == 0:
            getattr(ctx.nc, self._name).wait_ge(ctx.semh[sem], v)

    def pad_sem(self, sem, target):
        ctx = self._ctx
        cur = ctx.sems.get(sem, 0)
        if target > cur:
            if ctx.active == self._name and ctx.dry == 0:
                eng = getattr(ctx.nc, self._name)
                step = 16 if sem in DMA_SEMS else 1
                n = target - cur
                assert n % step == 0, (sem, cur, target)
                for _ in range(n // step):
                    eng.nop(cycle_cnt=4, nofuse=True).then_inc(ctx.semh[sem], step)
            ctx.sems[sem] = target


class Ctx:
    def __init__(self, nc, semh, active, owner):
        self.nc = nc
        self.semh = semh
        self.active = active
        self.owner = owner
        self.sems = {}
        self.hist = {}
        self.hist_floor = {}
        self.lastwait = {}
        self.dry = 0
        self.last_tensor = _Instr(self, None, False, 'tensor')
        self.sp = _Prox(self, "sync")
        self.v = _Prox(self, "vector")
        self.sc = _Prox(self, "scalar")
        self.t = _Prox(self, "tensor")
        self.g = _Prox(self, "gpsimd")
        self.prox = {"sync": self.sp, "vector": self.v, "scalar": self.sc,
                     "tensor": self.t, "gpsimd": self.g}

    def _snap(self):
        return (dict(self.sems), {k: dict(v) for k, v in self.hist.items()},
                dict(self.hist_floor), dict(self.lastwait))

    def _restore(self, snap):
        self.sems.clear()
        self.sems.update(snap[0])
        self.hist.clear()
        self.hist.update({k: dict(v) for k, v in snap[1].items()})
        self.hist_floor.clear()
        self.hist_floor.update(snap[2])
        self.lastwait.clear()
        self.lastwait.update(snap[3])

    def variant_region(self, engines, body_fn):
        snap = self._snap()
        ends = []
        for v in range(4):
            self._restore(snap)
            self.dry += 1
            body_fn(v)
            self.dry -= 1
            ends.append(self.sems)
        allsems = set()
        for e in ends:
            allsems |= set(e)
        incremented = sorted(k for k in allsems
                             if max(e.get(k, 0) for e in ends) > snap[0].get(k, 0))
        for k in incremented:
            if self.owner.get(k) not in engines:
                raise RuntimeError(
                    f"sem {k} (owner {self.owner.get(k)}) incremented in "
                    f"region with engines {engines}")

        def epilogue():
            for k in incremented:
                own = self.owner[k]
                if own in engines:
                    self.prox[own].wait(k)
            for e in engines:
                if e in _CHAINED:
                    self.prox[e].wait("ch_" + e)
                self.prox[e].drain()

        if self.active in engines and self.dry == 0:
            eng = getattr(self.nc, self.active)
            pid = eng.partition_id()
            idx = pid & 3
            for arm in eng.Switch(idx, 4):
                self._restore(snap)
                body_fn(arm)
                epilogue()
        else:
            self._restore(snap)
            self.dry += 1
            body_fn(0)
            epilogue()
            self.dry -= 1
        self._restore(snap)
        self.pending_region = (engines, incremented)

    def clear_region(self):
        engines, incremented = self.pending_region
        for k in incremented:
            own = self.owner[k]
            if self.active == own and self.dry == 0:
                getattr(self.nc, own).sem_clear(self.semh[k])
            self.sems[k] = 0
            self.hist.pop(k, None)
            self.hist_floor.pop(k, None)
            for kk in list(self.lastwait):
                if kk[1] == k:
                    del self.lastwait[kk]
        for e in engines:
            if e in _CHAINED:
                if self.active == e and self.dry == 0:
                    getattr(self.nc, e).sem_clear(self.semh["ch_" + e])
                self.sems["ch_" + e] = 0
                for kk in list(self.lastwait):
                    if kk[1] == "ch_" + e:
                        del self.lastwait[kk]
                for k in list(self.hist):
                    if self.owner.get(k) == e:
                        self.hist.pop(k)
                        self.hist_floor[k] = self.sems.get(k, 0)


SEM_OWNER = {
    "ld": "sync", "ngst": "sync", "ng": "vector",
    "wcast": "gpsimd", "vset": "gpsimd",
    "ch_vector": "vector", "ch_scalar": "scalar", "ch_gpsimd": "gpsimd",
    "p_x": "scalar", "p_stats": "vector", "p_rstd": "scalar", "p_xnt": "tensor",
    "p_xne": "scalar", "p_mm": "tensor", "p_pev": "scalar", "p_rot": "gpsimd",
    "p_kprod": "vector", "p_ktr": "tensor", "p_kte": "scalar", "p_kst": "scalar",
    "p_vprod": "gpsimd", "p_vst0": "gpsimd", "p_vst1": "gpsimd",
    "p_xnst0": "scalar", "p_xnst1": "scalar",
    "q_ld": "scalar", "q_mm": "tensor", "q_ev": "scalar", "q_prod": "vector",
    "q_tr": "tensor", "q_te": "scalar", "q_st": "scalar",
    "bar_v": "vector", "bar_t": "tensor", "bar_s": "scalar", "bar_g": "gpsimd",
    "rdone_vector": "vector", "rdone_scalar": "scalar", "rdone_tensor": "tensor",
    "rdone_gpsimd": "gpsimd", "rdone_sync": "sync",
    "a_kv0": "scalar", "a_kv1": "scalar",
    "a_qt": "scalar", "x_qb": "scalar", "x_st": "scalar", "dbst": "scalar",
    "x_mm": "tensor", "x_mx": "vector", "x_ls": "scalar",
    "x_tr": "tensor", "x_ev": "vector",
    "a_qk": "tensor", "a_exp": "scalar",
    "a_msk": "gpsimd", "a_av": "tensor", "a_evac": "vector",
    "a_ftr": "tensor", "a_fv": "vector", "a_ftr2": "tensor", "a_fe": "vector",
    "a_ynst": "scalar",
    "f_wld": "gpsimd", "f_xld": "scalar", "f_xt": "sync", "f_yld": "sync",
    "f_xold": "sync",
    "f_mm": "tensor", "f_ve": "vector", "f_act": "scalar", "f_ost": "sync", "f_res": "vector",
    "f_gel": "scalar",
}


def build_nc(phase=3, debug=False):
    nc = bass.Bass("TRN2", target_bir_lowering=False)

    def inp(name, shape, dt=F32):
        return nc.dram_tensor(name, shape, dt, kind="ExternalInput").ap()

    x_full = inp("x_full", [T, D])
    x_own = inp("x_own", [OWN, D])
    wabqT = inp("wabqT", [D, RQ * CH])
    wabkvT = inp("wabkvT", [D, 4 * CH])
    w_proj = inp("w_proj", [D, E16])
    w_c1 = inp("w_c1", [E16, RM])
    w_c2 = inp("w_c2", [E16, RM])
    w_in1 = inp("w_in1", [D, RM])
    w_b1 = inp("w_b1", [RM, 1])
    w_out1T = inp("w_out1T", [RM, OH])
    w_in2 = inp("w_in2", [OH, RM])
    w_b2 = inp("w_b2", [RM, 1])
    w_out2T = inp("w_out2T", [RM, D])
    w_oT = inp("w_oT", [D, D])
    c_identb = inp("c_identb", [128, 128], BF16)
    c_identf = inp("c_identf", [128, 128], F32)
    c_ropeC = inp("c_ropeC", [NT, 128, 32])
    c_ropeS = inp("c_ropeS", [NT, 128, 32])
    c_dmask = inp("c_dmask", [4, 128, 512], BF16)
    c_emx = inp("c_emx", [128, 56])
    c_auxmb = inp("c_auxmb", [128, 512])

    y_out = nc.dram_tensor("y_own_out", [OWN, D], F32, kind="ExternalOutput").ap()

    kv_k = nc.dram_tensor("kv_k", [NH, 65, T], BF16, kind="Internal").ap()
    kv_v = nc.dram_tensor("kv_v", [T, NH * 65], BF16, kind="Internal").ap()
    xnT_all = nc.dram_tensor("xnT_all", [8, 128, T], BF16, kind="Internal").ap()
    xnT_own = nc.dram_tensor("xnT_own", [8, 128, OWN], BF16, kind="Internal").ap()
    qT_d = nc.dram_tensor("qT_d", [8, 128, OWN], BF16, kind="Internal").ap()
    ynT_d = nc.dram_tensor("ynT_d", [8, 128, OWN], BF16, kind="Internal").ap()
    mu_d = nc.dram_tensor("mu_d", [64, 128], BF16, kind="Internal").ap()
    muT_d = nc.dram_tensor("muT_d", [16, OWN], BF16, kind="Internal").ap()

    dbg = {}
    if debug:
        def dout(name, shape, dt):
            dbg[name] = nc.dram_tensor("dbg_" + name, shape, dt, kind="ExternalOutput").ap()
        dout("xnT", [8, 128, T], BF16)
        dout("kvk", [NH, 65, T], BF16)
        dout("kvv", [T, NH * 65], BF16)
        dout("qT", [8, 128, OWN], BF16)
        dout("ynT", [8, 128, OWN], BF16)
        dout("yacc", [NST, 65, 16 * 512], F32)
        dout("mu", [NST, 1, 16 * 512], BF16)

    with contextlib.ExitStack() as stack:
        semh = {s: stack.enter_context(nc.semaphore(s)) for s in SEM_OWNER}

        def sb(name, shape, dt):
            return stack.enter_context(nc.sbuf_tensor(name, shape, dt)).ap()

        identb = sb("identb", [128, 128], BF16)
        identf = sb("identf", [128, 128], F32)
        dmask = sb("dmask", [128, 4 * 512], BF16)
        emx = sb("emx", [128, 56], F32)
        auxmb = sb("auxmb", [128, 512], F32)
        cols = sb("cols", [128, 16], F32)
        negrow = sb("negrow", [128, 1024], BF16)

        ARENA = 196 * 1024
        U8 = mybir.dt.uint8
        arena = sb("arena", [128, ARENA], U8)
        _esize = {F32: 4, BF16: 2}

        def mkregion():
            off = [0]

            def alloc(shape, dt):
                nb = shape[1] * _esize[dt]
                o = (off[0] + 31) // 32 * 32
                assert o + nb <= ARENA, f"arena overflow {o + nb}"
                off[0] = o + nb
                vw = arena[:, o:o + nb].bitcast(dt)
                if shape[0] != 128:
                    vw = vw[0:shape[0], :]
                return vw
            return alloc

        P = mkregion()
        ropeC = P([128, NT * 32], F32)
        ropeS = P([128, NT * 32], F32)
        wabq_b = P([128, 8 * 480], BF16)
        wabkv_b = P([128, 8 * 320], BF16)
        xt = P([128, 2 * 1024], F32)
        scr = P([128, 1024], F32)
        rscr = P([128, 2 * 256], F32)
        xnb = P([128, 1024], BF16)
        xnT_sb = P([128, 2 * 1024], BF16)
        akv = P([128, 2 * 64], F32)
        bkv = P([128, 2 * 256], F32)
        bkvr = P([128, 2 * 256], F32)
        Pt = P([128, 2 * 2048 + 1024], F32)
        ptb = P([128, 2 * 1024], BF16)
        Pvt = P([128, 1024], BF16)
        vstage = P([128, 2 * 1040], BF16)
        kT_sb = P([128, 2 * 1024], BF16)
        xq_t = P([128, 2 * 1024], BF16)
        aq = P([128, 2 * 96], F32)
        bq = P([128, 2 * 384], F32)
        qT_sb = P([128, 2 * 1024], BF16)
        kq = P([128, 1024], F32)
        ksq = P([128, 16], F32)
        kmax = P([128, 16], F32)
        kmaxT = P([16, 128], F32)
        km1 = P([16, 1], F32)
        km1r = P([1, 16], F32)
        kbr = P([128, 16], F32)
        ones1 = P([1, 128], F32)
        qsq = P([128, 16], F32)
        mu2 = P([128, 16], F32)
        muv = P([128, 16], F32)
        muT_sb = P([16, 128], BF16)

        A = mkregion()
        kr_k = A([65, 2 * 16 * 1024], BF16)
        kr_v = A([128, 2 * 8 * 1040], BF16)
        qbuf = A([64, 16 * 512], BF16)
        qT_ext = A([65, 2 * 16 * 512], BF16)
        rmx = A([128, 64], F32)
        mstage = A([64, 128], BF16)
        mtmp = A([128, 1], F32)
        junk = A([128, 512], F32)
        junk2 = A([128, 512], BF16)
        lsb = A([128, 512], F32)
        red32 = A([128, 32], F32)
        pT0 = A([128, 1024], BF16)
        pT1 = A([128, 1024], BF16)
        y_acc = A([65, 16 * 512], F32)
        ytokn = A([128, 4 * 64], BF16)
        ynT_sb = A([64, 512], BF16)
        recipS = A([128, 4], F32)

        F = mkregion()
        w1b = F([128, 8 * 512], BF16)
        wo1b = F([128, 4 * 4096], BF16)
        w2b = F([128, 32 * 512], BF16)
        wo2b = F([128, 4 * 1024], BF16)
        woTb = F([128, 8 * 1024], BF16)
        projb = F([128, 8 * 16], BF16)
        c1b = F([16, 512], BF16)
        c2b = F([16, 512], BF16)
        b1c = F([128, 4], F32)
        b2c = F([128, 4], F32)
        fxn = F([128, 8 * 512], BF16)
        fyn = F([128, 8 * 512], BF16)
        fxo = F([128, 1024], F32)
        zproj = F([16, 512], F32)
        ztok = F([128, 64], F32)
        ecols = F([128, 24], F32)
        ezs = F([128, 64], F32)
        ez3 = F([128, 64], F32)
        etmp = F([128, 64], F32)
        etmp2 = F([128, 64], F32)
        ecs = F([128, 64], F32)
        ecs2 = F([128, 64], F32)
        ab16 = F([128, 64], BF16)
        aT_all = F([16, 4 * 512], BF16)
        g1T = F([128, 4 * 512], F32)
        g2T = F([128, 4 * 512], F32)
        z1b = F([128, 4 * 512], BF16)
        hb = F([128, 32 * 512], BF16)
        z2b = F([128, 4 * 512], BF16)
        outsb = F([128, 2 * 1024], F32)

        ps_s0 = stack.enter_context(nc.psum_tensor("ps_s0", [128, 1024], F32)).ap()
        ps_s1 = stack.enter_context(nc.psum_tensor("ps_s1", [128, 1024], F32)).ap()
        ps_av = stack.enter_context(nc.psum_tensor("ps_av", [128, 1024], F32)).ap()
        ps_u = stack.enter_context(nc.psum_tensor("ps_u", [128, 512], F32)).ap()
        ps_x = stack.enter_context(nc.psum_tensor("ps_x", [128, 512], F32)).ap()

        def bfv(ps, bank):
            return ps[:, bank * 512:(bank + 1) * 512].bitcast(BF16)

        if True:
            def program(ctx: Ctx, sec: int):
                sp, v, sc, t, g = ctx.sp, ctx.v, ctx.sc, ctx.t, ctx.g
                S = ctx.sems

                def drain_dma():
                    for s_ in sorted(DMA_SEMS):
                        sp.wait(s_)

                if sec == 0:
                    run_init(ctx)
                elif sec == 1:
                    run_prep(ctx)
                elif sec == 2:
                    ctx.clear_region()
                elif sec == 3:
                    run_kfin(ctx)
                elif sec == 4:
                    run_qprep(ctx)
                elif sec == 5:
                    ctx.clear_region()
                elif sec == 6:
                    run_bias(ctx)
                elif sec == 7:
                    if phase < 2:
                        run_placeholder_out(ctx)
                    else:
                        run_attention(ctx)
                elif sec == 8:
                    ctx.clear_region()
                elif sec == 9:
                    run_postattn(ctx)
                elif sec == 10:
                    if phase < 3:
                        run_placeholder_out(ctx)
                    else:
                        run_ffn(ctx)
                else:
                    raise RuntimeError(sec)

            def run_init(ctx: Ctx):
                sp, v, sc, t, g = ctx.sp, ctx.v, ctx.sc, ctx.t, ctx.g
                S = ctx.sems
                sp.dma_start(out=identb[:], in_=c_identb[:]).inc("ld", 16)
                sp.dma_start(out=identf[:], in_=c_identf[:]).inc("ld", 16)
                sp.dma_start(out=ropeC.rearrange("p (g c) -> p g c", c=32),
                             in_=c_ropeC.rearrange("g p c -> p g c")).inc("ld", 16)
                sp.dma_start(out=ropeS.rearrange("p (g c) -> p g c", c=32),
                             in_=c_ropeS.rearrange("g p c -> p g c")).inc("ld", 16)
                sp.dma_start(out=dmask.rearrange("p (j c) -> p j c", c=512),
                             in_=c_dmask.rearrange("j p c -> p j c")).inc("ld", 16)
                sp.dma_start(out=emx[:], in_=c_emx[:]).inc("ld", 16)
                sp.dma_start(out=auxmb[:], in_=c_auxmb[:]).inc("ld", 16)
                g.dma_start(out=wabq_b.rearrange("p (c f) -> p c f", c=8),
                            in_=wabqT.rearrange("(c p) f -> p c f", p=128)).inc("wcast", 16)
                g.dma_start(out=wabkv_b.rearrange("p (c f) -> p c f", c=8),
                            in_=wabkvT.rearrange("(c p) f -> p c f", p=128)).inc("wcast", 16)
                g.memset(vstage.rearrange("p (b h c) -> p b h c", b=2, h=16)[:, :, :, 64:65],
                         1.0).inc("vset", 1)
                v.memset(ones1[0:1, :], 1.0)
                v.memset(negrow[:], -1.0).inc("ng", 1)
                sp.wait("ng")
                sp.dma_start(
                    out=kv_k[:, 64:65, :].rearrange("h o (p t) -> p h o t", p=128),
                    in_=negrow.rearrange("p (h o t) -> p h o t", h=16, o=1)
                ).inc("ngst", 16)

            def run_prep(ctx: Ctx):
                sp, v, sc, t, g = ctx.sp, ctx.v, ctx.sc, ctx.t, ctx.g
                S = ctx.sems
                def kv_prep(vv):
                    ntl = var_kv_tiles(vv)
                    v.memset(kmax[:, 0:16], 0.0)

                    def stage1(i):
                        pb = i % 2
                        xti = xt[:, pb * 1024:(pb + 1) * 1024]
                        sp.wait("p_stats", max(0, 2 * (i - 1)))
                        sp.dma_start(out=xti, in_=x_full[i * 128:(i + 1) * 128, :]).inc("p_x", 16)
                        v.wait("p_x")
                        if i == 0:
                            v.wait("wcast", 32)
                            v.wait("vset", 1)
                            v.wait("ld", 112)
                        v.reduce_sum(cols[:, 0:1], xti, axis=AX)
                        v.scalar_tensor_tensor(scr[:], xti, 1.0, xti, ALU.mult, ALU.mult,
                                               accum_out=cols[:, 1:2])
                        v.tensor_scalar(cols[:, 2:3], cols[:, 0:1], 1.0 / D, None, ALU.mult)
                        v.tensor_scalar(cols[:, 3:4], cols[:, 1:2], 1.0 / D, None, ALU.mult)
                        v.tensor_mul(cols[:, 4:5], cols[:, 2:3], cols[:, 2:3])
                        v.tensor_sub(cols[:, 5:6], cols[:, 3:4], cols[:, 4:5]).inc("p_stats", 1)
                        sc.wait("p_stats", 2 * i + 1)
                        sc.activation(cols[:, 6:7], cols[:, 5:6], AF.Ln, bias=emx[:, 48:49])
                        sc.activation(cols[:, 7:8], cols[:, 6:7], AF.Exp, scale=-0.5).inc("p_rstd", 1)
                        v.wait("p_rstd")
                        v.tensor_scalar(xnb[:], xti, cols[:, 2:3], cols[:, 7:8],
                                        ALU.subtract, ALU.mult).inc("p_stats", 1)
                        xnt_ps = bfv(ps_u if i % 2 == 0 else ps_x, 0)
                        t.wait("p_stats", 2 * i + 2)
                        t.wait("p_xne", max(0, i - 1))
                        for fc in range(8):
                            t.matmul(xnt_ps[:, fc * 128:(fc + 1) * 128],
                                     xnb[:, fc * 128:(fc + 1) * 128], identb[:],
                                     is_transpose=True, start=True, stop=True)
                        t.mark("p_xnt", 1)
                        xnti = xnT_sb[:, pb * 1024:(pb + 1) * 1024]
                        sc.wait("p_xnt")
                        sc.wait("p_mm", max(0, i - 1))
                        sc.wait("p_xnst0" if pb == 0 else "p_xnst1")
                        sc.activation(xnti, xnt_ps, AF.Copy, bias=0.0).inc("p_xne", 1)
                        sp.wait("p_xne")
                        sp.dma_start(out=xnT_all.rearrange("c p t -> p c t")[:, :, i * 128:(i + 1) * 128],
                                     in_=xnti.rearrange("p (c t) -> p c t", c=8)).inc("p_xnst0" if pb == 0 else "p_xnst1", 16)
                        t.wait("p_xne")
                        t.wait("p_pev", max(0, i - 1))
                        s0c = ps_s0[:, pb * 512:pb * 512 + 320]
                        for dc in range(8):
                            t.matmul(s0c, xnti[:, dc * 128:(dc + 1) * 128],
                                     wabkv_b[:, dc * 320:(dc + 1) * 320],
                                     start=(dc == 0), stop=(dc == 7))
                        t.mark("p_mm", 1)
                        akv_i = akv[:, pb * 64:(pb + 1) * 64]
                        bkv_i = bkv[:, pb * 256:(pb + 1) * 256]
                        bkvr_i = bkvr[:, pb * 256:(pb + 1) * 256]
                        sc.wait("p_mm")
                        sc.wait("p_vprod", max(0, i - 1))
                        sc.wait("p_rot", max(0, i - 1))
                        aps = s0c.rearrange("p (r c) -> p r c", r=4)
                        sc.activation(akv_i.rearrange("p (r h) -> p r h", r=4),
                                      aps[:, :, 0:16], AF.Copy, scale=0.5, bias=0.0)
                        sc.activation(bkv_i.rearrange("p (r c) -> p r c", r=4),
                                      aps[:, :, 16:80], AF.Copy, bias=0.0).inc("p_pev", 1)

                    def stage2(i):
                        pb = i % 2
                        akv_i = akv[:, pb * 64:(pb + 1) * 64]
                        bkv_i = bkv[:, pb * 256:(pb + 1) * 256]
                        bkvr_i = bkvr[:, pb * 256:(pb + 1) * 256]
                        g.wait("p_pev", i + 1)
                        g.wait("p_kprod", max(0, i - 1))
                        cosr = ropeC[:, i * 32:(i + 1) * 32].rearrange(
                            "p (o c) -> p o c", o=1).broadcast_to([128, 4, 32])
                        sinr = ropeS[:, i * 32:(i + 1) * 32].rearrange(
                            "p (o c) -> p o c", o=1).broadcast_to([128, 4, 32])
                        bk4 = bkv_i.rearrange("p (r c) -> p r c", r=4)
                        br4 = bkvr_i.rearrange("p (r c) -> p r c", r=4)
                        rs4 = rscr[:, pb * 256:(pb + 1) * 256].rearrange(
                            "p (r c) -> p r c", r=8)
                        g.tensor_tensor(br4[:, :, 0:32], bk4[:, :, 0:32], cosr, ALU.mult)
                        g.tensor_tensor(rs4[:, 0:4, :], bk4[:, :, 32:64], sinr, ALU.mult)
                        g.tensor_tensor(br4[:, :, 0:32], br4[:, :, 0:32], rs4[:, 0:4, :], ALU.add)
                        g.tensor_tensor(br4[:, :, 32:64], bk4[:, :, 32:64], cosr, ALU.mult)
                        g.tensor_tensor(rs4[:, 4:8, :], bk4[:, :, 0:32], sinr, ALU.mult)
                        g.tensor_tensor(br4[:, :, 32:64], br4[:, :, 32:64], rs4[:, 4:8, :],
                                        ALU.subtract).inc("p_rot", 1)
                        ptv = Pt[:, pb * 2048:(pb + 1) * 2048]
                        v.wait("p_rot", i + 1)
                        v.wait("p_ktr", max(0, i - 1))
                        for s_ in range(2):
                            a_b = akv_i[:, s_ * 16:(s_ + 1) * 16].rearrange(
                                "p (o h) -> p h o", o=1).broadcast_to([128, 16, 64])
                            b_b = bkvr_i[:, s_ * 64:(s_ + 1) * 64].rearrange(
                                "p (o c) -> p o c", o=1).broadcast_to([128, 16, 64])
                            v.tensor_tensor(
                                ptv[:, s_ * 1024:(s_ + 1) * 1024].rearrange("p (h c) -> p h c", h=16),
                                a_b, b_b, ALU.mult)
                        v.tensor_tensor(ptb[:, pb * 1024:(pb + 1) * 1024],
                                        ptv[:, 0:1024], ptv[:, 1024:2048],
                                        ALU.add)
                        v.nop().inc("p_kprod", 1)
                        ptbk = ptb[:, pb * 1024:(pb + 1) * 1024]
                        v.scalar_tensor_tensor(kq[:], ptbk, 1.0, ptbk,
                                               ALU.mult, ALU.mult)
                        v.tensor_reduce(ksq[:, 0:16],
                                        kq.rearrange("p (h c) -> p h c", h=16),
                                        axis=AX, op=ALU.add)
                        v.tensor_tensor(kmax[:, 0:16], kmax[:, 0:16],
                                        ksq[:, 0:16], ALU.max)
                        kt_ps = bfv(ps_s1 if pb == 0 else ps_av, 0)
                        t.wait("p_kprod", i + 1)
                        t.wait("p_kte", max(0, i - 1))
                        for fc in range(8):
                            t.matmul(kt_ps[:, fc * 128:(fc + 1) * 128],
                                     ptb[:, pb * 1024 + fc * 128:
                                         pb * 1024 + (fc + 1) * 128],
                                     identb[:], is_transpose=True,
                                     start=True, stop=True)
                        t.mark("p_ktr", 1)
                        kT_i = kT_sb[:, pb * 1024:(pb + 1) * 1024]
                        sc.wait("p_ktr", i + 1)
                        sc.wait("p_kst", 32 * i)
                        sc.activation(kT_i, kt_ps[:, 0:1024], AF.Copy,
                                      bias=0.0).inc("p_kte", 1)
                        sp.wait("p_kte", i + 1)
                        kv2 = kv_k.rearrange("(c two) r t -> two r c t", two=2)
                        for two in range(2):
                            sp.dma_start(
                                out=kv2[two][0:64, :, i * 128:(i + 1) * 128],
                                in_=kT_i[64 * two:64 * two + 64, :].rearrange(
                                    "p (c t) -> p c t", c=8)).inc("p_kst", 16)
                        vst = vstage[:, pb * 1040:(pb + 1) * 1040]
                        vsh = vst.rearrange("p (h c) -> p h c", h=16)[:, :, 0:64]
                        g.wait("p_vst0" if pb == 0 else "p_vst1")
                        for s_ in (2, 3):
                            a_b = akv_i[:, s_ * 16:(s_ + 1) * 16].rearrange(
                                "p (o h) -> p h o", o=1).broadcast_to([128, 16, 64])
                            b_b = bkvr_i[:, s_ * 64:(s_ + 1) * 64].rearrange(
                                "p (o c) -> p o c", o=1).broadcast_to([128, 16, 64])
                            if s_ == 2:
                                g.tensor_tensor(vsh, a_b, b_b, ALU.mult)
                            else:
                                g.tensor_tensor(Pvt.rearrange("p (h c) -> p h c", h=16),
                                                a_b, b_b, ALU.mult)
                                g.tensor_tensor(vsh, vsh,
                                                Pvt.rearrange("p (h c) -> p h c", h=16),
                                                ALU.add)
                        g.nop().inc("p_vprod", 1)
                        sp.wait("p_vprod", i + 1)
                        sp.dma_start(out=kv_v[i * 128:(i + 1) * 128, :], in_=vst).inc("p_vst0" if pb == 0 else "p_vst1", 16)

                    for i in range(ntl + 1):
                        if i < ntl:
                            stage1(i)
                        if i >= 1:
                            stage2(i - 1)

                ctx.variant_region(("sync", "vector", "scalar", "tensor", "gpsimd"),
                                   kv_prep)


            def run_kfin(ctx: Ctx):
                pass

            def run_qprep(ctx: Ctx):
                sp, v, sc, t, g = ctx.sp, ctx.v, ctx.sc, ctx.t, ctx.g
                S = ctx.sems
                def q_prep(vv):
                    qtr_n = [S.get("q_tr", 0)]
                    t.matmul(ps_u[0:16, 0:128], kmax[:, 0:16], identf[:],
                             is_transpose=True, start=True, stop=True)
                    t.mark("x_tr", 1)
                    v.wait("x_tr", S.get("x_tr", 0))
                    v.tensor_copy(kmaxT[0:16, :], ps_u[0:16, 0:128])
                    v.tensor_reduce(km1[0:16, 0:1], kmaxT[0:16, :], axis=AX,
                                    op=ALU.max)
                    v.nop().inc("x_mx", 1)
                    t.wait("x_mx", S.get("x_mx", 0))
                    t.matmul(ps_u[0:1, 128:144], km1[0:16, 0:1],
                             identf[0:16, 0:16], is_transpose=True,
                             start=True, stop=True)
                    t.mark("x_tr", 1)
                    v.wait("x_tr", S.get("x_tr", 0))
                    v.tensor_copy(km1r[0:1, :], ps_u[0:1, 128:144])
                    v.nop().inc("x_mx", 1)
                    t.wait("x_mx", S.get("x_mx", 0))
                    t.matmul(ps_u[0:128, 256:272], ones1[0:1, :], km1r[0:1, :],
                             start=True, stop=True)
                    t.mark("x_tr", 1)
                    v.wait("x_tr", S.get("x_tr", 0))
                    v.tensor_copy(kbr[:, 0:16], ps_u[:, 256:272])
                    v.nop().inc("x_mx", 1)

                    def stage1(o):
                        gt = own_tile_global(vv, o)
                        pb = o % 2
                        xq_i = xq_t[:, pb * 1024:(pb + 1) * 1024]
                        sc.wait("q_mm", max(0, o - 1))
                        sc.dma_start(out=xq_i.rearrange("p (c t) -> p c t", c=8),
                                     in_=xnT_all.rearrange("c p t -> p c t")[:, :, gt * 128:(gt + 1) * 128]).inc("q_ld", 16)
                        sc.dma_start(out=xnT_own[:, :, o * 128:(o + 1) * 128],
                                     in_=xnT_all[:, :, gt * 128:(gt + 1) * 128]).inc("q_ld", 16)
                        t.wait("q_ld")
                        t.wait("q_ev", max(0, o - 1))
                        s0c = ps_s0[:, pb * 512:pb * 512 + 480]
                        for dc in range(8):
                            t.matmul(s0c, xq_i[:, dc * 128:(dc + 1) * 128],
                                     wabq_b[:, dc * 480:(dc + 1) * 480],
                                     start=(dc == 0), stop=(dc == 7))
                        t.mark("q_mm", 1)
                        sc.wait("q_mm")
                        sc.wait("q_prod", max(0, o - 1))
                        apq = s0c.rearrange("p (r c) -> p r c", r=6)
                        sc.activation(aq[:, pb * 96:(pb + 1) * 96].rearrange(
                                          "p (r h) -> p r h", r=6),
                                      apq[:, :, 0:16], AF.Copy, scale=1.0 / RQ,
                                      bias=0.0)
                        sc.activation(bq[:, pb * 384:(pb + 1) * 384].rearrange(
                                          "p (r c) -> p r c", r=6),
                                      apq[:, :, 16:80], AF.Copy, bias=0.0).inc("q_ev", 1)

                    qacc = Pt[:, 4096:5120]
                    qacc4 = qacc.rearrange("p (h c) -> p h c", h=16)

                    def stage2(o):
                        gt = own_tile_global(vv, o)
                        pb = o % 2
                        qt_ps = bfv(ps_s1 if pb == 0 else ps_av, 0)
                        ptbq = ptb[:, pb * 1024:(pb + 1) * 1024]
                        v.wait("q_ev", o + 1)
                        for r in range(6):
                            pr = r % 2
                            a_b = aq[:, pb * 96 + r * 16:pb * 96 + (r + 1) * 16].rearrange(
                                "p (o h) -> p h o", o=1).broadcast_to([128, 16, 64])
                            b_b = bq[:, pb * 384 + r * 64:pb * 384 + (r + 1) * 64].rearrange(
                                "p (o c) -> p o c", o=1).broadcast_to([128, 16, 64])
                            if r == 0:
                                v.tensor_tensor(qacc4, a_b, b_b, ALU.mult)
                            else:
                                v.tensor_tensor(
                                    Pt[:, pr * 1024:(pr + 1) * 1024].rearrange(
                                        "p (h c) -> p h c", h=16),
                                    a_b, b_b, ALU.mult)
                                if r < 5:
                                    v.scalar_tensor_tensor(
                                        qacc, Pt[:, pr * 1024:(pr + 1) * 1024],
                                        1.0, qacc, ALU.mult, ALU.add)
                                else:
                                    v.wait("q_tr", max(0, o - 1))
                                    v.scalar_tensor_tensor(
                                        ptbq, Pt[:, pr * 1024:(pr + 1) * 1024],
                                        1.0, qacc, ALU.mult, ALU.add)
                        v.nop().inc("q_prod", 1)
                        t.wait("q_prod", o + 1)
                        t.wait("q_te", max(0, o - 1))
                        for fc in range(8):
                            t.matmul(qt_ps[:, fc * 128:(fc + 1) * 128],
                                     ptbq[:, fc * 128:(fc + 1) * 128],
                                     identb[:], is_transpose=True,
                                     start=True, stop=True)
                        t.mark("q_tr", 1)
                        qT_i = qT_sb[:, pb * 1024:(pb + 1) * 1024]
                        sc.wait("q_tr", o + 1)
                        sc.wait("q_st", 16 * o)
                        sc.activation(qT_i, qt_ps[:, 0:1024], AF.Copy,
                                      bias=0.0).inc("q_te", 1)
                        sc.wait("q_te", o + 1)
                        sc.dma_start(out=qT_d.rearrange("c p t -> p c t")[:, :, o * 128:(o + 1) * 128],
                                     in_=qT_i.rearrange("p (c t) -> p c t", c=8)).inc("q_st", 16)
                        v.wait("x_ls", S.get("x_ls", 0))
                        v.scalar_tensor_tensor(kq[:], ptbq, 1.0, ptbq,
                                               ALU.mult, ALU.mult)
                        v.tensor_reduce(qsq[:, 0:16],
                                        kq.rearrange("p (h c) -> p h c", h=16),
                                        axis=AX, op=ALU.add)
                        v.tensor_tensor(mu2[:, 0:16], qsq[:, 0:16],
                                        kbr[:, 0:16], ALU.mult)
                        v.nop().inc("x_mx", 1)
                        sc.wait("x_mx", S.get("x_mx", 0))
                        sc.activation(qsq[:, 0:16], mu2[:, 0:16], AF.Ln,
                                      bias=emx[:, 49:50])
                        sc.activation(muv[:, 0:16], qsq[:, 0:16], AF.Exp,
                                      scale=0.5, bias=emx[:, 54:55]).inc("x_ls", 1)
                        t.wait("x_ls", S.get("x_ls", 0))
                        t.wait("x_mx", S.get("x_mx", 0))
                        t.matmul(ps_u[0:16, 0:128], muv[:, 0:16], identf[:],
                                 is_transpose=True, start=True, stop=True)
                        t.mark("x_tr", 1)
                        sc.wait("x_tr", S.get("x_tr", 0))
                        sc.wait("x_st", S.get("x_st", 0))
                        sc.activation(muT_sb[0:16, :], ps_u[0:16, 0:128],
                                      AF.Copy, bias=0.0).inc("x_ls", 1)
                        sc.dma_start(out=muT_d[:, o * 128:(o + 1) * 128],
                                     in_=muT_sb[0:16, :]).inc("x_st", 16)

                    for o in range(NOT_ + 1):
                        if o < NOT_:
                            stage1(o)
                        if o >= 1:
                            stage2(o - 1)

                ctx.variant_region(("vector", "scalar", "tensor"), q_prep)

            def run_bias(ctx: Ctx):
                sp, v, sc, t, g = ctx.sp, ctx.v, ctx.sc, ctx.t, ctx.g
                if debug:
                    sp.wait("ngst", ctx.sems.get("ngst", 0))
                    sp.wait("ld", ctx.sems.get("ld", 0))
                    sp.dma_start(out=dbg["xnT"][:], in_=xnT_all[:]).inc("ld", 16)
                    sp.dma_start(out=dbg["kvk"][:], in_=kv_k[:]).inc("ld", 16)
                    sp.dma_start(out=dbg["kvv"][:], in_=kv_v[:]).inc("ld", 16)
                    sp.dma_start(out=dbg["qT"][:], in_=qT_d[:]).inc("ld", 16)

            def run_placeholder_out(ctx: Ctx):
                sp, v, sc, t, g = ctx.sp, ctx.v, ctx.sc, ctx.t, ctx.g
                S = ctx.sems

                def drain_dma():
                    for s_ in sorted(DMA_SEMS):
                        sp.wait(s_)
                v.memset(outsb[:, 0:1024], 0.0).inc("f_res", 1)
                sp.wait("f_res")
                for ti in range(NOT_):
                    sp.dma_start(out=y_out[ti * 128:(ti + 1) * 128, :],
                                 in_=outsb[:, 0:1024]).inc("f_ost", 16)
                sp.wait("f_ost")
                drain_dma()
                v.wait("f_ost")
                t.wait("f_ost")
                sc.wait("f_ost")
                g.wait("f_ost")

            def run_attention(ctx: Ctx):
                sp, v, sc, t, g = ctx.sp, ctx.v, ctx.sc, ctx.t, ctx.g
                S = ctx.sems
                def attention(vv):
                    nblk = [stripe_nblk(vv, s_) for s_ in range(NST)]
                    CC = [(n + 7) // 8 for n in nblk]
                    gts = {(s_, qt): own_tile_global(vv, 4 * s_ + qt)
                           for s_ in range(NST) for qt in range(4)}
                    avcnt = [0]
                    ucnt = [0]
                    pend_av = []
                    last_expS = {0: 0, 1: 0}
                    last_avpT = {0: 0, 1: 0}
                    kbusy = {}
                    vbusy = {}
                    akv_end = {}
                    qk_end = {}

                    def kvload(ci, nbig, with_v, nv=0):
                        blocks = min(8, nbig - 8 * ci)
                        slot = ci % 2
                        sem = "a_kv%d" % slot
                        sp.wait(sem, S.get(sem, 0))
                        if slot in kbusy:
                            sp.wait("a_qk", kbusy[slot])
                        g0 = ci * 1024
                        sp.dma_start(
                            out=kr_k[:, slot * 16384:(slot + 1) * 16384].rearrange(
                                "r (h t) -> r h t", h=16)[:, :, 0:blocks * 128],
                            in_=kv_k.rearrange("h r t -> r h t")[:, :, g0:g0 + blocks * 128]
                        ).inc(sem, 16)
                        if with_v:
                            bv = min(8, nv - 8 * ci)
                            if slot in vbusy:
                                sp.wait("a_av", vbusy[slot])
                            sp.dma_start(
                                out=kr_v[:, slot * 8320:slot * 8320 + bv * 1040].rearrange(
                                    "p (b c) -> p b c", b=bv),
                                in_=kv_v[g0:g0 + bv * 128, :].rearrange(
                                    "(b p) c -> p b c", p=128)).inc(sem, 16)
                        akv_end[ci] = (sem, S.get(sem, 0))

                    def load_qext_rows(sa):
                        sp.wait("a_qt", S.get("a_qt", 0))
                        if sa > 1:
                            sp.wait("a_qk", qk_end[sa - 2])
                        par = sa % 2
                        for h in range(16):
                            sp.dma_start(
                                out=qT_ext[0:64, par * 8192 + h * 512:
                                           par * 8192 + (h + 1) * 512],
                                in_=qT_d[h // 2][64 * (h % 2):64 * (h % 2) + 64,
                                                 sa * 512:(sa + 1) * 512]).inc("a_qt", 16)
                        sp.wait("x_st", S.get("x_st", 0))
                        sp.dma_start(
                            out=qT_ext[64:65, par * 8192:(par + 1) * 8192].rearrange(
                                "o (h t) -> o h t", h=16),
                            in_=muT_d[:, sa * 512:(sa + 1) * 512].rearrange(
                                "(o h) t -> o h t", o=1)).inc("x_st", 16)

                    def emit_av(pa):
                        (s_, ci, h, bp, blocks, slot, up, expn, mskn) = pa
                        first = (bp == 0)
                        avs = avcnt[0] % 2
                        pT = pT0 if up == 0 else pT1
                        t.wait("a_exp", expn)
                        if mskn:
                            t.wait("a_msk", mskn)
                        if first:
                            t.wait("a_evac", max(0, avcnt[0] - 1))
                            if ci == 0 and h == 0:
                                t.wait("a_fv", S.get("a_fv", 0))
                        b0 = 2 * bp
                        for j in (0, 1):
                            vco = slot * 8320 + (b0 + j) * 1040 + h * 65
                            t.matmul(ps_av[0:65, avs * 512:(avs + 1) * 512],
                                     kr_v[:, vco:vco + 65],
                                     pT[:, j * 512:(j + 1) * 512],
                                     start=(first and j == 0),
                                     stop=(bp == blocks // 2 - 1 and j == 1))
                        t.mark("a_av", 1)
                        last_avpT[up] = S["a_av"]
                        if bp == blocks // 2 - 1:
                            v.wait("a_av")
                            ya = y_acc[:, h * 512:(h + 1) * 512]
                            if ci == 0:
                                if debug and h == 0:
                                    v.wait("dbst", S.get("dbst", 0))
                                v.tensor_copy(ya, ps_av[0:65, avs * 512:(avs + 1) * 512])
                            else:
                                v.scalar_tensor_tensor(
                                    ya, ps_av[0:65, avs * 512:(avs + 1) * 512],
                                    1.0, ya, ALU.mult, ALU.add)
                            v.nop().inc("a_evac", 1)
                            avcnt[0] += 1

                    def flush_av(all_=True):
                        while pend_av and (all_ or len(pend_av) > 1):
                            emit_av(pend_av.pop(0))

                    def main_chunk(s_, ci):
                        blocks = min(8, nblk[s_] - 8 * ci)
                        slot = ci % 2
                        N = nblk[s_]
                        for h in range(16):
                            for bp in range(blocks // 2):
                                up = ucnt[0] % 2
                                ucnt[0] += 1
                                Sps = ps_s0 if up == 0 else ps_s1
                                pT = pT0 if up == 0 else pT1
                                b0 = 2 * bp
                                t.wait(*akv_end[ci])
                                if ci == 0 and h == 0 and bp == 0:
                                    t.wait("a_qt", S.get("a_qt", 0))
                                    t.wait("x_st", S.get("x_st", 0))
                                if ci == 0 and h == 0 and bp <= 1:
                                    t.wait("a_fe", S.get("a_fe", 0))
                                t.wait("a_exp", last_expS[up])
                                qpar = (s_ % 2) * 8192
                                for j in (0, 1):
                                    co = slot * 16384 + h * 1024 + (b0 + j) * 128
                                    t.matmul(Sps[:, j * 512:(j + 1) * 512],
                                             kr_k[0:65, co:co + 128],
                                             qT_ext[0:65, qpar + h * 512:
                                                    qpar + (h + 1) * 512],
                                             start=True, stop=True)
                                t.mark("a_qk", 1)
                                flush_av(all_=False)
                                sc.wait("a_qk")
                                sc.wait("a_av", last_avpT[up])
                                sc.activation(pT[:], Sps[:], AF.Exp,
                                              bias=emx[:, 53:54],
                                              scale=0.125).inc("a_exp", 1)
                                last_expS[up] = S["a_exp"]
                                ndiag = 0
                                for j in (0, 1):
                                    blk = ci * 8 + b0 + j
                                    dj = blk - (N - 4)
                                    if dj >= 0:
                                        g.wait("a_exp")
                                        g.tensor_tensor(pT[:, j * 512:(j + 1) * 512],
                                                        pT[:, j * 512:(j + 1) * 512],
                                                        dmask[:, dj * 512:(dj + 1) * 512],
                                                        ALU.mult)
                                        ndiag += 1
                                if ndiag:
                                    g.nop().inc("a_msk", 1)
                                pend_av.append(
                                    (s_, ci, h, bp, blocks, slot, up,
                                     S["a_exp"], S.get("a_msk", 0) if ndiag else 0))
                                yield

                    def y_finalize(s_):
                        if debug:
                            sc.wait("dbst", S.get("dbst", 0))
                            sc.wait("a_evac", S.get("a_evac", 0))
                            sc.dma_start(out=dbg["yacc"][s_], in_=y_acc[:]).inc("dbst", 16)
                        for h in range(16):
                            ya = y_acc[:, h * 512:(h + 1) * 512]
                            t.wait("a_evac")
                            t.wait("a_fv", S.get("a_fv", 0))
                            for q4 in range(4):
                                t.matmul(ps_av[:, q4 * 128:q4 * 128 + 65],
                                         ya[:, q4 * 128:(q4 + 1) * 128], identf[0:65, 0:65],
                                         is_transpose=True, start=True, stop=True)
                            t.mark("a_ftr", 1)
                            v.wait("a_ftr")
                            for q4 in range(4):
                                pu = ps_av[:, q4 * 128:q4 * 128 + 65]
                                v.reciprocal(recipS[:, q4:q4 + 1], pu[:, 64:65])
                                v.tensor_scalar(ytokn[:, q4 * 64:(q4 + 1) * 64],
                                                pu[:, 0:64], recipS[:, q4:q4 + 1],
                                                None, ALU.mult)
                            v.nop().inc("a_fv", 1)
                            t.wait("a_fv")
                            t.wait("a_fe", S.get("a_fe", 0))
                            ynps = bfv(ps_s1, 1)
                            for q4 in range(4):
                                t.matmul(ynps[0:64, q4 * 128:(q4 + 1) * 128],
                                         ytokn[:, q4 * 64:(q4 + 1) * 64], identb[:],
                                         is_transpose=True, start=True, stop=True)
                            t.mark("a_ftr2", 1)
                            v.wait("a_ftr2")
                            v.wait("a_ynst", max(0, S.get("a_ynst", 0)))
                            v.tensor_copy(ynT_sb[:], ynps[0:64, 0:512]).inc("a_fe", 1)
                            sp.wait("a_fe")
                            sp.wait("a_ynst", S.get("a_ynst", 0))
                            sp.dma_start(out=ynT_d[h // 2][64 * (h % 2):64 * (h % 2) + 64,
                                                           s_ * 512:(s_ + 1) * 512],
                                         in_=ynT_sb[:]).inc("a_ynst", 16)

                    sp.wait("ld", S.get("ld", 0))
                    sp.wait("ngst", S.get("ngst", 0))
                    load_qext_rows(0)
                    kvload(0, nblk[0], True, nblk[0])
                    for s_ in range(NST):
                        for ci in range(CC[s_]):
                            if ci + 1 < CC[s_]:
                                kvload(ci + 1, nblk[s_], True, nblk[s_])
                            for _ in main_chunk(s_, ci):
                                pass
                            flush_av()
                            kbusy[ci % 2] = S.get("a_qk", 0)
                            vbusy[ci % 2] = S.get("a_av", 0)
                        qk_end[s_] = S.get("a_qk", 0)
                        if debug:
                            sc.wait("dbst", S.get("dbst", 0))
                            sc.dma_start(out=dbg["mu"][s_],
                                         in_=qT_ext[64:65, (s_ % 2) * 8192:
                                                    (s_ % 2 + 1) * 8192]).inc("dbst", 16)
                        if s_ < 3:
                            load_qext_rows(s_ + 1)
                            kvload(0, nblk[s_ + 1], True, nblk[s_ + 1])
                        y_finalize(s_)

                ctx.variant_region(("sync", "vector", "scalar", "tensor",
                                    "gpsimd"), attention)

            def run_postattn(ctx: Ctx):
                sp, v, sc, t, g = ctx.sp, ctx.v, ctx.sc, ctx.t, ctx.g
                if debug:
                    sp.dma_start(out=dbg["ynT"][:], in_=ynT_d[:]).inc("ld", 16)

            def run_ffn(ctx: Ctx):
                sp, v, sc, t, g = ctx.sp, ctx.v, ctx.sc, ctx.t, ctx.g
                S = ctx.sems

                def drain_dma():
                    for s_ in sorted(DMA_SEMS):
                        sp.wait(s_)
                def wld(out, in_, **kw):
                    g.wait("f_wld", S.get("f_wld", 0))
                    g.dma_start(out=out, in_=in_, **kw).inc("f_wld", 16)
                wld(projb.rearrange("p (c f) -> p c f", c=8),
                    w_proj.rearrange("(c p) f -> p c f", p=128))
                wld(c1b[:], w_c1[:])
                wld(c2b[:], w_c2[:])
                wld(w1b.rearrange("p (c f) -> p c f", c=8),
                    w_in1.rearrange("(c p) f -> p c f", p=128))
                wld(wo1b.rearrange("p (c f) -> p c f", c=4),
                    w_out1T.rearrange("(c p) f -> p c f", p=128))
                wld(w2b.rearrange("p (c f) -> p c f", c=32),
                    w_in2.rearrange("(c p) f -> p c f", p=128))
                wld(wo2b.rearrange("p (c f) -> p c f", c=4),
                    w_out2T.rearrange("(c p) f -> p c f", p=128))
                wld(woTb.rearrange("p (c f) -> p c f", c=8),
                    w_oT.rearrange("(c p) f -> p c f", p=128))
                wld(b1c.rearrange("p c -> p c"),
                    w_b1.rearrange("(c p) o -> p (c o)", p=128),
                    allow_slow_non_contiguous=True)
                wld(b2c.rearrange("p c -> p c"),
                    w_b2.rearrange("(c p) o -> p (c o)", p=128),
                    allow_slow_non_contiguous=True)

                inv_k = emx[:, 0:16]
                kvec = emx[:, 16:32]
                iota16 = emx[:, 32:48]

                for ti in range(4):
                    sc.wait("f_act", S.get("f_act", 0))
                    sc.wait("f_xld", S.get("f_xld", 0))
                    sc.wait("f_mm", S.get("f_mm", 0))
                    sc.dma_start(out=fxn.rearrange("p (c f) -> p c f", c=8),
                                 in_=xnT_own.rearrange("c p t -> p c t")[:, :, ti * 512:(ti + 1) * 512]).inc("f_xld", 16)
                    t.wait("f_xld", 16 * (ti + 1))
                    if ti == 0:
                        t.wait("f_wld", 16 * 3)
                    t.wait("f_ve", S.get("f_ve", 0))
                    for dc in range(8):
                        t.matmul(ps_u[0:16, 0:512], projb[:, dc * 16:(dc + 1) * 16],
                                 fxn[:, dc * 512:(dc + 1) * 512],
                                 start=(dc == 0), stop=(dc == 7))
                    t.mark("f_mm", 1)
                    v.wait("f_mm")
                    v.tensor_copy(zproj[:], ps_u[0:16, 0:512]).inc("f_ve", 1)
                    t.wait("f_ve")
                    for q4 in range(4):
                        t.matmul(ps_s0[:, q4 * 128:q4 * 128 + 16],
                                 zproj[0:16, q4 * 128:(q4 + 1) * 128], identf[0:16, 0:16],
                                 is_transpose=True, start=True, stop=True)
                    t.mark("f_mm", 1)
                    v.wait("f_mm")
                    for q4 in range(4):
                        v.tensor_copy(ztok[:, q4 * 16:(q4 + 1) * 16],
                                      ps_s0[:, q4 * 128:q4 * 128 + 16])
                    v.nop().inc("f_ve", 1)
                    z4 = ztok.rearrange("p (g e) -> p g e", g=4)
                    v.tensor_reduce(ecols[:, 0:4], z4, axis=AX, op=ALU.add)
                    v.tensor_scalar(ecols[:, 0:4], ecols[:, 0:4], 1.0 / 16, None, ALU.mult)
                    mb = ecols[:, 0:4].rearrange("p (g e) -> p g e", e=1).broadcast_to([128, 4, 16])
                    ez4 = ezs.rearrange("p (g e) -> p g e", g=4)
                    v.tensor_tensor(ez4, z4, mb, ALU.subtract)
                    t4_ = etmp.rearrange("p (g e) -> p g e", g=4)
                    v.tensor_tensor(t4_, ez4, ez4, ALU.mult)
                    v.tensor_reduce(ecols[:, 4:8], t4_, axis=AX, op=ALU.add)
                    v.tensor_scalar(ecols[:, 4:8], ecols[:, 4:8], 1.0 / 16, None,
                                    ALU.mult).inc("f_ve", 1)
                    sc.wait("f_ve")
                    sc.activation(ecols[:, 8:12], ecols[:, 4:8], AF.Ln, bias=emx[:, 48:49])
                    sc.activation(ecols[:, 12:16], ecols[:, 8:12], AF.Exp,
                                  scale=-0.5).inc("f_act", 1)
                    v.wait("f_act")
                    rb = ecols[:, 12:16].rearrange("p (g e) -> p g e", e=1).broadcast_to([128, 4, 16])
                    v.tensor_tensor(ez4, ez4, rb, ALU.mult)
                    v.tensor_scalar(ezs[:], ezs[:], 0.5, None, ALU.mult)
                    v.tensor_reduce(ecols[:, 16:20], ez4, axis=AX, op=ALU.max)
                    xb_ = ecols[:, 16:20].rearrange("p (g e) -> p g e", e=1).broadcast_to([128, 4, 16])
                    v.tensor_tensor(ez4, ez4, xb_, ALU.subtract)
                    v.tensor_copy(ez3[:], ezs[:])
                    for rnd in range(16):
                        off = rnd % 2
                        n = (16 - off) // 2
                        src = ezs.rearrange("p (g e) -> p g e", g=4)[:, :, off:off + 2 * n]
                        pa = src.rearrange("p g (e2 two) -> p g e2 two", two=2)[:, :, :, 0:1]
                        pb_ = src.rearrange("p g (e2 two) -> p g e2 two", two=2)[:, :, :, 1:2]
                        ta = etmp.rearrange("p (g e) -> p g e", g=4)[:, :, 0:n].rearrange(
                            "p g (e2 o) -> p g e2 o", o=1)
                        v.tensor_tensor(ta, pa, pb_, ALU.max)
                        v.tensor_tensor(pb_, pa, pb_, ALU.min)
                        v.tensor_copy(pa, ta)
                    cs4 = ecs.rearrange("p (g e) -> p g e", g=4)
                    cq4 = ecs2.rearrange("p (g e) -> p g e", g=4)
                    v.tensor_tensor(cq4, ez4, ez4, ALU.mult)
                    v.tensor_copy(ecs[:], ezs[:])
                    for d_ in (1, 2, 4, 8):
                        for buf in (ecs, ecs2):
                            c4 = buf.rearrange("p (g e) -> p g e", g=4)
                            tt4 = etmp2.rearrange("p (g e) -> p g e", g=4)
                            v.tensor_copy(tt4[:, :, 0:16 - d_], c4[:, :, 0:16 - d_])
                            v.tensor_tensor(c4[:, :, d_:16], c4[:, :, d_:16],
                                            tt4[:, :, 0:16 - d_], ALU.add)
                    ik4 = inv_k.rearrange("p (o e) -> p o e", o=1).broadcast_to([128, 4, 16])
                    kv4 = kvec.rearrange("p (o e) -> p o e", o=1).broadcast_to([128, 4, 16])
                    v.tensor_tensor(cs4, cs4, ik4, ALU.mult)
                    v.tensor_tensor(cq4, cq4, ik4, ALU.mult)
                    v.tensor_tensor(t4_, cs4, cs4, ALU.mult)
                    v.tensor_tensor(t4_, cq4, t4_, ALU.subtract)
                    v.tensor_tensor(t4_, t4_, kv4, ALU.mult)
                    v.tensor_scalar(etmp[:], etmp[:], -1.0, 1.0, ALU.mult, ALU.add)
                    v.tensor_tensor(t4_, t4_, ik4, ALU.mult)
                    v.tensor_scalar(etmp[:], etmp[:], 0.0, None, ALU.max).inc("f_ve", 1)
                    sc.wait("f_ve")
                    sc.activation(etmp2[:], etmp[:], AF.Ln, bias=emx[:, 50:51])
                    sc.activation(etmp2[:], etmp2[:], AF.Exp, scale=0.5).inc("f_act", 1)
                    v.wait("f_act")
                    t24 = etmp2.rearrange("p (g e) -> p g e", g=4)
                    v.tensor_tensor(t24, cs4, t24, ALU.subtract)
                    s4_ = cs4
                    v.tensor_tensor(s4_, t24, ez4, ALU.is_le)
                    v.tensor_reduce(ecols[:, 20:24], s4_, axis=AX, op=ALU.add)
                    v.tensor_scalar(ecols[:, 20:24], ecols[:, 20:24], -1.0, None, ALU.add)
                    io4 = iota16.rearrange("p (o e) -> p o e", o=1).broadcast_to([128, 4, 16])
                    sbx = ecols[:, 20:24].rearrange("p (g e) -> p g e", e=1).broadcast_to([128, 4, 16])
                    v.tensor_tensor(s4_, io4, sbx, ALU.is_equal)
                    v.tensor_tensor(s4_, s4_, t24, ALU.mult)
                    v.tensor_reduce(ecols[:, 16:20], s4_, axis=AX, op=ALU.add)
                    tb_ = ecols[:, 16:20].rearrange("p (g e) -> p g e", e=1).broadcast_to([128, 4, 16])
                    e34 = ez3.rearrange("p (g e) -> p g e", g=4)
                    v.tensor_tensor(e34, e34, tb_, ALU.subtract)
                    v.tensor_scalar(ez3[:], ez3[:], 0.0, None, ALU.max)
                    v.tensor_tensor(e34, e34, e34, ALU.mult)
                    v.tensor_copy(ab16[:], ez3[:]).inc("f_ve", 1)
                    t.wait("f_ve")
                    aps_ = bfv(ps_u, 0)
                    for q4 in range(4):
                        t.matmul(aps_[0:16, q4 * 128:(q4 + 1) * 128],
                                 ab16[:, q4 * 16:(q4 + 1) * 16], identb[:],
                                 is_transpose=True, start=True, stop=True)
                    t.mark("f_mm", 1)
                    v.wait("f_mm")
                    v.tensor_copy(aT_all[0:16, ti * 512:(ti + 1) * 512],
                                  aps_[0:16, 0:512]).inc("f_ve", 1)

                for ti in range(4):
                    sp.wait("f_ost", max(0, S.get("f_ost", 0) - 16))
                    sp.wait("f_xt", S.get("f_xt", 0))
                    sp.wait("f_yld", S.get("f_yld", 0))
                    if ti == 0:
                        sp.wait("f_xld", S.get("f_xld", 0))
                    sp.wait("f_mm", S.get("f_mm", 0))
                    sp.dma_start(out=fxn.rearrange("p (c f) -> p c f", c=8),
                                 in_=xnT_own.rearrange("c p t -> p c t")[:, :, ti * 512:(ti + 1) * 512]).inc("f_xt", 16)
                    sp.dma_start(out=fyn.rearrange("p (c f) -> p c f", c=8),
                                 in_=ynT_d.rearrange("c p t -> p c t")[:, :, ti * 512:(ti + 1) * 512]).inc("f_yld", 16)
                    t.wait("f_wld", 16 * 10)
                    if ti == 0:
                        v.wait("f_wld", 16 * 10)
                    for rc in range(4):
                        t.wait("f_ve", S.get("f_ve", 0))
                        t.matmul(ps_s0[:, 0:512], c1b[0:16, rc * 128:(rc + 1) * 128],
                                 aT_all[0:16, ti * 512:(ti + 1) * 512], start=True, stop=True)
                        t.mark("f_mm", 1)
                        v.wait("f_mm")
                        v.tensor_copy(g1T[:, rc * 512:(rc + 1) * 512], ps_s0[:, 0:512]).inc("f_ve", 1)
                        t.wait("f_ve")
                        t.matmul(ps_s0[:, 0:512], c2b[0:16, rc * 128:(rc + 1) * 128],
                                 aT_all[0:16, ti * 512:(ti + 1) * 512], start=True, stop=True)
                        t.mark("f_mm", 1)
                        v.wait("f_mm")
                        v.tensor_copy(g2T[:, rc * 512:(rc + 1) * 512], ps_s0[:, 0:512]).inc("f_ve", 1)
                    t.wait("f_xt", S.get("f_xt", 0))
                    for rc in range(4):
                        t.wait("f_ve", S.get("f_ve", 0))
                        for dc in range(8):
                            t.matmul(ps_s1[:, 0:512],
                                     w1b[:, dc * 512 + rc * 128: dc * 512 + (rc + 1) * 128],
                                     fxn[:, dc * 512:(dc + 1) * 512],
                                     start=(dc == 0), stop=(dc == 7))
                        t.mark("f_mm", 1)
                        v.wait("f_mm")
                        v.scalar_tensor_tensor(z1b[:, rc * 512:(rc + 1) * 512],
                                               ps_s1[:, 0:512], b1c[:, rc:rc + 1],
                                               g1T[:, rc * 512:(rc + 1) * 512],
                                               ALU.add, ALU.mult).inc("f_ve", 1)
                    for oc in range(32):
                        pp = oc % 2
                        hps = ps_s0 if pp == 0 else ps_s1
                        t.wait("f_ve", S.get("f_ve", 0))
                        t.wait("f_gel", max(0, S.get("f_gel", 0) - 1))
                        for rc in range(4):
                            t.matmul(hps[:, 0:512],
                                     wo1b[:, rc * 4096 + oc * 128: rc * 4096 + (oc + 1) * 128],
                                     z1b[:, rc * 512:(rc + 1) * 512],
                                     start=(rc == 0), stop=(rc == 3))
                        t.mark("f_mm", 1)
                        sc.wait("f_mm")
                        sc.activation(hb[:, oc * 512:(oc + 1) * 512], hps[:, 0:512],
                                      AF.Gelu).inc("f_gel", 1)
                    for rc in range(4):
                        sc.wait("f_gel", S.get("f_gel", 0))
                        t.wait("f_gel", S.get("f_gel", 0))
                        t.wait("f_ve", S.get("f_ve", 0))
                        for oc in range(32):
                            t.matmul(ps_s1[:, 0:512],
                                     w2b[:, oc * 512 + rc * 128: oc * 512 + (rc + 1) * 128],
                                     hb[:, oc * 512:(oc + 1) * 512],
                                     start=(oc == 0), stop=(oc == 31))
                        t.mark("f_mm", 1)
                        v.wait("f_mm")
                        v.scalar_tensor_tensor(z2b[:, rc * 512:(rc + 1) * 512],
                                               ps_s1[:, 0:512], b2c[:, rc:rc + 1],
                                               g2T[:, rc * 512:(rc + 1) * 512],
                                               ALU.add, ALU.mult).inc("f_ve", 1)
                    t.wait("f_yld", S.get("f_yld", 0))
                    v.wait("f_yld", S.get("f_yld", 0))
                    for tt in range(4):
                        ob = outsb[:, (tt % 2) * 1024:(tt % 2) * 1024 + 1024]
                        sp.wait("f_res", S.get("f_res", 0))
                        sp.wait("f_xold", S.get("f_xold", 0))
                        sp.dma_start(out=fxo[:],
                                     in_=x_own[(ti * 4 + tt) * 128:(ti * 4 + tt + 1) * 128, :]).inc("f_xold", 16)
                        for half in range(2):
                            pcmb = ps_av[:, half * 512:(half + 1) * 512]
                            t.wait("f_ve", S.get("f_ve", 0))
                            t.wait("f_res", max(0, S.get("f_res", 0) - 1))
                            for dc in range(8):
                                t.matmul(pcmb,
                                         fyn[:, dc * 512 + tt * 128: dc * 512 + tt * 128 + 128],
                                         woTb[:, dc * 1024 + half * 512: dc * 1024 + (half + 1) * 512],
                                         start=(dc == 0), stop=False)
                            for rc in range(4):
                                t.matmul(pcmb,
                                         z2b[:, rc * 512 + tt * 128: rc * 512 + tt * 128 + 128],
                                         wo2b[:, rc * 1024 + half * 512: rc * 1024 + (half + 1) * 512],
                                         start=False, stop=(rc == 3))
                            t.mark("f_mm", 1)
                            v.wait("f_mm")
                            v.wait("f_xold")
                            v.wait("f_ost", max(0, S.get("f_ost", 0) - 16))
                            v.tensor_tensor(ob[:, half * 512:(half + 1) * 512], pcmb,
                                            fxo[:, half * 512:(half + 1) * 512],
                                            ALU.add).inc("f_res", 1)
                        sp.wait("f_res")
                        sp.wait("f_ost", S.get("f_ost", 0))
                        sp.dma_start(out=y_out[(ti * 4 + tt) * 128:(ti * 4 + tt + 1) * 128, :],
                                     in_=ob).inc("f_ost", 16)
                sp.wait("f_ost")
                drain_dma()
                v.wait("f_ost")
                t.wait("f_ost")
                sc.wait("f_ost")
                g.wait("f_ost")

            ctxs = {e: Ctx(nc, semh, e, SEM_OWNER) for e in ENGS}
            nsec = 8 if phase < 2 else 11
            for sec_i in range(nsec):
                with nc.Block(f"sec{sec_i}") as blk:
                    for eng in ENGS:
                        def mk(engname, s):
                            def f(_eng):
                                program(ctxs[engname], s)
                            return f
                        getattr(blk, eng)(mk(eng, sec_i))
                nc.all_engine_barrier()

    return nc


def _consts():
    identb = np.eye(128, dtype=BF)
    identf = np.eye(128, dtype=np.float32)
    inv_freq = (1.0 / ROPE_BASE) ** (np.arange(0, DH, 2, dtype=np.float32) / DH)
    tpos = np.arange(T, dtype=np.float32)
    freqs = np.outer(tpos, inv_freq)
    ropeC = np.cos(freqs).astype(np.float32).reshape(NT, 128, 32)
    ropeS = np.sin(freqs).astype(np.float32).reshape(NT, 128, 32)
    qi = np.arange(512)[None, :]
    ki = np.arange(128)[:, None]
    dmaskv = np.stack([(qi >= ki + 128 * j) for j in range(4)]).astype(BF)
    emx = np.zeros((128, 56), dtype=np.float32)
    emx[:, 0:16] = 1.0 / np.arange(1, 17)[None, :]
    emx[:, 16:32] = np.arange(1, 17)[None, :]
    emx[:, 32:48] = np.arange(16)[None, :]
    emx[:, 48] = 1e-5
    emx[:, 49] = 1e-12
    emx[:, 50] = 1e-30
    emx[:, 53] = 0.0
    emx[:, 54] = np.log(0.75)
    auxmb = np.zeros((128, 512), dtype=np.float32)
    cc = np.arange(128)[None, :]
    pp = np.arange(128)[:, None]
    auxmb[:, 384:512] = np.where(cc <= pp, 0.0, -1e9)
    return dict(c_identb=identb, c_identf=identf, c_ropeC=ropeC, c_ropeS=ropeS,
                c_dmask=dmaskv, c_emx=emx, c_auxmb=auxmb)


def _stage_weights(ins):
    f = np.float32
    return dict(
        wabqT=np.ascontiguousarray(np.asarray(ins["L0_attn_W_abq"], f).T),
        wabkvT=np.ascontiguousarray(np.asarray(ins["L0_attn_W_abkv"], f).T),
        w_proj=np.ascontiguousarray(np.asarray(ins["L0_ffn_proj"], f)),
        w_c1=np.ascontiguousarray(np.asarray(ins["L0_ffn_C1"], f)),
        w_c2=np.ascontiguousarray(np.asarray(ins["L0_ffn_C2"], f)),
        w_in1=np.ascontiguousarray(np.asarray(ins["L0_ffn_Win1"], f)[0:D, :]),
        w_b1=np.ascontiguousarray(np.asarray(ins["L0_ffn_Win1"], f)[D:D + 1, :].T),
        w_out1T=np.ascontiguousarray(np.asarray(ins["L0_ffn_Wout1"], f).T),
        w_in2=np.ascontiguousarray(np.asarray(ins["L0_ffn_Win2"], f)[0:OH, :]),
        w_b2=np.ascontiguousarray(np.asarray(ins["L0_ffn_Win2"], f)[OH:OH + 1, :].T),
        w_out2T=np.ascontiguousarray(np.asarray(ins["L0_ffn_Wout2"], f).T),
        w_oT=np.ascontiguousarray(np.asarray(ins["L0_attn_W_o"], f).T),
    )


def make_in_maps(inputs):
    x = np.asarray(inputs["L0_attn_x"], np.float32)
    weights = _stage_weights(inputs)
    consts = _consts()
    in_maps = []
    for c in range(8):
        b, v = c // 4, c % 4
        xb = np.ascontiguousarray(x[b])
        xo = np.ascontiguousarray(
            np.concatenate([xb[1024 * v:1024 * (v + 1)],
                            xb[1024 * (7 - v):1024 * (8 - v)]], axis=0))
        m = {"x_full": xb, "x_own": xo}
        m.update(weights)
        m.update(consts)
        in_maps.append(m)
    return in_maps


def kernel(**inputs):
    nc = build_nc(phase=3, debug=False)
    in_maps = make_in_maps(inputs)
    res = run_bass_kernel_spmd(nc, in_maps, core_ids=list(range(8)))
    out = np.empty((B, T, D), np.float32)
    for c in range(8):
        b, v = c // 4, c % 4
        yo = res.results[c]["y_own_out"]
        out[b, 1024 * v:1024 * (v + 1)] = yo[0:1024]
        out[b, 1024 * (7 - v):1024 * (8 - v)] = yo[1024:2048]
    return out


if __name__ == "__main__":
    import reference as R
    ins = R.setup_inputs(0)
    out = kernel(**{k: np.asarray(v) for k, v in ins.items()})
    ref = np.asarray(R.reference(**ins))
    num = float(np.mean((out - ref) ** 2))
    den = float(np.mean(ref ** 2))
    print("resid var ratio:", num / den)


# version 20
# speedup vs baseline: 1.4184x; 1.0036x over previous
import contextlib
import numpy as np
import ml_dtypes
import concourse.bass as bass
import concourse.mybir as mybir
from concourse.bass_utils import run_bass_kernel_spmd

F32 = mybir.dt.float32
BF16 = mybir.dt.bfloat16
AF = mybir.ActivationFunctionType
ALU = mybir.AluOpType
AX = mybir.AxisListType.X
BF = ml_dtypes.bfloat16

B, T, D = 2, 8192, 1024
NH, DH, CH = 16, 64, 80
RQ = 6
NT = T // 128
OWN = 2048
NOT_ = OWN // 128
NST = 4
E16, RM, OH = 16, 512, 4096
ROPE_BASE = 10000.0
LN_EPS = 1e-5
ENGS = ("sync", "vector", "scalar", "tensor", "gpsimd")


def var_kv_tiles(v):
    return 8 * (8 - v)


def own_tile_global(v, o):
    return 8 * v + o if o < 8 else (56 - 8 * v) + (o - 8)


def stripe_nblk(v, s):
    return [8 * v + 4, 8 * v + 8, 60 - 8 * v, 64 - 8 * v][s]


_CHAINED = {"vector", "scalar", "gpsimd"}
_NOCHAIN_OPS = {"dma_start", "dma_start_transpose", "nop", "sem_inc",
                "partition_id", "collective_compute"}
DMA_SEMS = {"ld", "ngst", "wcast", "p_x", "p_kst", "p_vst0", "p_vst1",
            "p_xnst0", "p_xnst1", "q_ld", "q_st",
            "a_kv0", "a_kv1", "a_qt", "x_qb", "x_st", "dbst", "a_ynst",
            "f_wld", "f_xld", "f_xt",
            "f_yld", "f_xold", "f_ost"}
BAR_SEMS = {"bar_v", "bar_s", "bar_g", "bar_t", "rdone_vector", "rdone_scalar",
            "rdone_tensor", "rdone_gpsimd", "rdone_sync"}


def _is_virtual(sem, owner):
    return (owner.get(sem) in _CHAINED and sem not in DMA_SEMS
            and sem not in BAR_SEMS and not sem.startswith("ch_"))


class _Instr:
    __slots__ = ("ctx", "real", "chained", "eng")

    def __init__(self, ctx, real, chained, eng):
        self.ctx = ctx
        self.real = real
        self.chained = chained
        self.eng = eng

    def inc(self, sem, k=1):
        ctx = self.ctx
        ctx.sems[sem] = ctx.sems.get(sem, 0) + k
        if self.eng in _CHAINED and _is_virtual(sem, ctx.owner):
            ctx.hist.setdefault(sem, {})[ctx.sems[sem]] = ctx.sems.get(
                "ch_" + self.eng, 0)
        else:
            assert not self.chained, f"real sem {sem} inc on chained op"
            if self.real is not None:
                self.real.then_inc(ctx.semh[sem], k)
        return self


class _Prox:
    def __init__(self, ctx, name):
        self._ctx = ctx
        self._name = name

    def __getattr__(self, op):
        ctx = self._ctx
        name = self._name
        chain = name in _CHAINED and op not in _NOCHAIN_OPS
        csem = "ch_" + name

        def call(*a, **kw):
            if op == "nop" and "nofuse" not in kw:
                kw["nofuse"] = True
            real = None
            if chain:
                cur = ctx.sems.get(csem, 0)
                if cur > 0 and ctx.active == name and ctx.dry == 0:
                    getattr(ctx.nc, name).wait_ge(ctx.semh[csem], cur)
            if ctx.active == name and ctx.dry == 0:
                real = getattr(getattr(ctx.nc, name), op)(*a, **kw)
            ins = _Instr(ctx, real, chain, name)
            if chain:
                ctx.sems[csem] = ctx.sems.get(csem, 0) + 1
                if real is not None:
                    real.then_inc(ctx.semh[csem], 1)
            if name == "tensor" and op in ("matmul", "transpose"):
                ctx.last_tensor = ins
            return ins

        return call

    def mark(self, sem, k=1):
        assert self._name == "tensor"
        self._ctx.last_tensor.inc(sem, k)

    def wait(self, sem, val=None):
        ctx = self._ctx
        v = ctx.sems.get(sem, 0) if val is None else val
        if v <= 0:
            return
        own = ctx.owner.get(sem)
        if _is_virtual(sem, ctx.owner):
            if v <= ctx.hist_floor.get(sem, 0):
                return
            cc = ctx.hist.get(sem, {}).get(v)
            if cc is None:
                raise RuntimeError(f"wait {sem}>={v}: no hist entry "
                                   f"(have {sorted(ctx.hist.get(sem, {}))[-4:]})")
            sem, v = "ch_" + own, cc
        key = (self._name, sem)
        if ctx.lastwait.get(key, -1) >= v:
            return
        ctx.lastwait[key] = v
        if ctx.active == self._name and ctx.dry == 0:
            getattr(ctx.nc, self._name).wait_ge(ctx.semh[sem], v)

    def pad_sem(self, sem, target):
        ctx = self._ctx
        cur = ctx.sems.get(sem, 0)
        if target > cur:
            if ctx.active == self._name and ctx.dry == 0:
                eng = getattr(ctx.nc, self._name)
                step = 16 if sem in DMA_SEMS else 1
                n = target - cur
                assert n % step == 0, (sem, cur, target)
                for _ in range(n // step):
                    eng.nop(cycle_cnt=4, nofuse=True).then_inc(ctx.semh[sem], step)
            ctx.sems[sem] = target


class Ctx:
    def __init__(self, nc, semh, active, owner):
        self.nc = nc
        self.semh = semh
        self.active = active
        self.owner = owner
        self.sems = {}
        self.hist = {}
        self.hist_floor = {}
        self.lastwait = {}
        self.dry = 0
        self.last_tensor = _Instr(self, None, False, 'tensor')
        self.sp = _Prox(self, "sync")
        self.v = _Prox(self, "vector")
        self.sc = _Prox(self, "scalar")
        self.t = _Prox(self, "tensor")
        self.g = _Prox(self, "gpsimd")
        self.prox = {"sync": self.sp, "vector": self.v, "scalar": self.sc,
                     "tensor": self.t, "gpsimd": self.g}

    def _snap(self):
        return (dict(self.sems), {k: dict(v) for k, v in self.hist.items()},
                dict(self.hist_floor), dict(self.lastwait))

    def _restore(self, snap):
        self.sems.clear()
        self.sems.update(snap[0])
        self.hist.clear()
        self.hist.update({k: dict(v) for k, v in snap[1].items()})
        self.hist_floor.clear()
        self.hist_floor.update(snap[2])
        self.lastwait.clear()
        self.lastwait.update(snap[3])

    def variant_region(self, engines, body_fn):
        snap = self._snap()
        ends = []
        for v in range(4):
            self._restore(snap)
            self.dry += 1
            body_fn(v)
            self.dry -= 1
            ends.append(self.sems)
        allsems = set()
        for e in ends:
            allsems |= set(e)
        incremented = sorted(k for k in allsems
                             if max(e.get(k, 0) for e in ends) > snap[0].get(k, 0))
        for k in incremented:
            if self.owner.get(k) not in engines:
                raise RuntimeError(
                    f"sem {k} (owner {self.owner.get(k)}) incremented in "
                    f"region with engines {engines}")

        def epilogue():
            for k in incremented:
                own = self.owner[k]
                if own in engines:
                    self.prox[own].wait(k)
            for e in engines:
                if e in _CHAINED:
                    self.prox[e].wait("ch_" + e)
                self.prox[e].drain()

        if self.active in engines and self.dry == 0:
            eng = getattr(self.nc, self.active)
            pid = eng.partition_id()
            idx = pid & 3
            for arm in eng.Switch(idx, 4):
                self._restore(snap)
                body_fn(arm)
                epilogue()
        else:
            self._restore(snap)
            self.dry += 1
            body_fn(0)
            epilogue()
            self.dry -= 1
        self._restore(snap)
        self.pending_region = (engines, incremented)

    def clear_region(self):
        engines, incremented = self.pending_region
        for k in incremented:
            own = self.owner[k]
            if self.active == own and self.dry == 0:
                getattr(self.nc, own).sem_clear(self.semh[k])
            self.sems[k] = 0
            self.hist.pop(k, None)
            self.hist_floor.pop(k, None)
            for kk in list(self.lastwait):
                if kk[1] == k:
                    del self.lastwait[kk]
        for e in engines:
            if e in _CHAINED:
                if self.active == e and self.dry == 0:
                    getattr(self.nc, e).sem_clear(self.semh["ch_" + e])
                self.sems["ch_" + e] = 0
                for kk in list(self.lastwait):
                    if kk[1] == "ch_" + e:
                        del self.lastwait[kk]
                for k in list(self.hist):
                    if self.owner.get(k) == e:
                        self.hist.pop(k)
                        self.hist_floor[k] = self.sems.get(k, 0)


SEM_OWNER = {
    "ld": "sync", "ngst": "sync", "ng": "vector",
    "wcast": "gpsimd", "vset": "gpsimd",
    "ch_vector": "vector", "ch_scalar": "scalar", "ch_gpsimd": "gpsimd",
    "p_x": "scalar", "p_stats": "vector", "p_rstd": "scalar", "p_xnt": "tensor",
    "p_xne": "scalar", "p_mm": "tensor", "p_pev": "scalar", "p_rot": "gpsimd",
    "p_kprod": "vector", "p_ktr": "tensor", "p_kte": "scalar", "p_kst": "scalar",
    "p_vprod": "gpsimd", "p_vst0": "gpsimd", "p_vst1": "gpsimd",
    "p_xnst0": "scalar", "p_xnst1": "scalar",
    "q_ld": "scalar", "q_mm": "tensor", "q_ev": "scalar", "q_prod": "vector",
    "q_tr": "tensor", "q_te": "scalar", "q_st": "scalar",
    "bar_v": "vector", "bar_t": "tensor", "bar_s": "scalar", "bar_g": "gpsimd",
    "rdone_vector": "vector", "rdone_scalar": "scalar", "rdone_tensor": "tensor",
    "rdone_gpsimd": "gpsimd", "rdone_sync": "sync",
    "a_kv0": "scalar", "a_kv1": "scalar",
    "a_qt": "scalar", "x_qb": "scalar", "x_st": "scalar", "dbst": "scalar",
    "x_mm": "tensor", "x_mx": "vector", "x_ls": "scalar",
    "x_tr": "tensor", "x_ev": "vector",
    "a_qk": "tensor", "a_exp": "scalar",
    "a_msk": "gpsimd", "a_av": "tensor", "a_evac": "vector",
    "a_ftr": "tensor", "a_fv": "vector", "a_ftr2": "tensor", "a_fe": "vector",
    "a_ynst": "scalar",
    "f_wld": "gpsimd", "f_xld": "scalar", "f_xt": "sync", "f_yld": "sync",
    "f_xold": "sync",
    "f_mm": "tensor", "f_ve": "vector", "f_act": "scalar", "f_ost": "sync", "f_res": "vector",
    "f_gel": "scalar",
}


def build_nc(phase=3, debug=False):
    nc = bass.Bass("TRN2", target_bir_lowering=False)

    def inp(name, shape, dt=F32):
        return nc.dram_tensor(name, shape, dt, kind="ExternalInput").ap()

    x_full = inp("x_full", [T, D])
    x_own = inp("x_own", [OWN, D])
    wabqT = inp("wabqT", [D, RQ * CH])
    wabkvT = inp("wabkvT", [D, 4 * CH])
    w_proj = inp("w_proj", [D, E16])
    w_c1 = inp("w_c1", [E16, RM])
    w_c2 = inp("w_c2", [E16, RM])
    w_in1 = inp("w_in1", [D, RM])
    w_b1 = inp("w_b1", [RM, 1])
    w_out1T = inp("w_out1T", [RM, OH])
    w_in2 = inp("w_in2", [OH, RM])
    w_b2 = inp("w_b2", [RM, 1])
    w_out2T = inp("w_out2T", [RM, D])
    w_oT = inp("w_oT", [D, D])
    c_identb = inp("c_identb", [128, 128], BF16)
    c_identf = inp("c_identf", [128, 128], F32)
    c_ropeC = inp("c_ropeC", [NT, 128, 32])
    c_ropeS = inp("c_ropeS", [NT, 128, 32])
    c_dmask = inp("c_dmask", [4, 128, 512], BF16)
    c_emx = inp("c_emx", [128, 56])
    c_auxmb = inp("c_auxmb", [128, 512])

    y_out = nc.dram_tensor("y_own_out", [OWN, D], F32, kind="ExternalOutput").ap()

    kv_k = nc.dram_tensor("kv_k", [NH, 65, T], BF16, kind="Internal").ap()
    kv_v = nc.dram_tensor("kv_v", [T, NH * 65], BF16, kind="Internal").ap()
    xnT_all = nc.dram_tensor("xnT_all", [8, 128, T], BF16, kind="Internal").ap()
    xnT_own = nc.dram_tensor("xnT_own", [8, 128, OWN], BF16, kind="Internal").ap()
    qT_d = nc.dram_tensor("qT_d", [8, 128, OWN], BF16, kind="Internal").ap()
    ynT_d = nc.dram_tensor("ynT_d", [8, 128, OWN], BF16, kind="Internal").ap()
    mu_d = nc.dram_tensor("mu_d", [64, 128], BF16, kind="Internal").ap()
    muT_d = nc.dram_tensor("muT_d", [16, OWN], BF16, kind="Internal").ap()

    dbg = {}
    if debug:
        def dout(name, shape, dt):
            dbg[name] = nc.dram_tensor("dbg_" + name, shape, dt, kind="ExternalOutput").ap()
        dout("xnT", [8, 128, T], BF16)
        dout("kvk", [NH, 65, T], BF16)
        dout("kvv", [T, NH * 65], BF16)
        dout("qT", [8, 128, OWN], BF16)
        dout("ynT", [8, 128, OWN], BF16)
        dout("yacc", [NST, 65, 16 * 512], F32)
        dout("mu", [NST, 1, 16 * 512], BF16)

    with contextlib.ExitStack() as stack:
        semh = {s: stack.enter_context(nc.semaphore(s)) for s in SEM_OWNER}

        def sb(name, shape, dt):
            return stack.enter_context(nc.sbuf_tensor(name, shape, dt)).ap()

        identb = sb("identb", [128, 128], BF16)
        identf = sb("identf", [128, 128], F32)
        dmask = sb("dmask", [128, 4 * 512], BF16)
        emx = sb("emx", [128, 56], F32)
        auxmb = sb("auxmb", [128, 512], F32)
        cols = sb("cols", [128, 16], F32)
        negrow = sb("negrow", [128, 1024], BF16)

        ARENA = 196 * 1024
        U8 = mybir.dt.uint8
        arena = sb("arena", [128, ARENA], U8)
        _esize = {F32: 4, BF16: 2}

        def mkregion():
            off = [0]

            def alloc(shape, dt):
                nb = shape[1] * _esize[dt]
                o = (off[0] + 31) // 32 * 32
                assert o + nb <= ARENA, f"arena overflow {o + nb}"
                off[0] = o + nb
                vw = arena[:, o:o + nb].bitcast(dt)
                if shape[0] != 128:
                    vw = vw[0:shape[0], :]
                return vw
            return alloc

        P = mkregion()
        ropeC = P([128, NT * 32], F32)
        ropeS = P([128, NT * 32], F32)
        wabq_b = P([128, 8 * 480], BF16)
        wabkv_b = P([128, 8 * 320], BF16)
        xt = P([128, 2 * 1024], F32)
        scr = P([128, 1024], F32)
        rscr = P([128, 2 * 256], F32)
        xnb = P([128, 1024], BF16)
        xnT_sb = P([128, 2 * 1024], BF16)
        akv = P([128, 2 * 64], F32)
        bkv = P([128, 2 * 256], F32)
        bkvr = P([128, 2 * 256], F32)
        Pt = P([128, 2 * 2048 + 1024], F32)
        ptb = P([128, 2 * 1024], BF16)
        Pvt = P([128, 1024], BF16)
        vstage = P([128, 2 * 1040], BF16)
        kT_sb = P([128, 2 * 1024], BF16)
        xq_t = P([128, 2 * 1024], BF16)
        aq = P([128, 2 * 96], F32)
        bq = P([128, 2 * 384], F32)
        qT_sb = P([128, 2 * 1024], BF16)
        kq = P([128, 1024], F32)
        ksq = P([128, 16], F32)
        kmax = P([128, 16], F32)
        kmaxT = P([16, 128], F32)
        km1 = P([16, 1], F32)
        km1r = P([1, 16], F32)
        kbr = P([128, 16], F32)
        ones1 = P([1, 128], F32)
        qsq = P([128, 16], F32)
        mu2 = P([128, 16], F32)
        muv = P([128, 16], F32)
        muT_sb = P([16, 128], BF16)

        A = mkregion()
        kr_k = A([65, 2 * 16 * 1024], BF16)
        kr_v = A([128, 2 * 8 * 1040], BF16)
        qbuf = A([64, 16 * 512], BF16)
        qT_ext = A([65, 2 * 16 * 512], BF16)
        rmx = A([128, 64], F32)
        mstage = A([64, 128], BF16)
        mtmp = A([128, 1], F32)
        junk = A([128, 512], F32)
        junk2 = A([128, 512], BF16)
        lsb = A([128, 512], F32)
        red32 = A([128, 32], F32)
        pT0 = A([128, 1024], BF16)
        pT1 = A([128, 1024], BF16)
        y_acc = A([65, 16 * 512], F32)
        ytokn = A([128, 4 * 64], BF16)
        ynT_sb = A([64, 512], BF16)
        recipS = A([128, 4], F32)

        F = mkregion()
        w1b = F([128, 8 * 512], BF16)
        wo1b = F([128, 4 * 4096], BF16)
        w2b = F([128, 32 * 512], BF16)
        wo2b = F([128, 4 * 1024], BF16)
        woTb = F([128, 8 * 1024], BF16)
        projb = F([128, 8 * 16], BF16)
        c1b = F([16, 512], BF16)
        c2b = F([16, 512], BF16)
        b1c = F([128, 4], F32)
        b2c = F([128, 4], F32)
        fxn = F([128, 8 * 512], BF16)
        fyn = F([128, 8 * 512], BF16)
        fxo = F([128, 1024], F32)
        zproj = F([16, 512], F32)
        ztok = F([128, 64], F32)
        ecols = F([128, 24], F32)
        ezs = F([128, 64], F32)
        ez3 = F([128, 64], F32)
        etmp = F([128, 64], F32)
        etmp2 = F([128, 64], F32)
        ecs = F([128, 64], F32)
        ecs2 = F([128, 64], F32)
        ab16 = F([128, 64], BF16)
        aT_all = F([16, 4 * 512], BF16)
        g1T = F([128, 4 * 512], F32)
        g2T = F([128, 4 * 512], F32)
        z1b = F([128, 4 * 512], BF16)
        hb = F([128, 32 * 512], BF16)
        z2b = F([128, 4 * 512], BF16)
        outsb = F([128, 2 * 1024], F32)

        ps_s0 = stack.enter_context(nc.psum_tensor("ps_s0", [128, 1024], F32)).ap()
        ps_s1 = stack.enter_context(nc.psum_tensor("ps_s1", [128, 1024], F32)).ap()
        ps_av = stack.enter_context(nc.psum_tensor("ps_av", [128, 1024], F32)).ap()
        ps_u = stack.enter_context(nc.psum_tensor("ps_u", [128, 512], F32)).ap()
        ps_x = stack.enter_context(nc.psum_tensor("ps_x", [128, 512], F32)).ap()

        def bfv(ps, bank):
            return ps[:, bank * 512:(bank + 1) * 512].bitcast(BF16)

        if True:
            def program(ctx: Ctx, sec: int):
                sp, v, sc, t, g = ctx.sp, ctx.v, ctx.sc, ctx.t, ctx.g
                S = ctx.sems

                def drain_dma():
                    for s_ in sorted(DMA_SEMS):
                        sp.wait(s_)

                if sec == 0:
                    run_init(ctx)
                elif sec == 1:
                    run_prep(ctx)
                elif sec == 2:
                    ctx.clear_region()
                elif sec == 3:
                    run_kfin(ctx)
                elif sec == 4:
                    run_qprep(ctx)
                elif sec == 5:
                    ctx.clear_region()
                elif sec == 6:
                    run_bias(ctx)
                elif sec == 7:
                    if phase < 2:
                        run_placeholder_out(ctx)
                    else:
                        run_attention(ctx)
                elif sec == 8:
                    ctx.clear_region()
                elif sec == 9:
                    run_postattn(ctx)
                elif sec == 10:
                    if phase < 3:
                        run_placeholder_out(ctx)
                    else:
                        run_ffn(ctx)
                else:
                    raise RuntimeError(sec)

            def run_init(ctx: Ctx):
                sp, v, sc, t, g = ctx.sp, ctx.v, ctx.sc, ctx.t, ctx.g
                S = ctx.sems
                sp.dma_start(out=identb[:], in_=c_identb[:]).inc("ld", 16)
                sp.dma_start(out=identf[:], in_=c_identf[:]).inc("ld", 16)
                sp.dma_start(out=ropeC.rearrange("p (g c) -> p g c", c=32),
                             in_=c_ropeC.rearrange("g p c -> p g c")).inc("ld", 16)
                sp.dma_start(out=ropeS.rearrange("p (g c) -> p g c", c=32),
                             in_=c_ropeS.rearrange("g p c -> p g c")).inc("ld", 16)
                sp.dma_start(out=dmask.rearrange("p (j c) -> p j c", c=512),
                             in_=c_dmask.rearrange("j p c -> p j c")).inc("ld", 16)
                sp.dma_start(out=emx[:], in_=c_emx[:]).inc("ld", 16)
                sp.dma_start(out=auxmb[:], in_=c_auxmb[:]).inc("ld", 16)
                g.dma_start(out=wabq_b.rearrange("p (c f) -> p c f", c=8),
                            in_=wabqT.rearrange("(c p) f -> p c f", p=128)).inc("wcast", 16)
                g.dma_start(out=wabkv_b.rearrange("p (c f) -> p c f", c=8),
                            in_=wabkvT.rearrange("(c p) f -> p c f", p=128)).inc("wcast", 16)
                g.memset(vstage.rearrange("p (b h c) -> p b h c", b=2, h=16)[:, :, :, 64:65],
                         1.0).inc("vset", 1)
                v.memset(ones1[0:1, :], 1.0)
                v.memset(negrow[:], -1.0).inc("ng", 1)
                sp.wait("ng")
                sp.dma_start(
                    out=kv_k[:, 64:65, :].rearrange("h o (p t) -> p h o t", p=128),
                    in_=negrow.rearrange("p (h o t) -> p h o t", h=16, o=1)
                ).inc("ngst", 16)

            def run_prep(ctx: Ctx):
                sp, v, sc, t, g = ctx.sp, ctx.v, ctx.sc, ctx.t, ctx.g
                S = ctx.sems
                def kv_prep(vv):
                    ntl = var_kv_tiles(vv)
                    v.memset(kmax[:, 0:16], 0.0)

                    def stage1(i):
                        pb = i % 2
                        xti = xt[:, pb * 1024:(pb + 1) * 1024]
                        sp.wait("p_stats", max(0, 2 * (i - 1)))
                        sp.dma_start(out=xti, in_=x_full[i * 128:(i + 1) * 128, :]).inc("p_x", 16)
                        v.wait("p_x")
                        if i == 0:
                            v.wait("wcast", 32)
                            v.wait("vset", 1)
                            v.wait("ld", 112)
                        v.reduce_sum(cols[:, 0:1], xti, axis=AX)
                        v.scalar_tensor_tensor(scr[:], xti, 1.0, xti, ALU.mult, ALU.mult,
                                               accum_out=cols[:, 1:2])
                        v.tensor_scalar(cols[:, 2:3], cols[:, 0:1], 1.0 / D, None, ALU.mult)
                        v.tensor_scalar(cols[:, 3:4], cols[:, 1:2], 1.0 / D, None, ALU.mult)
                        v.tensor_mul(cols[:, 4:5], cols[:, 2:3], cols[:, 2:3])
                        v.tensor_sub(cols[:, 5:6], cols[:, 3:4], cols[:, 4:5]).inc("p_stats", 1)
                        sc.wait("p_stats", 2 * i + 1)
                        sc.activation(cols[:, 6:7], cols[:, 5:6], AF.Ln, bias=emx[:, 48:49])
                        sc.activation(cols[:, 7:8], cols[:, 6:7], AF.Exp, scale=-0.5).inc("p_rstd", 1)
                        v.wait("p_rstd")
                        v.tensor_scalar(xnb[:], xti, cols[:, 2:3], cols[:, 7:8],
                                        ALU.subtract, ALU.mult).inc("p_stats", 1)
                        xnt_ps = bfv(ps_u if i % 2 == 0 else ps_x, 0)
                        t.wait("p_stats", 2 * i + 2)
                        t.wait("p_xne", max(0, i - 1))
                        for fc in range(8):
                            t.matmul(xnt_ps[:, fc * 128:(fc + 1) * 128],
                                     xnb[:, fc * 128:(fc + 1) * 128], identb[:],
                                     is_transpose=True, start=True, stop=True)
                        t.mark("p_xnt", 1)
                        xnti = xnT_sb[:, pb * 1024:(pb + 1) * 1024]
                        sc.wait("p_xnt")
                        sc.wait("p_mm", max(0, i - 1))
                        sc.wait("p_xnst0" if pb == 0 else "p_xnst1")
                        sc.activation(xnti, xnt_ps, AF.Copy, bias=0.0).inc("p_xne", 1)
                        sp.wait("p_xne")
                        sp.dma_start(out=xnT_all.rearrange("c p t -> p c t")[:, :, i * 128:(i + 1) * 128],
                                     in_=xnti.rearrange("p (c t) -> p c t", c=8)).inc("p_xnst0" if pb == 0 else "p_xnst1", 16)
                        t.wait("p_xne")
                        t.wait("p_pev", max(0, i - 1))
                        s0c = ps_s0[:, pb * 512:pb * 512 + 320]
                        for dc in range(8):
                            t.matmul(s0c, xnti[:, dc * 128:(dc + 1) * 128],
                                     wabkv_b[:, dc * 320:(dc + 1) * 320],
                                     start=(dc == 0), stop=(dc == 7))
                        t.mark("p_mm", 1)
                        akv_i = akv[:, pb * 64:(pb + 1) * 64]
                        bkv_i = bkv[:, pb * 256:(pb + 1) * 256]
                        bkvr_i = bkvr[:, pb * 256:(pb + 1) * 256]
                        sc.wait("p_mm")
                        sc.wait("p_vprod", max(0, i - 1))
                        sc.wait("p_rot", max(0, i - 1))
                        aps = s0c.rearrange("p (r c) -> p r c", r=4)
                        sc.activation(akv_i.rearrange("p (r h) -> p r h", r=4),
                                      aps[:, :, 0:16], AF.Copy, scale=0.5, bias=0.0)
                        sc.activation(bkv_i.rearrange("p (r c) -> p r c", r=4),
                                      aps[:, :, 16:80], AF.Copy, bias=0.0).inc("p_pev", 1)

                    def stage2(i):
                        pb = i % 2
                        akv_i = akv[:, pb * 64:(pb + 1) * 64]
                        bkv_i = bkv[:, pb * 256:(pb + 1) * 256]
                        bkvr_i = bkvr[:, pb * 256:(pb + 1) * 256]
                        g.wait("p_pev", i + 1)
                        g.wait("p_kprod", max(0, i - 1))
                        cosr = ropeC[:, i * 32:(i + 1) * 32].rearrange(
                            "p (o c) -> p o c", o=1).broadcast_to([128, 4, 32])
                        sinr = ropeS[:, i * 32:(i + 1) * 32].rearrange(
                            "p (o c) -> p o c", o=1).broadcast_to([128, 4, 32])
                        bk4 = bkv_i.rearrange("p (r c) -> p r c", r=4)
                        br4 = bkvr_i.rearrange("p (r c) -> p r c", r=4)
                        rs4 = rscr[:, pb * 256:(pb + 1) * 256].rearrange(
                            "p (r c) -> p r c", r=8)
                        g.tensor_tensor(br4[:, :, 0:32], bk4[:, :, 0:32], cosr, ALU.mult)
                        g.tensor_tensor(rs4[:, 0:4, :], bk4[:, :, 32:64], sinr, ALU.mult)
                        g.tensor_tensor(br4[:, :, 0:32], br4[:, :, 0:32], rs4[:, 0:4, :], ALU.add)
                        g.tensor_tensor(br4[:, :, 32:64], bk4[:, :, 32:64], cosr, ALU.mult)
                        g.tensor_tensor(rs4[:, 4:8, :], bk4[:, :, 0:32], sinr, ALU.mult)
                        g.tensor_tensor(br4[:, :, 32:64], br4[:, :, 32:64], rs4[:, 4:8, :],
                                        ALU.subtract).inc("p_rot", 1)
                        ptv = Pt[:, pb * 2048:(pb + 1) * 2048]
                        v.wait("p_rot", i + 1)
                        v.wait("p_ktr", max(0, i - 1))
                        for s_ in range(2):
                            a_b = akv_i[:, s_ * 16:(s_ + 1) * 16].rearrange(
                                "p (o h) -> p h o", o=1).broadcast_to([128, 16, 64])
                            b_b = bkvr_i[:, s_ * 64:(s_ + 1) * 64].rearrange(
                                "p (o c) -> p o c", o=1).broadcast_to([128, 16, 64])
                            v.tensor_tensor(
                                ptv[:, s_ * 1024:(s_ + 1) * 1024].rearrange("p (h c) -> p h c", h=16),
                                a_b, b_b, ALU.mult)
                        v.tensor_tensor(ptb[:, pb * 1024:(pb + 1) * 1024],
                                        ptv[:, 0:1024], ptv[:, 1024:2048],
                                        ALU.add)
                        v.nop().inc("p_kprod", 1)
                        ptbk = ptb[:, pb * 1024:(pb + 1) * 1024]
                        v.scalar_tensor_tensor(kq[:], ptbk, 1.0, ptbk,
                                               ALU.mult, ALU.mult)
                        v.tensor_reduce(ksq[:, 0:16],
                                        kq.rearrange("p (h c) -> p h c", h=16),
                                        axis=AX, op=ALU.add)
                        v.tensor_tensor(kmax[:, 0:16], kmax[:, 0:16],
                                        ksq[:, 0:16], ALU.max)
                        kt_ps = bfv(ps_s1 if pb == 0 else ps_av, 0)
                        t.wait("p_kprod", i + 1)
                        t.wait("p_kte", max(0, i - 1))
                        for fc in range(8):
                            t.matmul(kt_ps[:, fc * 128:(fc + 1) * 128],
                                     ptb[:, pb * 1024 + fc * 128:
                                         pb * 1024 + (fc + 1) * 128],
                                     identb[:], is_transpose=True,
                                     start=True, stop=True)
                        t.mark("p_ktr", 1)
                        kT_i = kT_sb[:, pb * 1024:(pb + 1) * 1024]
                        sc.wait("p_ktr", i + 1)
                        sc.wait("p_kst", 32 * i)
                        sc.activation(kT_i, kt_ps[:, 0:1024], AF.Copy,
                                      bias=0.0).inc("p_kte", 1)
                        sp.wait("p_kte", i + 1)
                        kv2 = kv_k.rearrange("(c two) r t -> two r c t", two=2)
                        for two in range(2):
                            sp.dma_start(
                                out=kv2[two][0:64, :, i * 128:(i + 1) * 128],
                                in_=kT_i[64 * two:64 * two + 64, :].rearrange(
                                    "p (c t) -> p c t", c=8)).inc("p_kst", 16)
                        vst = vstage[:, pb * 1040:(pb + 1) * 1040]
                        vsh = vst.rearrange("p (h c) -> p h c", h=16)[:, :, 0:64]
                        g.wait("p_vst0" if pb == 0 else "p_vst1")
                        for s_ in (2, 3):
                            a_b = akv_i[:, s_ * 16:(s_ + 1) * 16].rearrange(
                                "p (o h) -> p h o", o=1).broadcast_to([128, 16, 64])
                            b_b = bkvr_i[:, s_ * 64:(s_ + 1) * 64].rearrange(
                                "p (o c) -> p o c", o=1).broadcast_to([128, 16, 64])
                            if s_ == 2:
                                g.tensor_tensor(vsh, a_b, b_b, ALU.mult)
                            else:
                                g.tensor_tensor(Pvt.rearrange("p (h c) -> p h c", h=16),
                                                a_b, b_b, ALU.mult)
                                g.tensor_tensor(vsh, vsh,
                                                Pvt.rearrange("p (h c) -> p h c", h=16),
                                                ALU.add)
                        g.nop().inc("p_vprod", 1)
                        sp.wait("p_vprod", i + 1)
                        sp.dma_start(out=kv_v[i * 128:(i + 1) * 128, :], in_=vst).inc("p_vst0" if pb == 0 else "p_vst1", 16)

                    for i in range(ntl + 1):
                        if i < ntl:
                            stage1(i)
                        if i >= 1:
                            stage2(i - 1)

                ctx.variant_region(("sync", "vector", "scalar", "tensor", "gpsimd"),
                                   kv_prep)


            def run_kfin(ctx: Ctx):
                pass

            def run_qprep(ctx: Ctx):
                sp, v, sc, t, g = ctx.sp, ctx.v, ctx.sc, ctx.t, ctx.g
                S = ctx.sems
                def q_prep(vv):
                    qtr_n = [S.get("q_tr", 0)]
                    t.matmul(ps_u[0:16, 0:128], kmax[:, 0:16], identf[:],
                             is_transpose=True, start=True, stop=True)
                    t.mark("x_tr", 1)
                    v.wait("x_tr", S.get("x_tr", 0))
                    v.tensor_copy(kmaxT[0:16, :], ps_u[0:16, 0:128])
                    v.tensor_reduce(km1[0:16, 0:1], kmaxT[0:16, :], axis=AX,
                                    op=ALU.max)
                    v.nop().inc("x_mx", 1)
                    t.wait("x_mx", S.get("x_mx", 0))
                    t.matmul(ps_u[0:1, 128:144], km1[0:16, 0:1],
                             identf[0:16, 0:16], is_transpose=True,
                             start=True, stop=True)
                    t.mark("x_tr", 1)
                    v.wait("x_tr", S.get("x_tr", 0))
                    v.tensor_copy(km1r[0:1, :], ps_u[0:1, 128:144])
                    v.nop().inc("x_mx", 1)
                    t.wait("x_mx", S.get("x_mx", 0))
                    t.matmul(ps_u[0:128, 256:272], ones1[0:1, :], km1r[0:1, :],
                             start=True, stop=True)
                    t.mark("x_tr", 1)
                    v.wait("x_tr", S.get("x_tr", 0))
                    v.tensor_copy(kbr[:, 0:16], ps_u[:, 256:272])
                    v.nop().inc("x_mx", 1)

                    def stage1(o):
                        gt = own_tile_global(vv, o)
                        pb = o % 2
                        xq_i = xq_t[:, pb * 1024:(pb + 1) * 1024]
                        sp.wait("q_mm", max(0, o - 1))
                        sp.dma_start(out=xq_i.rearrange("p (c t) -> p c t", c=8),
                                     in_=xnT_all.rearrange("c p t -> p c t")[:, :, gt * 128:(gt + 1) * 128]).inc("q_ld", 16)
                        sp.dma_start(out=xnT_own[:, :, o * 128:(o + 1) * 128],
                                     in_=xnT_all[:, :, gt * 128:(gt + 1) * 128]).inc("q_ld", 16)
                        t.wait("q_ld")
                        t.wait("q_ev", max(0, o - 1))
                        s0c = ps_s0[:, pb * 512:pb * 512 + 480]
                        for dc in range(8):
                            t.matmul(s0c, xq_i[:, dc * 128:(dc + 1) * 128],
                                     wabq_b[:, dc * 480:(dc + 1) * 480],
                                     start=(dc == 0), stop=(dc == 7))
                        t.mark("q_mm", 1)
                        sc.wait("q_mm")
                        sc.wait("q_prod", max(0, o - 1))
                        apq = s0c.rearrange("p (r c) -> p r c", r=6)
                        sc.activation(aq[:, pb * 96:(pb + 1) * 96].rearrange(
                                          "p (r h) -> p r h", r=6),
                                      apq[:, :, 0:16], AF.Copy, scale=1.0 / RQ,
                                      bias=0.0)
                        sc.activation(bq[:, pb * 384:(pb + 1) * 384].rearrange(
                                          "p (r c) -> p r c", r=6),
                                      apq[:, :, 16:80], AF.Copy, bias=0.0).inc("q_ev", 1)

                    qacc = Pt[:, 4096:5120]
                    qacc4 = qacc.rearrange("p (h c) -> p h c", h=16)

                    def stage2(o):
                        gt = own_tile_global(vv, o)
                        pb = o % 2
                        qt_ps = bfv(ps_s1 if pb == 0 else ps_av, 0)
                        ptbq = ptb[:, pb * 1024:(pb + 1) * 1024]
                        v.wait("q_ev", o + 1)
                        for r in range(6):
                            pr = r % 2
                            a_b = aq[:, pb * 96 + r * 16:pb * 96 + (r + 1) * 16].rearrange(
                                "p (o h) -> p h o", o=1).broadcast_to([128, 16, 64])
                            b_b = bq[:, pb * 384 + r * 64:pb * 384 + (r + 1) * 64].rearrange(
                                "p (o c) -> p o c", o=1).broadcast_to([128, 16, 64])
                            if r == 0:
                                v.tensor_tensor(qacc4, a_b, b_b, ALU.mult)
                            else:
                                v.tensor_tensor(
                                    Pt[:, pr * 1024:(pr + 1) * 1024].rearrange(
                                        "p (h c) -> p h c", h=16),
                                    a_b, b_b, ALU.mult)
                                if r < 5:
                                    v.scalar_tensor_tensor(
                                        qacc, Pt[:, pr * 1024:(pr + 1) * 1024],
                                        1.0, qacc, ALU.mult, ALU.add)
                                else:
                                    v.wait("q_tr", max(0, o - 1))
                                    v.scalar_tensor_tensor(
                                        ptbq, Pt[:, pr * 1024:(pr + 1) * 1024],
                                        1.0, qacc, ALU.mult, ALU.add)
                        v.nop().inc("q_prod", 1)
                        t.wait("q_prod", o + 1)
                        t.wait("q_te", max(0, o - 1))
                        for fc in range(8):
                            t.matmul(qt_ps[:, fc * 128:(fc + 1) * 128],
                                     ptbq[:, fc * 128:(fc + 1) * 128],
                                     identb[:], is_transpose=True,
                                     start=True, stop=True)
                        t.mark("q_tr", 1)
                        qT_i = qT_sb[:, pb * 1024:(pb + 1) * 1024]
                        sc.wait("q_tr", o + 1)
                        sc.wait("q_st", 16 * o)
                        sc.activation(qT_i, qt_ps[:, 0:1024], AF.Copy,
                                      bias=0.0).inc("q_te", 1)
                        sp.wait("q_te", o + 1)
                        sp.dma_start(out=qT_d.rearrange("c p t -> p c t")[:, :, o * 128:(o + 1) * 128],
                                     in_=qT_i.rearrange("p (c t) -> p c t", c=8)).inc("q_st", 16)
                        v.wait("x_ls", S.get("x_ls", 0))
                        v.scalar_tensor_tensor(kq[:], ptbq, 1.0, ptbq,
                                               ALU.mult, ALU.mult)
                        v.tensor_reduce(qsq[:, 0:16],
                                        kq.rearrange("p (h c) -> p h c", h=16),
                                        axis=AX, op=ALU.add)
                        v.tensor_tensor(mu2[:, 0:16], qsq[:, 0:16],
                                        kbr[:, 0:16], ALU.mult)
                        v.nop().inc("x_mx", 1)
                        sc.wait("x_mx", S.get("x_mx", 0))
                        sc.activation(qsq[:, 0:16], mu2[:, 0:16], AF.Ln,
                                      bias=emx[:, 49:50])
                        sc.activation(muv[:, 0:16], qsq[:, 0:16], AF.Exp,
                                      scale=0.5, bias=emx[:, 54:55]).inc("x_ls", 1)
                        t.wait("x_ls", S.get("x_ls", 0))
                        t.wait("x_mx", S.get("x_mx", 0))
                        t.matmul(ps_u[0:16, 0:128], muv[:, 0:16], identf[:],
                                 is_transpose=True, start=True, stop=True)
                        t.mark("x_tr", 1)
                        sc.wait("x_tr", S.get("x_tr", 0))
                        sc.wait("x_st", S.get("x_st", 0))
                        sc.activation(muT_sb[0:16, :], ps_u[0:16, 0:128],
                                      AF.Copy, bias=0.0).inc("x_ls", 1)
                        sp.wait("x_ls", S.get("x_ls", 0))
                        sp.dma_start(out=muT_d[:, o * 128:(o + 1) * 128],
                                     in_=muT_sb[0:16, :]).inc("x_st", 16)

                    for o in range(NOT_ + 1):
                        if o < NOT_:
                            stage1(o)
                        if o >= 1:
                            stage2(o - 1)

                ctx.variant_region(("sync", "vector", "scalar", "tensor"), q_prep)

            def run_bias(ctx: Ctx):
                sp, v, sc, t, g = ctx.sp, ctx.v, ctx.sc, ctx.t, ctx.g
                if debug:
                    sp.wait("ngst", ctx.sems.get("ngst", 0))
                    sp.wait("ld", ctx.sems.get("ld", 0))
                    sp.dma_start(out=dbg["xnT"][:], in_=xnT_all[:]).inc("ld", 16)
                    sp.dma_start(out=dbg["kvk"][:], in_=kv_k[:]).inc("ld", 16)
                    sp.dma_start(out=dbg["kvv"][:], in_=kv_v[:]).inc("ld", 16)
                    sp.dma_start(out=dbg["qT"][:], in_=qT_d[:]).inc("ld", 16)

            def run_placeholder_out(ctx: Ctx):
                sp, v, sc, t, g = ctx.sp, ctx.v, ctx.sc, ctx.t, ctx.g
                S = ctx.sems

                def drain_dma():
                    for s_ in sorted(DMA_SEMS):
                        sp.wait(s_)
                v.memset(outsb[:, 0:1024], 0.0).inc("f_res", 1)
                sp.wait("f_res")
                for ti in range(NOT_):
                    sp.dma_start(out=y_out[ti * 128:(ti + 1) * 128, :],
                                 in_=outsb[:, 0:1024]).inc("f_ost", 16)
                sp.wait("f_ost")
                drain_dma()
                v.wait("f_ost")
                t.wait("f_ost")
                sc.wait("f_ost")
                g.wait("f_ost")

            def run_attention(ctx: Ctx):
                sp, v, sc, t, g = ctx.sp, ctx.v, ctx.sc, ctx.t, ctx.g
                S = ctx.sems
                def attention(vv):
                    nblk = [stripe_nblk(vv, s_) for s_ in range(NST)]
                    CC = [(n + 7) // 8 for n in nblk]
                    gts = {(s_, qt): own_tile_global(vv, 4 * s_ + qt)
                           for s_ in range(NST) for qt in range(4)}
                    avcnt = [0]
                    ucnt = [0]
                    pend_av = []
                    last_expS = {0: 0, 1: 0}
                    last_avpT = {0: 0, 1: 0}
                    kbusy = {}
                    vbusy = {}
                    akv_end = {}
                    qk_end = {}

                    def kvload(ci, nbig, with_v, nv=0):
                        blocks = min(8, nbig - 8 * ci)
                        slot = ci % 2
                        sem = "a_kv%d" % slot
                        sp.wait(sem, S.get(sem, 0))
                        if slot in kbusy:
                            sp.wait("a_qk", kbusy[slot])
                        g0 = ci * 1024
                        sp.dma_start(
                            out=kr_k[:, slot * 16384:(slot + 1) * 16384].rearrange(
                                "r (h t) -> r h t", h=16)[:, :, 0:blocks * 128],
                            in_=kv_k.rearrange("h r t -> r h t")[:, :, g0:g0 + blocks * 128]
                        ).inc(sem, 16)
                        if with_v:
                            bv = min(8, nv - 8 * ci)
                            if slot in vbusy:
                                sp.wait("a_av", vbusy[slot])
                            sp.dma_start(
                                out=kr_v[:, slot * 8320:slot * 8320 + bv * 1040].rearrange(
                                    "p (b c) -> p b c", b=bv),
                                in_=kv_v[g0:g0 + bv * 128, :].rearrange(
                                    "(b p) c -> p b c", p=128)).inc(sem, 16)
                        akv_end[ci] = (sem, S.get(sem, 0))

                    def load_qext_rows(sa):
                        sp.wait("a_qt", S.get("a_qt", 0))
                        if sa > 1:
                            sp.wait("a_qk", qk_end[sa - 2])
                        par = sa % 2
                        for h in range(16):
                            sp.dma_start(
                                out=qT_ext[0:64, par * 8192 + h * 512:
                                           par * 8192 + (h + 1) * 512],
                                in_=qT_d[h // 2][64 * (h % 2):64 * (h % 2) + 64,
                                                 sa * 512:(sa + 1) * 512]).inc("a_qt", 16)
                        sp.wait("x_st", S.get("x_st", 0))
                        sp.dma_start(
                            out=qT_ext[64:65, par * 8192:(par + 1) * 8192].rearrange(
                                "o (h t) -> o h t", h=16),
                            in_=muT_d[:, sa * 512:(sa + 1) * 512].rearrange(
                                "(o h) t -> o h t", o=1)).inc("x_st", 16)

                    def emit_av(pa):
                        (s_, ci, h, bp, blocks, slot, up, expn, mskn) = pa
                        first = (bp == 0)
                        avs = avcnt[0] % 2
                        pT = pT0 if up == 0 else pT1
                        t.wait("a_exp", expn)
                        if mskn:
                            t.wait("a_msk", mskn)
                        if first:
                            t.wait("a_evac", max(0, avcnt[0] - 1))
                            if ci == 0 and h == 0:
                                t.wait("a_fv", S.get("a_fv", 0))
                        b0 = 2 * bp
                        for j in (0, 1):
                            vco = slot * 8320 + (b0 + j) * 1040 + h * 65
                            t.matmul(ps_av[0:65, avs * 512:(avs + 1) * 512],
                                     kr_v[:, vco:vco + 65],
                                     pT[:, j * 512:(j + 1) * 512],
                                     start=(first and j == 0),
                                     stop=(bp == blocks // 2 - 1 and j == 1))
                        t.mark("a_av", 1)
                        last_avpT[up] = S["a_av"]
                        if bp == blocks // 2 - 1:
                            v.wait("a_av")
                            ya = y_acc[:, h * 512:(h + 1) * 512]
                            if ci == 0:
                                if debug and h == 0:
                                    v.wait("dbst", S.get("dbst", 0))
                                v.tensor_copy(ya, ps_av[0:65, avs * 512:(avs + 1) * 512])
                            else:
                                v.scalar_tensor_tensor(
                                    ya, ps_av[0:65, avs * 512:(avs + 1) * 512],
                                    1.0, ya, ALU.mult, ALU.add)
                            v.nop().inc("a_evac", 1)
                            avcnt[0] += 1

                    def flush_av(all_=True):
                        while pend_av and (all_ or len(pend_av) > 1):
                            emit_av(pend_av.pop(0))

                    def main_chunk(s_, ci):
                        blocks = min(8, nblk[s_] - 8 * ci)
                        slot = ci % 2
                        N = nblk[s_]
                        for h in range(16):
                            for bp in range(blocks // 2):
                                up = ucnt[0] % 2
                                ucnt[0] += 1
                                Sps = ps_s0 if up == 0 else ps_s1
                                pT = pT0 if up == 0 else pT1
                                b0 = 2 * bp
                                t.wait(*akv_end[ci])
                                if ci == 0 and h == 0 and bp == 0:
                                    t.wait("a_qt", S.get("a_qt", 0))
                                    t.wait("x_st", S.get("x_st", 0))
                                if ci == 0 and h == 0 and bp <= 1:
                                    t.wait("a_fe", S.get("a_fe", 0))
                                t.wait("a_exp", last_expS[up])
                                qpar = (s_ % 2) * 8192
                                for j in (0, 1):
                                    co = slot * 16384 + h * 1024 + (b0 + j) * 128
                                    t.matmul(Sps[:, j * 512:(j + 1) * 512],
                                             kr_k[0:65, co:co + 128],
                                             qT_ext[0:65, qpar + h * 512:
                                                    qpar + (h + 1) * 512],
                                             start=True, stop=True)
                                t.mark("a_qk", 1)
                                flush_av(all_=False)
                                sc.wait("a_qk")
                                sc.wait("a_av", last_avpT[up])
                                sc.activation(pT[:], Sps[:], AF.Exp,
                                              bias=emx[:, 53:54],
                                              scale=0.125).inc("a_exp", 1)
                                last_expS[up] = S["a_exp"]
                                ndiag = 0
                                for j in (0, 1):
                                    blk = ci * 8 + b0 + j
                                    dj = blk - (N - 4)
                                    if dj >= 0:
                                        g.wait("a_exp")
                                        g.tensor_tensor(pT[:, j * 512:(j + 1) * 512],
                                                        pT[:, j * 512:(j + 1) * 512],
                                                        dmask[:, dj * 512:(dj + 1) * 512],
                                                        ALU.mult)
                                        ndiag += 1
                                if ndiag:
                                    g.nop().inc("a_msk", 1)
                                pend_av.append(
                                    (s_, ci, h, bp, blocks, slot, up,
                                     S["a_exp"], S.get("a_msk", 0) if ndiag else 0))
                                yield

                    def y_finalize(s_):
                        if debug:
                            sc.wait("dbst", S.get("dbst", 0))
                            sc.wait("a_evac", S.get("a_evac", 0))
                            sc.dma_start(out=dbg["yacc"][s_], in_=y_acc[:]).inc("dbst", 16)
                        for h in range(16):
                            ya = y_acc[:, h * 512:(h + 1) * 512]
                            t.wait("a_evac")
                            t.wait("a_fv", S.get("a_fv", 0))
                            for q4 in range(4):
                                t.matmul(ps_av[:, q4 * 128:q4 * 128 + 65],
                                         ya[:, q4 * 128:(q4 + 1) * 128], identf[0:65, 0:65],
                                         is_transpose=True, start=True, stop=True)
                            t.mark("a_ftr", 1)
                            v.wait("a_ftr")
                            for q4 in range(4):
                                pu = ps_av[:, q4 * 128:q4 * 128 + 65]
                                v.reciprocal(recipS[:, q4:q4 + 1], pu[:, 64:65])
                                v.tensor_scalar(ytokn[:, q4 * 64:(q4 + 1) * 64],
                                                pu[:, 0:64], recipS[:, q4:q4 + 1],
                                                None, ALU.mult)
                            v.nop().inc("a_fv", 1)
                            t.wait("a_fv")
                            t.wait("a_fe", S.get("a_fe", 0))
                            ynps = bfv(ps_s1, 1)
                            for q4 in range(4):
                                t.matmul(ynps[0:64, q4 * 128:(q4 + 1) * 128],
                                         ytokn[:, q4 * 64:(q4 + 1) * 64], identb[:],
                                         is_transpose=True, start=True, stop=True)
                            t.mark("a_ftr2", 1)
                            v.wait("a_ftr2")
                            v.wait("a_ynst", max(0, S.get("a_ynst", 0)))
                            v.tensor_copy(ynT_sb[:], ynps[0:64, 0:512]).inc("a_fe", 1)
                            sp.wait("a_fe")
                            sp.wait("a_ynst", S.get("a_ynst", 0))
                            sp.dma_start(out=ynT_d[h // 2][64 * (h % 2):64 * (h % 2) + 64,
                                                           s_ * 512:(s_ + 1) * 512],
                                         in_=ynT_sb[:]).inc("a_ynst", 16)

                    sp.wait("ld", S.get("ld", 0))
                    sp.wait("ngst", S.get("ngst", 0))
                    load_qext_rows(0)
                    kvload(0, nblk[0], True, nblk[0])
                    for s_ in range(NST):
                        for ci in range(CC[s_]):
                            if ci + 1 < CC[s_]:
                                kvload(ci + 1, nblk[s_], True, nblk[s_])
                            for _ in main_chunk(s_, ci):
                                pass
                            flush_av()
                            kbusy[ci % 2] = S.get("a_qk", 0)
                            vbusy[ci % 2] = S.get("a_av", 0)
                        qk_end[s_] = S.get("a_qk", 0)
                        if debug:
                            sc.wait("dbst", S.get("dbst", 0))
                            sc.dma_start(out=dbg["mu"][s_],
                                         in_=qT_ext[64:65, (s_ % 2) * 8192:
                                                    (s_ % 2 + 1) * 8192]).inc("dbst", 16)
                        if s_ < 3:
                            load_qext_rows(s_ + 1)
                            kvload(0, nblk[s_ + 1], True, nblk[s_ + 1])
                        y_finalize(s_)

                ctx.variant_region(("sync", "vector", "scalar", "tensor",
                                    "gpsimd"), attention)

            def run_postattn(ctx: Ctx):
                sp, v, sc, t, g = ctx.sp, ctx.v, ctx.sc, ctx.t, ctx.g
                if debug:
                    sp.dma_start(out=dbg["ynT"][:], in_=ynT_d[:]).inc("ld", 16)

            def run_ffn(ctx: Ctx):
                sp, v, sc, t, g = ctx.sp, ctx.v, ctx.sc, ctx.t, ctx.g
                S = ctx.sems

                def drain_dma():
                    for s_ in sorted(DMA_SEMS):
                        sp.wait(s_)
                def wld(out, in_, **kw):
                    g.wait("f_wld", S.get("f_wld", 0))
                    g.dma_start(out=out, in_=in_, **kw).inc("f_wld", 16)
                wld(projb.rearrange("p (c f) -> p c f", c=8),
                    w_proj.rearrange("(c p) f -> p c f", p=128))
                wld(c1b[:], w_c1[:])
                wld(c2b[:], w_c2[:])
                wld(w1b.rearrange("p (c f) -> p c f", c=8),
                    w_in1.rearrange("(c p) f -> p c f", p=128))
                wld(wo1b.rearrange("p (c f) -> p c f", c=4),
                    w_out1T.rearrange("(c p) f -> p c f", p=128))
                wld(w2b.rearrange("p (c f) -> p c f", c=32),
                    w_in2.rearrange("(c p) f -> p c f", p=128))
                wld(wo2b.rearrange("p (c f) -> p c f", c=4),
                    w_out2T.rearrange("(c p) f -> p c f", p=128))
                wld(woTb.rearrange("p (c f) -> p c f", c=8),
                    w_oT.rearrange("(c p) f -> p c f", p=128))
                wld(b1c.rearrange("p c -> p c"),
                    w_b1.rearrange("(c p) o -> p (c o)", p=128),
                    allow_slow_non_contiguous=True)
                wld(b2c.rearrange("p c -> p c"),
                    w_b2.rearrange("(c p) o -> p (c o)", p=128),
                    allow_slow_non_contiguous=True)

                inv_k = emx[:, 0:16]
                kvec = emx[:, 16:32]
                iota16 = emx[:, 32:48]

                for ti in range(4):
                    sc.wait("f_act", S.get("f_act", 0))
                    sc.wait("f_xld", S.get("f_xld", 0))
                    sc.wait("f_mm", S.get("f_mm", 0))
                    sc.dma_start(out=fxn.rearrange("p (c f) -> p c f", c=8),
                                 in_=xnT_own.rearrange("c p t -> p c t")[:, :, ti * 512:(ti + 1) * 512]).inc("f_xld", 16)
                    t.wait("f_xld", 16 * (ti + 1))
                    if ti == 0:
                        t.wait("f_wld", 16 * 3)
                    t.wait("f_ve", S.get("f_ve", 0))
                    for dc in range(8):
                        t.matmul(ps_u[0:16, 0:512], projb[:, dc * 16:(dc + 1) * 16],
                                 fxn[:, dc * 512:(dc + 1) * 512],
                                 start=(dc == 0), stop=(dc == 7))
                    t.mark("f_mm", 1)
                    v.wait("f_mm")
                    v.tensor_copy(zproj[:], ps_u[0:16, 0:512]).inc("f_ve", 1)
                    t.wait("f_ve")
                    for q4 in range(4):
                        t.matmul(ps_s0[:, q4 * 128:q4 * 128 + 16],
                                 zproj[0:16, q4 * 128:(q4 + 1) * 128], identf[0:16, 0:16],
                                 is_transpose=True, start=True, stop=True)
                    t.mark("f_mm", 1)
                    v.wait("f_mm")
                    for q4 in range(4):
                        v.tensor_copy(ztok[:, q4 * 16:(q4 + 1) * 16],
                                      ps_s0[:, q4 * 128:q4 * 128 + 16])
                    v.nop().inc("f_ve", 1)
                    z4 = ztok.rearrange("p (g e) -> p g e", g=4)
                    v.tensor_reduce(ecols[:, 0:4], z4, axis=AX, op=ALU.add)
                    v.tensor_scalar(ecols[:, 0:4], ecols[:, 0:4], 1.0 / 16, None, ALU.mult)
                    mb = ecols[:, 0:4].rearrange("p (g e) -> p g e", e=1).broadcast_to([128, 4, 16])
                    ez4 = ezs.rearrange("p (g e) -> p g e", g=4)
                    v.tensor_tensor(ez4, z4, mb, ALU.subtract)
                    t4_ = etmp.rearrange("p (g e) -> p g e", g=4)
                    v.tensor_tensor(t4_, ez4, ez4, ALU.mult)
                    v.tensor_reduce(ecols[:, 4:8], t4_, axis=AX, op=ALU.add)
                    v.tensor_scalar(ecols[:, 4:8], ecols[:, 4:8], 1.0 / 16, None,
                                    ALU.mult).inc("f_ve", 1)
                    sc.wait("f_ve")
                    sc.activation(ecols[:, 8:12], ecols[:, 4:8], AF.Ln, bias=emx[:, 48:49])
                    sc.activation(ecols[:, 12:16], ecols[:, 8:12], AF.Exp,
                                  scale=-0.5).inc("f_act", 1)
                    v.wait("f_act")
                    rb = ecols[:, 12:16].rearrange("p (g e) -> p g e", e=1).broadcast_to([128, 4, 16])
                    v.tensor_tensor(ez4, ez4, rb, ALU.mult)
                    v.tensor_scalar(ezs[:], ezs[:], 0.5, None, ALU.mult)
                    v.tensor_reduce(ecols[:, 16:20], ez4, axis=AX, op=ALU.max)
                    xb_ = ecols[:, 16:20].rearrange("p (g e) -> p g e", e=1).broadcast_to([128, 4, 16])
                    v.tensor_tensor(ez4, ez4, xb_, ALU.subtract)
                    v.tensor_copy(ez3[:], ezs[:])
                    for rnd in range(16):
                        off = rnd % 2
                        n = (16 - off) // 2
                        src = ezs.rearrange("p (g e) -> p g e", g=4)[:, :, off:off + 2 * n]
                        pa = src.rearrange("p g (e2 two) -> p g e2 two", two=2)[:, :, :, 0:1]
                        pb_ = src.rearrange("p g (e2 two) -> p g e2 two", two=2)[:, :, :, 1:2]
                        ta = etmp.rearrange("p (g e) -> p g e", g=4)[:, :, 0:n].rearrange(
                            "p g (e2 o) -> p g e2 o", o=1)
                        v.tensor_tensor(ta, pa, pb_, ALU.max)
                        v.tensor_tensor(pb_, pa, pb_, ALU.min)
                        v.tensor_copy(pa, ta)
                    cs4 = ecs.rearrange("p (g e) -> p g e", g=4)
                    cq4 = ecs2.rearrange("p (g e) -> p g e", g=4)
                    v.tensor_tensor(cq4, ez4, ez4, ALU.mult)
                    v.tensor_copy(ecs[:], ezs[:])
                    for d_ in (1, 2, 4, 8):
                        for buf in (ecs, ecs2):
                            c4 = buf.rearrange("p (g e) -> p g e", g=4)
                            tt4 = etmp2.rearrange("p (g e) -> p g e", g=4)
                            v.tensor_copy(tt4[:, :, 0:16 - d_], c4[:, :, 0:16 - d_])
                            v.tensor_tensor(c4[:, :, d_:16], c4[:, :, d_:16],
                                            tt4[:, :, 0:16 - d_], ALU.add)
                    ik4 = inv_k.rearrange("p (o e) -> p o e", o=1).broadcast_to([128, 4, 16])
                    kv4 = kvec.rearrange("p (o e) -> p o e", o=1).broadcast_to([128, 4, 16])
                    v.tensor_tensor(cs4, cs4, ik4, ALU.mult)
                    v.tensor_tensor(cq4, cq4, ik4, ALU.mult)
                    v.tensor_tensor(t4_, cs4, cs4, ALU.mult)
                    v.tensor_tensor(t4_, cq4, t4_, ALU.subtract)
                    v.tensor_tensor(t4_, t4_, kv4, ALU.mult)
                    v.tensor_scalar(etmp[:], etmp[:], -1.0, 1.0, ALU.mult, ALU.add)
                    v.tensor_tensor(t4_, t4_, ik4, ALU.mult)
                    v.tensor_scalar(etmp[:], etmp[:], 0.0, None, ALU.max).inc("f_ve", 1)
                    sc.wait("f_ve")
                    sc.activation(etmp2[:], etmp[:], AF.Ln, bias=emx[:, 50:51])
                    sc.activation(etmp2[:], etmp2[:], AF.Exp, scale=0.5).inc("f_act", 1)
                    v.wait("f_act")
                    t24 = etmp2.rearrange("p (g e) -> p g e", g=4)
                    v.tensor_tensor(t24, cs4, t24, ALU.subtract)
                    s4_ = cs4
                    v.tensor_tensor(s4_, t24, ez4, ALU.is_le)
                    v.tensor_reduce(ecols[:, 20:24], s4_, axis=AX, op=ALU.add)
                    v.tensor_scalar(ecols[:, 20:24], ecols[:, 20:24], -1.0, None, ALU.add)
                    io4 = iota16.rearrange("p (o e) -> p o e", o=1).broadcast_to([128, 4, 16])
                    sbx = ecols[:, 20:24].rearrange("p (g e) -> p g e", e=1).broadcast_to([128, 4, 16])
                    v.tensor_tensor(s4_, io4, sbx, ALU.is_equal)
                    v.tensor_tensor(s4_, s4_, t24, ALU.mult)
                    v.tensor_reduce(ecols[:, 16:20], s4_, axis=AX, op=ALU.add)
                    tb_ = ecols[:, 16:20].rearrange("p (g e) -> p g e", e=1).broadcast_to([128, 4, 16])
                    e34 = ez3.rearrange("p (g e) -> p g e", g=4)
                    v.tensor_tensor(e34, e34, tb_, ALU.subtract)
                    v.tensor_scalar(ez3[:], ez3[:], 0.0, None, ALU.max)
                    v.tensor_tensor(e34, e34, e34, ALU.mult)
                    v.tensor_copy(ab16[:], ez3[:]).inc("f_ve", 1)
                    t.wait("f_ve")
                    aps_ = bfv(ps_u, 0)
                    for q4 in range(4):
                        t.matmul(aps_[0:16, q4 * 128:(q4 + 1) * 128],
                                 ab16[:, q4 * 16:(q4 + 1) * 16], identb[:],
                                 is_transpose=True, start=True, stop=True)
                    t.mark("f_mm", 1)
                    v.wait("f_mm")
                    v.tensor_copy(aT_all[0:16, ti * 512:(ti + 1) * 512],
                                  aps_[0:16, 0:512]).inc("f_ve", 1)

                for ti in range(4):
                    sp.wait("f_ost", max(0, S.get("f_ost", 0) - 16))
                    sp.wait("f_xt", S.get("f_xt", 0))
                    sp.wait("f_yld", S.get("f_yld", 0))
                    if ti == 0:
                        sp.wait("f_xld", S.get("f_xld", 0))
                    sp.wait("f_mm", S.get("f_mm", 0))
                    sp.dma_start(out=fxn.rearrange("p (c f) -> p c f", c=8),
                                 in_=xnT_own.rearrange("c p t -> p c t")[:, :, ti * 512:(ti + 1) * 512]).inc("f_xt", 16)
                    sp.dma_start(out=fyn.rearrange("p (c f) -> p c f", c=8),
                                 in_=ynT_d.rearrange("c p t -> p c t")[:, :, ti * 512:(ti + 1) * 512]).inc("f_yld", 16)
                    t.wait("f_wld", 16 * 10)
                    if ti == 0:
                        v.wait("f_wld", 16 * 10)
                    for rc in range(4):
                        t.wait("f_ve", S.get("f_ve", 0))
                        t.matmul(ps_s0[:, 0:512], c1b[0:16, rc * 128:(rc + 1) * 128],
                                 aT_all[0:16, ti * 512:(ti + 1) * 512], start=True, stop=True)
                        t.mark("f_mm", 1)
                        v.wait("f_mm")
                        v.tensor_copy(g1T[:, rc * 512:(rc + 1) * 512], ps_s0[:, 0:512]).inc("f_ve", 1)
                        t.wait("f_ve")
                        t.matmul(ps_s0[:, 0:512], c2b[0:16, rc * 128:(rc + 1) * 128],
                                 aT_all[0:16, ti * 512:(ti + 1) * 512], start=True, stop=True)
                        t.mark("f_mm", 1)
                        v.wait("f_mm")
                        v.tensor_copy(g2T[:, rc * 512:(rc + 1) * 512], ps_s0[:, 0:512]).inc("f_ve", 1)
                    t.wait("f_xt", S.get("f_xt", 0))
                    for rc in range(4):
                        t.wait("f_ve", S.get("f_ve", 0))
                        for dc in range(8):
                            t.matmul(ps_s1[:, 0:512],
                                     w1b[:, dc * 512 + rc * 128: dc * 512 + (rc + 1) * 128],
                                     fxn[:, dc * 512:(dc + 1) * 512],
                                     start=(dc == 0), stop=(dc == 7))
                        t.mark("f_mm", 1)
                        v.wait("f_mm")
                        v.scalar_tensor_tensor(z1b[:, rc * 512:(rc + 1) * 512],
                                               ps_s1[:, 0:512], b1c[:, rc:rc + 1],
                                               g1T[:, rc * 512:(rc + 1) * 512],
                                               ALU.add, ALU.mult).inc("f_ve", 1)
                    for oc in range(32):
                        pp = oc % 2
                        hps = ps_s0 if pp == 0 else ps_s1
                        t.wait("f_ve", S.get("f_ve", 0))
                        t.wait("f_gel", max(0, S.get("f_gel", 0) - 1))
                        for rc in range(4):
                            t.matmul(hps[:, 0:512],
                                     wo1b[:, rc * 4096 + oc * 128: rc * 4096 + (oc + 1) * 128],
                                     z1b[:, rc * 512:(rc + 1) * 512],
                                     start=(rc == 0), stop=(rc == 3))
                        t.mark("f_mm", 1)
                        sc.wait("f_mm")
                        sc.activation(hb[:, oc * 512:(oc + 1) * 512], hps[:, 0:512],
                                      AF.Gelu).inc("f_gel", 1)
                    for rc in range(4):
                        sc.wait("f_gel", S.get("f_gel", 0))
                        t.wait("f_gel", S.get("f_gel", 0))
                        t.wait("f_ve", S.get("f_ve", 0))
                        for oc in range(32):
                            t.matmul(ps_s1[:, 0:512],
                                     w2b[:, oc * 512 + rc * 128: oc * 512 + (rc + 1) * 128],
                                     hb[:, oc * 512:(oc + 1) * 512],
                                     start=(oc == 0), stop=(oc == 31))
                        t.mark("f_mm", 1)
                        v.wait("f_mm")
                        v.scalar_tensor_tensor(z2b[:, rc * 512:(rc + 1) * 512],
                                               ps_s1[:, 0:512], b2c[:, rc:rc + 1],
                                               g2T[:, rc * 512:(rc + 1) * 512],
                                               ALU.add, ALU.mult).inc("f_ve", 1)
                    t.wait("f_yld", S.get("f_yld", 0))
                    v.wait("f_yld", S.get("f_yld", 0))
                    for tt in range(4):
                        ob = outsb[:, (tt % 2) * 1024:(tt % 2) * 1024 + 1024]
                        sp.wait("f_res", S.get("f_res", 0))
                        sp.wait("f_xold", S.get("f_xold", 0))
                        sp.dma_start(out=fxo[:],
                                     in_=x_own[(ti * 4 + tt) * 128:(ti * 4 + tt + 1) * 128, :]).inc("f_xold", 16)
                        for half in range(2):
                            pcmb = ps_av[:, half * 512:(half + 1) * 512]
                            t.wait("f_ve", S.get("f_ve", 0))
                            t.wait("f_res", max(0, S.get("f_res", 0) - 1))
                            for dc in range(8):
                                t.matmul(pcmb,
                                         fyn[:, dc * 512 + tt * 128: dc * 512 + tt * 128 + 128],
                                         woTb[:, dc * 1024 + half * 512: dc * 1024 + (half + 1) * 512],
                                         start=(dc == 0), stop=False)
                            for rc in range(4):
                                t.matmul(pcmb,
                                         z2b[:, rc * 512 + tt * 128: rc * 512 + tt * 128 + 128],
                                         wo2b[:, rc * 1024 + half * 512: rc * 1024 + (half + 1) * 512],
                                         start=False, stop=(rc == 3))
                            t.mark("f_mm", 1)
                            v.wait("f_mm")
                            v.wait("f_xold")
                            v.wait("f_ost", max(0, S.get("f_ost", 0) - 16))
                            v.tensor_tensor(ob[:, half * 512:(half + 1) * 512], pcmb,
                                            fxo[:, half * 512:(half + 1) * 512],
                                            ALU.add).inc("f_res", 1)
                        sp.wait("f_res")
                        sp.wait("f_ost", S.get("f_ost", 0))
                        sp.dma_start(out=y_out[(ti * 4 + tt) * 128:(ti * 4 + tt + 1) * 128, :],
                                     in_=ob).inc("f_ost", 16)
                sp.wait("f_ost")
                drain_dma()
                v.wait("f_ost")
                t.wait("f_ost")
                sc.wait("f_ost")
                g.wait("f_ost")

            ctxs = {e: Ctx(nc, semh, e, SEM_OWNER) for e in ENGS}
            nsec = 8 if phase < 2 else 11
            for sec_i in range(nsec):
                with nc.Block(f"sec{sec_i}") as blk:
                    for eng in ENGS:
                        def mk(engname, s):
                            def f(_eng):
                                program(ctxs[engname], s)
                            return f
                        getattr(blk, eng)(mk(eng, sec_i))
                nc.all_engine_barrier()

    return nc


def _consts():
    identb = np.eye(128, dtype=BF)
    identf = np.eye(128, dtype=np.float32)
    inv_freq = (1.0 / ROPE_BASE) ** (np.arange(0, DH, 2, dtype=np.float32) / DH)
    tpos = np.arange(T, dtype=np.float32)
    freqs = np.outer(tpos, inv_freq)
    ropeC = np.cos(freqs).astype(np.float32).reshape(NT, 128, 32)
    ropeS = np.sin(freqs).astype(np.float32).reshape(NT, 128, 32)
    qi = np.arange(512)[None, :]
    ki = np.arange(128)[:, None]
    dmaskv = np.stack([(qi >= ki + 128 * j) for j in range(4)]).astype(BF)
    emx = np.zeros((128, 56), dtype=np.float32)
    emx[:, 0:16] = 1.0 / np.arange(1, 17)[None, :]
    emx[:, 16:32] = np.arange(1, 17)[None, :]
    emx[:, 32:48] = np.arange(16)[None, :]
    emx[:, 48] = 1e-5
    emx[:, 49] = 1e-12
    emx[:, 50] = 1e-30
    emx[:, 53] = 0.0
    emx[:, 54] = np.log(0.75)
    auxmb = np.zeros((128, 512), dtype=np.float32)
    cc = np.arange(128)[None, :]
    pp = np.arange(128)[:, None]
    auxmb[:, 384:512] = np.where(cc <= pp, 0.0, -1e9)
    return dict(c_identb=identb, c_identf=identf, c_ropeC=ropeC, c_ropeS=ropeS,
                c_dmask=dmaskv, c_emx=emx, c_auxmb=auxmb)


def _stage_weights(ins):
    f = np.float32
    return dict(
        wabqT=np.ascontiguousarray(np.asarray(ins["L0_attn_W_abq"], f).T),
        wabkvT=np.ascontiguousarray(np.asarray(ins["L0_attn_W_abkv"], f).T),
        w_proj=np.ascontiguousarray(np.asarray(ins["L0_ffn_proj"], f)),
        w_c1=np.ascontiguousarray(np.asarray(ins["L0_ffn_C1"], f)),
        w_c2=np.ascontiguousarray(np.asarray(ins["L0_ffn_C2"], f)),
        w_in1=np.ascontiguousarray(np.asarray(ins["L0_ffn_Win1"], f)[0:D, :]),
        w_b1=np.ascontiguousarray(np.asarray(ins["L0_ffn_Win1"], f)[D:D + 1, :].T),
        w_out1T=np.ascontiguousarray(np.asarray(ins["L0_ffn_Wout1"], f).T),
        w_in2=np.ascontiguousarray(np.asarray(ins["L0_ffn_Win2"], f)[0:OH, :]),
        w_b2=np.ascontiguousarray(np.asarray(ins["L0_ffn_Win2"], f)[OH:OH + 1, :].T),
        w_out2T=np.ascontiguousarray(np.asarray(ins["L0_ffn_Wout2"], f).T),
        w_oT=np.ascontiguousarray(np.asarray(ins["L0_attn_W_o"], f).T),
    )


def make_in_maps(inputs):
    x = np.asarray(inputs["L0_attn_x"], np.float32)
    weights = _stage_weights(inputs)
    consts = _consts()
    in_maps = []
    for c in range(8):
        b, v = c // 4, c % 4
        xb = np.ascontiguousarray(x[b])
        xo = np.ascontiguousarray(
            np.concatenate([xb[1024 * v:1024 * (v + 1)],
                            xb[1024 * (7 - v):1024 * (8 - v)]], axis=0))
        m = {"x_full": xb, "x_own": xo}
        m.update(weights)
        m.update(consts)
        in_maps.append(m)
    return in_maps


def kernel(**inputs):
    nc = build_nc(phase=3, debug=False)
    in_maps = make_in_maps(inputs)
    res = run_bass_kernel_spmd(nc, in_maps, core_ids=list(range(8)))
    out = np.empty((B, T, D), np.float32)
    for c in range(8):
        b, v = c // 4, c % 4
        yo = res.results[c]["y_own_out"]
        out[b, 1024 * v:1024 * (v + 1)] = yo[0:1024]
        out[b, 1024 * (7 - v):1024 * (8 - v)] = yo[1024:2048]
    return out


if __name__ == "__main__":
    import reference as R
    ins = R.setup_inputs(0)
    out = kernel(**{k: np.asarray(v) for k, v in ins.items()})
    ref = np.asarray(R.reference(**ins))
    num = float(np.mean((out - ref) ** 2))
    den = float(np.mean(ref ** 2))
    print("resid var ratio:", num / den)
